# version 07
# baseline (speedup 1.0000x reference)
import math
from contextlib import ExitStack, contextmanager
import numpy as np
import ml_dtypes
import concourse.bass as bass
import concourse.mybir as mybir
from concourse.bass import AP
from concourse.bass_utils import run_bass_kernel_spmd

F32 = mybir.dt.float32
BF16 = mybir.dt.bfloat16
I32 = mybir.dt.int32
AF = mybir.ActivationFunctionType
ALU = mybir.AluOpType
AX = mybir.AxisListType

NSEQ = 2
S = 2048
D = 1024
NT = S // 128
NCH = D // 128


class Ctx:
    def __init__(self, nc, stack):
        self.nc = nc
        self.outer_E = stack.enter_context
        self.E = stack.enter_context
        self.counts = {}
        self.sems = {}
        for name in ("sd", "sgd", "sv", "st", "ss", "sg", "so"):
            self.sems[name] = self.E(nc.semaphore(name))
            self.counts[name] = 0

    def inc(self, name, instr, amt=1):
        instr.then_inc(self.sems[name], amt)
        self.counts[name] += amt
        return self.counts[name]

    def val(self, name):
        return self.counts[name]

    def wait(self, eng, name, val):
        if val > 0:
            eng.wait_ge(self.sems[name], val)

    def barrier(self, blk):
        snap = {k: v for k, v in self.counts.items() if v > 0}
        for eng_name in ("sync", "scalar", "vector", "tensor", "gpsimd"):
            sec = getattr(blk, eng_name)

            def _mk(engine):
                for k, v in snap.items():
                    engine.wait_ge(self.sems[k], v)
            sec(_mk)

    def make_sems(self, prefix, n):
        for k in range(n):
            name = f"{prefix}{k}"
            self.sems[name] = self.outer_E(self.nc.semaphore(name))
            self.counts[name] = 0
        return [f"{prefix}{k}" for k in range(n)]

    @contextmanager
    def stage(self):
        st = ExitStack()
        old = self.E
        self.E = st.enter_context
        try:
            yield
        finally:
            st.close()
            self.E = old


def ts(eng, out, in0, s1, op0, s2=None, op1=ALU.bypass, accum_out=None):
    return eng.tensor_scalar(out=out, in0=in0, scalar1=s1, scalar2=s2,
                             op0=op0, op1=op1, accum_out=accum_out)


def bcast32(tile_ap):
    return AP(tile_ap.tensor, tile_ap.offset,
              [list(tile_ap.ap[0]), list(tile_ap.ap[1]), [0, 32]])


def view32(ap2d):
    return ap2d.rearrange("p (b e) -> p b e", e=32)



def emit_stage1(ctx, blk, io):
    nc, E = ctx.nc, ctx.E
    x_in, xnT, xdT, ident_bf = io["x"], io["xnT"], io["xdT"], io["ident_bf"]

    NBUF = 4
    xa = [E(nc.sbuf_tensor(f"s1_xa{i}", [128, 1024], F32)) for i in range(NBUF)]
    t0 = E(nc.sbuf_tensor("s1_t0", [128, 1024], F32))
    t1 = E(nc.sbuf_tensor("s1_t1", [128, 1024], F32))
    t2 = E(nc.sbuf_tensor("s1_t2", [128, 1024], F32))
    t3 = E(nc.sbuf_tensor("s1_t3", [128, 1024], F32))
    t5 = E(nc.sbuf_tensor("s1_t5", [128, 1024], F32))
    xn = E(nc.sbuf_tensor("s1_xn", [128, 1024], F32))
    sqdump = E(nc.sbuf_tensor("s1_sqdump", [128, 1024], F32))
    xnb = [E(nc.sbuf_tensor(f"s1_xnb{i}", [128, 1024], BF16)) for i in range(2)]
    xdb = [E(nc.sbuf_tensor(f"s1_xdb{i}", [128, 1024], BF16)) for i in range(2)]
    sm = E(nc.sbuf_tensor("s1_sm", [128, 32], F32))
    sc = E(nc.sbuf_tensor("s1_sc", [128, 32], F32))
    rsc = E(nc.sbuf_tensor("s1_rsc", [128, 32], F32))
    stat = E(nc.sbuf_tensor("s1_stat", [128, 4], F32))
    rstd = E(nc.sbuf_tensor("s1_rstd", [128, 1], F32))
    epsb = E(nc.sbuf_tensor("s1_epsb", [128, 1], F32))
    asn = E(nc.sbuf_tensor("s1_asn", [128, NCH * S], BF16))
    asd = E(nc.sbuf_tensor("s1_asd", [128, NCH * S], BF16))
    ps_n = E(nc.psum_tensor("s1_psn", [128, 1024], BF16))
    ps_d = E(nc.psum_tensor("s1_psd", [128, 1024], BF16))

    tile_done = {}
    evac_done = {}
    trn_done = {}
    trd_done = {}
    sx = ctx.make_sems("s1x", NBUF)

    @blk.vector
    def _(v):
        v.memset(epsb[:], 1e-5)

    def emit_load(i):
        s, t = divmod(i, NT)

        @blk.sync
        def _(q):
            if i >= NBUF:
                ctx.wait(q, "sv", tile_done[i - NBUF])
            ctx.inc(sx[i % NBUF], q.dma_start(
                out=xa[i % NBUF][:],
                in_=x_in[s, t * 128:(t + 1) * 128, :]), 16)

    for i in range(NBUF):
        emit_load(i)

    for i in range(NSEQ * NT):
        s, t = divmod(i, NT)
        xt = xa[i % NBUF]

        @blk.vector
        def _(v):
            ctx.wait(v, sx[i % NBUF], ctx.val(sx[i % NBUF]))
            c = v.tensor_reduce(out=stat[:, 0:1], in_=xt[:], axis=AX.X, op=ALU.add)
            ctx.wait(v, "sv", ctx.inc("sv", c))
            c = ts(v, stat[:, 1:2], stat[:, 0:1], 1.0 / D, ALU.mult)
            ctx.wait(v, "sv", ctx.inc("sv", c))
            c = ts(v, t0[:], xt[:], stat[:, 1:2], ALU.subtract)
            io["_m_xc"] = ctx.inc("sv", c)

        @blk.scalar
        def _(sE):
            ctx.wait(sE, "sv", io["_m_xc"])
            c = sE.activation(sqdump[:], t0[:], AF.Square, bias=0.0, scale=1.0,
                              accum_out=stat[:, 2:3])
            io["_m_ssq"] = ctx.inc("ss", c)

        @blk.vector
        def _(v):
            ctx.wait(v, "ss", io["_m_ssq"])
            c = ts(v, stat[:, 3:4], stat[:, 2:3], 1.0 / D, ALU.mult)
            io["_m_var"] = ctx.inc("sv", c)

        @blk.scalar
        def _(sE):
            ctx.wait(sE, "sv", io["_m_var"])
            c = sE.activation(rstd[:], stat[:, 3:4], AF.Sqrt, bias=epsb[:], scale=1.0)
            io["_m_std"] = ctx.inc("ss", c)

        @blk.vector
        def _(v):
            ctx.wait(v, "ss", io["_m_std"])
            c = v.reciprocal(rstd[:], rstd[:])
            ctx.wait(v, "sv", ctx.inc("sv", c))
            ts(v, xn[:], t0[:], rstd[:], ALU.mult)
            if i >= 2:
                ctx.wait(v, "st", trn_done[i - 2])
            c = ts(v, xnb[i % 2][:], t0[:], rstd[:], ALU.mult)
            io["_m_xnb"] = ctx.inc("sv", c)

        @blk.tensor
        def _(tE):
            ctx.wait(tE, "sv", io["_m_xnb"])
            if i >= 1:
                ctx.wait(tE, "sv", evac_done.get(i - 1, 0))
            for p in range(NCH):
                ins = tE.transpose(ps_n[:, p * 128:(p + 1) * 128],
                                   xnb[i % 2][:, p * 128:(p + 1) * 128],
                                   ident_bf[:])
            trn_done[i] = ctx.inc("st", ins)

        @blk.vector
        def _(v):
            c = v.tensor_reduce(out=sm[:], in_=view32(xn[:]), axis=AX.X,
                                op=ALU.max, apply_absolute_value=True)
            ctx.wait(v, "sv", ctx.inc("sv", c))
            c = ts(v, sc[:], sm[:], 1.0 / 7.5, ALU.mult, 1e-30, ALU.max)
            ctx.wait(v, "sv", ctx.inc("sv", c))
            c = v.reciprocal(rsc[:], sc[:])
            ctx.wait(v, "sv", ctx.inc("sv", c))
            v.tensor_tensor(out=view32(t1[:]), in0=view32(xn[:]),
                            in1=bcast32(rsc[:]), op=ALU.mult)
            v.scalar_tensor_tensor(out=t2[:], in0=t1[:], scalar=-1.0,
                                   in1=t1[:], op0=ALU.mult, op1=ALU.max)
            ts(v, t3[:], t2[:], 2.0, ALU.is_ge, -4.0, ALU.mult)
            ts(v, t1[:], t2[:], 4.0, ALU.is_ge, -2.0, ALU.mult)
            v.scalar_tensor_tensor(out=t3[:], in0=t3[:], scalar=8.0,
                                   in1=t1[:], op0=ALU.add, op1=ALU.add)
            v.tensor_tensor(out=t1[:], in0=t2[:], in1=t3[:], op=ALU.mult)
            ts(v, t2[:], t1[:], 15.5, ALU.is_ge, 16.0, ALU.mult)
            v.tensor_tensor(out=t5[:], in0=t1[:], in1=t2[:], op=ALU.subtract)
            ts(v, t2[:], t5[:], 7.5, ALU.is_ge, 8.0, ALU.mult)
            v.tensor_tensor(out=t5[:], in0=t5[:], in1=t2[:], op=ALU.subtract)
            ts(v, t2[:], t5[:], 3.5, ALU.is_ge, 4.0, ALU.mult)
            v.tensor_tensor(out=t5[:], in0=t5[:], in1=t2[:], op=ALU.subtract)
            ts(v, t2[:], t5[:], 1.5, ALU.is_ge, 2.0, ALU.mult)
            v.tensor_tensor(out=t5[:], in0=t5[:], in1=t2[:], op=ALU.subtract)
            ts(v, t2[:], t5[:], 0.5, ALU.is_ge, 1.0, ALU.mult)
            v.tensor_tensor(out=t5[:], in0=t5[:], in1=t2[:], op=ALU.subtract)
            v.tensor_tensor(out=t1[:], in0=t1[:], in1=t5[:], op=ALU.subtract)
            v.reciprocal(t2[:], t3[:])
            v.tensor_tensor(out=t1[:], in0=t1[:], in1=t2[:], op=ALU.mult)
            ts(v, t2[:], xn[:], 0.0, ALU.is_lt, -2.0, ALU.mult)
            ts(v, t2[:], t2[:], 1.0, ALU.add)
            v.tensor_tensor(out=t1[:], in0=t1[:], in1=t2[:], op=ALU.mult)
            if i >= 2:
                ctx.wait(v, "st", trd_done[i - 2])
            c = v.tensor_tensor(out=view32(xdb[i % 2][:]), in0=view32(t1[:]),
                                in1=bcast32(sc[:]), op=ALU.mult)
            io["_m_xdb"] = ctx.inc("sv", c)

        @blk.tensor
        def _(tE):
            ctx.wait(tE, "sv", io["_m_xdb"])
            for p in range(NCH):
                ins = tE.transpose(ps_d[:, p * 128:(p + 1) * 128],
                                   xdb[i % 2][:, p * 128:(p + 1) * 128],
                                   ident_bf[:])
            trd_done[i] = ctx.inc("st", ins)

        @blk.vector
        def _(v):
            dstn = asn[:].rearrange("p (c q) -> p c q", q=S)[:, :, t * 128:(t + 1) * 128]
            dstd = asd[:].rearrange("p (c q) -> p c q", q=S)[:, :, t * 128:(t + 1) * 128]
            srcn = ps_n[:].rearrange("p (c q) -> p c q", q=128)
            srcd = ps_d[:].rearrange("p (c q) -> p c q", q=128)
            ctx.wait(v, "st", trn_done[i])
            v.tensor_copy(dstn, srcn)
            ctx.wait(v, "st", trd_done[i])
            c = v.tensor_copy(dstd, srcd)
            evac_done[i] = ctx.inc("sv", c)
            tile_done[i] = evac_done[i]

        if i + NBUF < NSEQ * NT:
            emit_load(i + NBUF)

        if t == NT - 1:
            @blk.gpsimd
            def _(g):
                ctx.wait(g, "sv", tile_done[i])
                for p in range(NCH):
                    ctx.inc("sgd", g.dma_start(
                        out=xnT[s, p * 128:(p + 1) * 128, :],
                        in_=asn[:, p * S:(p + 1) * S]), 16)
                    ctx.inc("sgd", g.dma_start(
                        out=xdT[s, p * 128:(p + 1) * 128, :],
                        in_=asd[:, p * S:(p + 1) * S]), 16)
            io["_m_s1_store"] = ctx.val("sgd")
            if s == 0:
                @blk.vector
                def _(v):
                    ctx.wait(v, "sgd", io["_m_s1_store"])

    io["s1_done_sgd"] = ctx.val("sgd")



def host_constants():
    c = {}
    c["ident_bf"] = np.eye(128, dtype=ml_dtypes.bfloat16)
    c["ident_f32"] = np.eye(128, dtype=np.float32)
    c["ones_row"] = np.ones((1, 128), dtype=np.float32)
    d = 64
    inv_freq = (1.0 / 10000.0 ** (np.arange(0, d, 2, dtype=np.float32) / d))
    pos = np.arange(S, dtype=np.float32)
    th = pos[None, :] * inv_freq[:, None]
    cos = np.cos(th).astype(np.float32)
    sin = np.sin(th).astype(np.float32)
    cos64 = np.repeat(cos, 2, axis=0)
    sin64 = np.empty((64, S), dtype=np.float32)
    sin64[0::2] = -sin
    sin64[1::2] = sin
    c["cos_ext"] = np.concatenate([cos64, cos64], axis=0)
    c["sin_ext"] = np.concatenate([sin64, sin64], axis=0)
    return c



def emit_stage2(ctx, blk, io):
    nc, E = ctx.nc, ctx.E
    xnT_d = io["xnT"]
    qT_d, kT_d, v_d = io["qT"], io["kT"], io["v"]
    wq_d, wqs_d = io["WqT"], io["WqTs"]
    wk_d, wks_d = io["WkT"], io["WkTs"]
    wv_d = io["WvT"]
    bq_d, bqs_d = io["bq_pc"], io["bqs_pc"]
    bk_d, bks_d = io["bk_pc"], io["bks_pc"]
    bv_d = io["bv_row"]
    cos_d, sin_d = io["cos_ext"], io["sin_ext"]

    xnT_sb = E(nc.sbuf_tensor("s2_xnT", [128, NCH * S], BF16))
    wsb = {}
    for nm in ("wq", "wqs", "wk", "wks", "wv"):
        wsb[nm] = E(nc.sbuf_tensor(f"s2_{nm}", [128, NCH * D], BF16))
    cos_sb = E(nc.sbuf_tensor("s2_cos", [128, S], F32))
    sin_sb = E(nc.sbuf_tensor("s2_sin", [128, S], F32))
    bqs_sb = E(nc.sbuf_tensor("s2_bq", [128, NCH], F32))
    bqss_sb = E(nc.sbuf_tensor("s2_bqs", [128, NCH], F32))
    bks_sb = E(nc.sbuf_tensor("s2_bk", [128, NCH], F32))
    bkss_sb = E(nc.sbuf_tensor("s2_bks", [128, NCH], F32))
    bv_sb = E(nc.sbuf_tensor("s2_bv", [1, D], BF16))
    ones_bf = E(nc.sbuf_tensor("s2_ones1", [1, 128], BF16))
    tq = E(nc.sbuf_tensor("s2_tq", [128, 512], F32))
    tq2 = E(nc.sbuf_tensor("s2_tq2", [128, 512], F32))
    cbuf = {"q": [E(nc.sbuf_tensor(f"s2_cq{j}", [128, S], BF16)) for j in range(2)],
            "k": [E(nc.sbuf_tensor(f"s2_ck{j}", [128, S], BF16)) for j in range(2)]}
    v_sb = E(nc.sbuf_tensor("s2_v", [128, NT * D], BF16))
    ps_a = E(nc.psum_tensor("s2_psa", [128, 512], F32))
    ps_b = E(nc.psum_tensor("s2_psb", [128, 512], F32))
    ps_v = [E(nc.psum_tensor(f"s2_psv{j}", [128, 512], F32)) for j in range(2)]

    sxl = ctx.make_sems("s2xl", 1)[0]
    scb = ctx.make_sems("s2cb", 4)

    @blk.gpsimd
    def _(g):
        for nm, d_ in (("wq", wq_d), ("wqs", wqs_d), ("wk", wk_d),
                       ("wks", wks_d), ("wv", wv_d)):
            src3 = d_.rearrange("(c p) o -> p c o", p=128)
            dst3 = wsb[nm][:].rearrange("p (c o) -> p c o", o=D)
            ctx.inc("sgd", g.dma_start(out=dst3, in_=src3), 16)
        ctx.inc("sgd", g.dma_start(out=bv_sb[:], in_=bv_d), 16)
    w_loaded = ctx.val("sgd")

    @blk.sync
    def _(q):
        ctx.inc("sd", q.dma_start(out=cos_sb[:], in_=cos_d), 16)
        ctx.inc("sd", q.dma_start(out=sin_sb[:], in_=sin_d), 16)
        for bd, bs_ in ((bq_d, bqs_sb), (bqs_d, bqss_sb), (bk_d, bks_sb),
                        (bks_d, bkss_sb)):
            ctx.inc("sd", q.dma_start(out=bs_[:], in_=bd), 16)
    misc_loaded = ctx.val("sd")

    @blk.vector
    def _(v):
        c0 = v.memset(ones_bf[:], 1.0)
        io["_s2_ones"] = ctx.inc("sv", c0)

    for s in range(NSEQ):
        @blk.sync
        def _(q):
            if s > 0:
                ctx.wait(q, "st", io["_s2_vmm"])
            src3 = xnT_d[s].rearrange("(c p) t -> p c t", p=128)
            dst3 = xnT_sb[:].rearrange("p (c t) -> p c t", t=S)
            ctx.inc(sxl, q.dma_start(out=dst3, in_=src3), 16)
        xn_ready = ctx.val(sxl)

        for kind, wa, wb_, ba, bb_, dram in (
                ("q", "wq", "wqs", bqs_sb, bqss_sb, qT_d),
                ("k", "wk", "wks", bks_sb, bkss_sb, kT_d)):
            for c in range(NCH):
                slot = c % 2
                for g_ in range(4):
                    @blk.tensor
                    def _(t):
                        if s == 0 and kind == "q" and c == 0 and g_ == 0:
                            ctx.wait(t, "sgd", w_loaded)
                            ctx.wait(t, "sd", misc_loaded)
                        ctx.wait(t, sxl, xn_ready)
                        if not (c == 0 and g_ == 0):
                            ctx.wait(t, "sv", io["_s2_evac"])
                        elif s > 0 or kind == "k":
                            ctx.wait(t, "sv", io.get("_s2_evac", 0))
                        for kc in range(NCH):
                            t.matmul(ps_a[:],
                                     wsb[wa][:, kc * D + c * 128:kc * D + (c + 1) * 128],
                                     xnT_sb[:, kc * S + g_ * 512:kc * S + g_ * 512 + 512],
                                     start=(kc == 0), stop=(kc == NCH - 1))
                        for kc in range(NCH):
                            ins = t.matmul(ps_b[:],
                                     wsb[wb_][:, kc * D + c * 128:kc * D + (c + 1) * 128],
                                     xnT_sb[:, kc * S + g_ * 512:kc * S + g_ * 512 + 512],
                                     start=(kc == 0), stop=(kc == NCH - 1))
                        io["_s2_mm"] = ctx.inc("st", ins)
                        io["_s2_lastmm"] = io["_s2_mm"]

                    @blk.vector
                    def _(v):
                        ctx.wait(v, "st", io["_s2_mm"])
                        if g_ == 0:
                            ctx.wait(v, scb[(0 if kind == "q" else 2) + slot],
                                     ctx.val(scb[(0 if kind == "q" else 2) + slot]))
                        cs = cos_sb[:, g_ * 512:(g_ + 1) * 512]
                        sn = sin_sb[:, g_ * 512:(g_ + 1) * 512]
                        v.scalar_tensor_tensor(out=tq[:], in0=ps_a[:],
                                               scalar=ba[:, c:c + 1], in1=cs,
                                               op0=ALU.add, op1=ALU.mult)
                        v.scalar_tensor_tensor(out=tq2[:], in0=ps_b[:],
                                               scalar=bb_[:, c:c + 1], in1=sn,
                                               op0=ALU.add, op1=ALU.mult)
                        c2 = v.tensor_tensor(
                            out=cbuf[kind][slot][:, g_ * 512:(g_ + 1) * 512],
                            in0=tq[:], in1=tq2[:], op=ALU.add)
                        io["_s2_evac"] = ctx.inc("sv", c2)

                @blk.gpsimd
                def _(g):
                    ctx.wait(g, "sv", io["_s2_evac"])
                    ctx.inc(scb[(0 if kind == "q" else 2) + slot],
                            g.dma_start(out=dram[s, c * 128:(c + 1) * 128, :],
                                        in_=cbuf[kind][slot][:]), 16)

        for tt in range(NT):
            for hf in range(2):
                j = (tt * 2 + hf) % 2
                @blk.tensor
                def _(t):
                    if s == 0 and tt == 0 and hf == 0:
                        ctx.wait(t, "sv", io["_s2_ones"])
                    if tt * 2 + hf >= 2:
                        ctx.wait(t, "sv", io["_s2_vevac"][j])
                    for kc in range(NCH):
                        t.matmul(ps_v[j][:],
                                 xnT_sb[:, kc * S + tt * 128:kc * S + (tt + 1) * 128],
                                 wsb["wv"][:, kc * D + hf * 512:kc * D + (hf + 1) * 512],
                                 start=(kc == 0), stop=False)
                    ins = t.matmul(ps_v[j][:], ones_bf[:],
                                   bv_sb[:, hf * 512:(hf + 1) * 512],
                                   start=False, stop=True)
                    io["_s2_vmm"] = ctx.inc("st", ins)

                @blk.vector
                def _(vE):
                    ctx.wait(vE, "st", io["_s2_vmm"])
                    c2 = vE.tensor_copy(
                        v_sb[:, tt * D + hf * 512:tt * D + (hf + 1) * 512],
                        ps_v[j][:])
                    if "_s2_vevac" not in io or tt * 2 + hf < 2:
                        io.setdefault("_s2_vevac", {})
                    io["_s2_vevac"][j] = ctx.inc("sv", c2)

        @blk.gpsimd
        def _(g):
            ctx.wait(g, "sv", io["_s2_vevac"][0])
            ctx.wait(g, "sv", io["_s2_vevac"][1])
            dst3 = v_d[s].rearrange("(tt p) f -> p tt f", p=128)
            src3 = v_sb[:].rearrange("p (tt f) -> p tt f", f=D)
            ctx.inc("sgd", g.dma_start(out=dst3, in_=src3), 16)
        io["_s2_vstore"] = ctx.val("sgd")
        if s == 0:
            @blk.vector
            def _(vE):
                ctx.wait(vE, "sgd", io["_s2_vstore"])

    io["s2_done_sgd"] = ctx.val("sgd")



def emit_stage3(ctx, blk, io):
    nc, E = ctx.nc, ctx.E
    qT_d, kT_d, v_d, oT_d = io["qT"], io["kT"], io["v"], io["oT"]
    e16_d = io["e16"]
    identf_d = io["ident_f32"]

    qT_sb = E(nc.sbuf_tensor("s3_qT", [128, NCH * S], BF16))
    kT_sb = E(nc.sbuf_tensor("s3_kT", [128, NCH * S], BF16))
    v_sb = E(nc.sbuf_tensor("s3_v", [128, NT * D], BF16))
    v65 = [E(nc.sbuf_tensor(f"s3_v65_{j}", [128, NT * 65], BF16)) for j in range(2)]
    pT = [E(nc.sbuf_tensor(f"s3_pT{j}", [128, NT * 512], BF16)) for j in range(2)]
    sq_scr = E(nc.sbuf_tensor("s3_sq", [128, NCH * 512], F32))
    e16_sb = E(nc.sbuf_tensor("s3_e16", [128, NCH * 16], F32))
    identf = E(nc.sbuf_tensor("s3_idf", [128, 128], F32))
    onesf = E(nc.sbuf_tensor("s3_onesf", [1, 128], F32))
    bnd = E(nc.sbuf_tensor("s3_bnd", [16, 8], F32))
    nrow = E(nc.sbuf_tensor("s3_nrow", [1, 16], F32))
    bias16 = E(nc.sbuf_tensor("s3_bias16", [128, 16], F32))
    l_sb = E(nc.sbuf_tensor("s3_l", [1, 512], F32))
    linv = E(nc.sbuf_tensor("s3_linv", [1, 512], F32))
    lexp = E(nc.sbuf_tensor("s3_lexp", [64, 512], F32))
    ot_stage = [E(nc.sbuf_tensor(f"s3_ot{j}", [64, 512], BF16)) for j in range(2)]

    ps_s = [E(nc.psum_tensor(f"s3_pss{j}", [128, 512], F32)) for j in range(4)]
    ps_o = [E(nc.psum_tensor(f"s3_pso{j}", [128, 512], F32)) for j in range(2)]
    ps_l = E(nc.psum_tensor("s3_psl", [64, 512], F32))
    ps_n1 = ps_o[1][0:16, 0:512]
    ps_r = ps_o[0][0:128, 0:16]

    sq_ld = ctx.make_sems("s3ld", 3)
    s_ot = ctx.make_sems("s3ot", 2)

    @blk.sync
    def _(q):
        ctx.inc("sd", q.dma_start(out=e16_sb[:], in_=e16_d), 16)
        ctx.inc("sd", q.dma_start(out=identf[:], in_=identf_d), 16)
    misc_ld = ctx.val("sd")

    @blk.vector
    def _(v):
        c = v.memset(onesf[:], 1.0)
        io["_s3_ones"] = ctx.inc("sv", c)

    exp_done = {}
    av_done = {}
    tt_done = {}

    for s in range(NSEQ):
        @blk.sync
        def _(q):
            if s > 0:
                ctx.wait(q, "sv", io["_s3_lastTT"])
            src_q = qT_d[s].rearrange("(c p) t -> p c t", p=128)
            ctx.inc(sq_ld[0], q.dma_start(
                out=qT_sb[:].rearrange("p (c t) -> p c t", t=S), in_=src_q), 16)
            src_k = kT_d[s].rearrange("(c p) t -> p c t", p=128)
            ctx.inc(sq_ld[1], q.dma_start(
                out=kT_sb[:].rearrange("p (c t) -> p c t", t=S), in_=src_k), 16)
            src_v = v_d[s].rearrange("(tt p) f -> p tt f", p=128)
            ctx.inc(sq_ld[2], q.dma_start(
                out=v_sb[:].rearrange("p (tt f) -> p tt f", f=D), in_=src_v), 16)
        qld, kld, vld = ctx.val(sq_ld[0]), ctx.val(sq_ld[1]), ctx.val(sq_ld[2])

        for pi, (src_sb, ldsem, ldval, col) in enumerate(
                ((qT_sb, sq_ld[0], qld, 2), (kT_sb, sq_ld[1], kld, 3))):
            @blk.vector
            def _(v):
                c2 = v.memset(bnd[:, 0:1], 0.0)
                io["_s3_bacc"] = ctx.inc("sv", c2)

            for g_ in range(4):
                for c in range(NCH):
                    @blk.scalar
                    def _(sE):
                        ctx.wait(sE, ldsem, ldval)
                        if s > 0 and pi == 0 and g_ == 0 and c == 0:
                            ctx.wait(sE, "sv", io["_s3_lastTT"])
                        c2 = sE.activation(
                            sq_scr[:, c * 512:(c + 1) * 512],
                            src_sb[:, c * S + g_ * 512:c * S + (g_ + 1) * 512],
                            AF.Square, bias=0.0, scale=1.0)
                        io["_s3_sq"] = ctx.inc("ss", c2)

                    @blk.tensor
                    def _(t):
                        ctx.wait(t, "ss", io["_s3_sq"])
                        ctx.wait(t, "sd", misc_ld)
                        if c == 0 and "_s3_red" in io:
                            ctx.wait(t, "sv", io["_s3_red"])
                        c2 = t.matmul(ps_n1, e16_sb[:, c * 16:(c + 1) * 16],
                                      sq_scr[:, c * 512:(c + 1) * 512],
                                      start=(c == 0), stop=(c == NCH - 1))
                        if c == NCH - 1:
                            io["_s3_nmm"] = ctx.inc("st", c2)

                @blk.vector
                def _(v):
                    ctx.wait(v, "st", io["_s3_nmm"])
                    if g_ == 0:
                        ctx.wait(v, "sv", io["_s3_bacc"])
                    c2 = v.tensor_reduce(out=bnd[:, 1:2], in_=ps_n1, axis=AX.X,
                                         op=ALU.max)
                    ctx.wait(v, "sv", ctx.inc("sv", c2))
                    c2 = v.tensor_tensor(out=bnd[:, 0:1], in0=bnd[:, 0:1],
                                         in1=bnd[:, 1:2], op=ALU.max)
                    ctx.wait(v, "sv", ctx.inc("sv", c2))
                    io["_s3_red"] = ctx.val("sv")

            @blk.scalar
            def _(sE):
                ctx.wait(sE, "sv", io["_s3_red"])
                c2 = sE.activation(bnd[:, col:col + 1], bnd[:, 0:1], AF.Sqrt,
                                   bias=0.0, scale=1.0)
                io["_s3_sqrt"] = ctx.inc("ss", c2)

        @blk.vector
        def _(v):
            ctx.wait(v, "ss", io["_s3_sqrt"])
            c2 = v.tensor_tensor(out=bnd[:, 4:5], in0=bnd[:, 2:3],
                                 in1=bnd[:, 3:4], op=ALU.mult)
            ctx.wait(v, "sv", ctx.inc("sv", c2))
            c2 = ts(v, bnd[:, 4:5], bnd[:, 4:5], -0.125, ALU.mult)
            io["_s3_negc"] = ctx.inc("sv", c2)

        @blk.tensor
        def _(t):
            ctx.wait(t, "sv", io["_s3_negc"])
            c2 = t.matmul(ps_r[0:1, 0:16], bnd[0:16, 4:5], identf[0:16, 0:16],
                          start=True, stop=True)
            io["_s3_rowmm"] = ctx.inc("st", c2)

        @blk.vector
        def _(v):
            ctx.wait(v, "st", io["_s3_rowmm"])
            c2 = v.tensor_copy(nrow[:], ps_r[0:1, 0:16])
            io["_s3_nrowc"] = ctx.inc("sv", c2)

        @blk.tensor
        def _(t):
            ctx.wait(t, "sv", io["_s3_nrowc"])
            c2 = t.matmul(ps_r, onesf[:], nrow[:], start=True, stop=True)
            io["_s3_bmm"] = ctx.inc("st", c2)

        @blk.vector
        def _(v):
            ctx.wait(v, "st", io["_s3_bmm"])
            ctx.wait(v, "sv", io["_s3_ones"])
            c2 = v.tensor_copy(bias16[:], ps_r)
            io["_s3_bias"] = ctx.inc("sv", c2)

        for h in range(16):
            hb = h % 2
            ch = h // 2
            po = (h % 2) * 64
            @blk.vector
            def _(v):
                ctx.wait(v, sq_ld[2], vld)
                if (s, h - 2, 3) in av_done:
                    ctx.wait(v, "st", av_done[(s, h - 2, 3)])
                src = v_sb[:].rearrange("p (tt f) -> p tt f", f=D)[
                    :, :, h * 64:(h + 1) * 64]
                dst = v65[hb][:].rearrange("p (tt f) -> p tt f", f=65)[:, :, 0:64]
                v.tensor_copy(dst, src)
                onescol = v65[hb][:].rearrange("p (tt f) -> p tt f", f=65)[
                    :, :, 64:65]
                c2 = v.memset(onescol, 1.0)
                io["_s3_v65"] = ctx.inc("sv", c2)
            v65_ready = io["_s3_v65"]

            for g_ in range(4):
                u = (s, h, g_)
                up = (h * 4 + g_) % 2
                for g4 in range(4):
                    @blk.tensor
                    def _(t):
                        if g4 == 0 and g_ == 0 and h == 0:
                            ctx.wait(t, sq_ld[0], qld)
                            ctx.wait(t, sq_ld[1], kld)
                            ctx.wait(t, "sv", io["_s3_bias"])
                        for kc in range(4 * g4, 4 * g4 + 4):
                            bank = kc % 4
                            key = ("expb", bank)
                            if key in io:
                                ctx.wait(t, "ss", io[key])
                            c2 = t.matmul(
                                ps_s[bank][:],
                                kT_sb[po:po + 64, ch * S + kc * 128:ch * S + (kc + 1) * 128],
                                qT_sb[po:po + 64, ch * S + g_ * 512:ch * S + (g_ + 1) * 512],
                                start=True, stop=True)
                            io[("mm", kc)] = ctx.inc("st", c2)

                    @blk.scalar
                    def _(sE):
                        if g4 == 0 and g_ == 0 and h == 0:
                            ctx.wait(sE, "sv", io["_s3_bias"])
                        if g4 == 0:
                            key = ("avd", up)
                            if key in io:
                                ctx.wait(sE, "st", io[key])
                        for kc in range(4 * g4, 4 * g4 + 4):
                            ctx.wait(sE, "st", io[("mm", kc)])
                            c2 = sE.activation(pT[up][:, kc * 512:(kc + 1) * 512],
                                               ps_s[kc % 4][:], AF.Exp,
                                               bias=bias16[:, h:h + 1], scale=0.125)
                            io[("expb", kc % 4)] = ctx.inc("ss", c2)
                            io[("exp", kc)] = io[("expb", kc % 4)]

                ob = (h * 4 + g_) % 2
                @blk.tensor
                def _(t):
                    ctx.wait(t, "sv", v65_ready)
                    key = ("ttd", ob)
                    if key in io:
                        ctx.wait(t, "sv", io[key])
                    for kc in range(NT):
                        ctx.wait(t, "ss", io[("exp", kc)])
                        c2 = t.matmul(ps_o[ob][0:65, :],
                                      v65[hb][:, kc * 65:(kc + 1) * 65],
                                      pT[up][:, kc * 512:(kc + 1) * 512],
                                      start=(kc == 0), stop=(kc == NT - 1))
                    io[("avd", up)] = ctx.inc("st", c2)
                    av_done[(s, h, g_)] = io[("avd", up)]

                @blk.vector
                def _(v):
                    ctx.wait(v, "st", io[("avd", up)])
                    c2 = v.tensor_copy(l_sb[:], ps_o[ob][64:65, :])
                    ctx.wait(v, "sv", ctx.inc("sv", c2))
                    c2 = v.reciprocal(linv[:], l_sb[:])
                    io["_s3_linv"] = ctx.inc("sv", c2)

                @blk.tensor
                def _(t):
                    ctx.wait(t, "sv", io["_s3_linv"])
                    c2 = t.matmul(ps_l[:], onesf[:, 0:64], linv[:],
                                  start=True, stop=True)
                    io["_s3_lexp"] = ctx.inc("st", c2)

                @blk.vector
                def _(v):
                    ctx.wait(v, "st", io["_s3_lexp"])
                    v.tensor_copy(lexp[:], ps_l[:])
                    jj = (h * 4 + g_) % 2
                    ctx.wait(v, s_ot[jj], ctx.val(s_ot[jj]))
                    c2 = v.tensor_tensor(out=ot_stage[jj][:], in0=ps_o[ob][0:64, :],
                                         in1=lexp[:], op=ALU.mult)
                    io[("ttd", ob)] = ctx.inc("sv", c2)
                    io["_s3_lastTT"] = io[("ttd", ob)]
                    tt_done[u] = io[("ttd", ob)]

                @blk.gpsimd
                def _(g):
                    jj = (h * 4 + g_) % 2
                    ctx.wait(g, "sv", tt_done[u])
                    ctx.inc(s_ot[jj], g.dma_start(
                        out=oT_d[s, h * 64:(h + 1) * 64, g_ * 512:(g_ + 1) * 512],
                        in_=ot_stage[jj][:]), 16)

    io["s3_done_sgd"] = ctx.val("sgd")


def emit_quant_chain(ctx, v, src_ap, dst_ap, q):
    t1, t2, t3, t5 = q["t1"], q["t2"], q["t3"], q["t5"]
    sm, sc, rsc = q["sm"], q["sc"], q["rsc"]
    c = v.tensor_reduce(out=sm[:], in_=src_ap.rearrange("p (b e) -> p b e", e=32),
                        axis=AX.X, op=ALU.max, apply_absolute_value=True)
    ctx.wait(v, "sv", ctx.inc("sv", c))
    c = ts(v, sc[:], sm[:], 1.0 / 7.5, ALU.mult, 1e-30, ALU.max)
    ctx.wait(v, "sv", ctx.inc("sv", c))
    c = v.reciprocal(rsc[:], sc[:])
    ctx.wait(v, "sv", ctx.inc("sv", c))
    v.tensor_tensor(out=view32(t1[:]), in0=src_ap.rearrange("p (b e) -> p b e", e=32),
                    in1=bcast32(rsc[:]), op=ALU.mult)
    v.scalar_tensor_tensor(out=t2[:], in0=t1[:], scalar=-1.0,
                           in1=t1[:], op0=ALU.mult, op1=ALU.max)
    ts(v, t3[:], t2[:], 2.0, ALU.is_ge, -4.0, ALU.mult)
    ts(v, t1[:], t2[:], 4.0, ALU.is_ge, -2.0, ALU.mult)
    v.scalar_tensor_tensor(out=t3[:], in0=t3[:], scalar=8.0,
                           in1=t1[:], op0=ALU.add, op1=ALU.add)
    v.tensor_tensor(out=t1[:], in0=t2[:], in1=t3[:], op=ALU.mult)
    ts(v, t2[:], t1[:], 15.5, ALU.is_ge, 16.0, ALU.mult)
    v.tensor_tensor(out=t5[:], in0=t1[:], in1=t2[:], op=ALU.subtract)
    ts(v, t2[:], t5[:], 7.5, ALU.is_ge, 8.0, ALU.mult)
    v.tensor_tensor(out=t5[:], in0=t5[:], in1=t2[:], op=ALU.subtract)
    ts(v, t2[:], t5[:], 3.5, ALU.is_ge, 4.0, ALU.mult)
    v.tensor_tensor(out=t5[:], in0=t5[:], in1=t2[:], op=ALU.subtract)
    ts(v, t2[:], t5[:], 1.5, ALU.is_ge, 2.0, ALU.mult)
    v.tensor_tensor(out=t5[:], in0=t5[:], in1=t2[:], op=ALU.subtract)
    ts(v, t2[:], t5[:], 0.5, ALU.is_ge, 1.0, ALU.mult)
    v.tensor_tensor(out=t5[:], in0=t5[:], in1=t2[:], op=ALU.subtract)
    v.tensor_tensor(out=t1[:], in0=t1[:], in1=t5[:], op=ALU.subtract)
    v.reciprocal(t2[:], t3[:])
    v.tensor_tensor(out=t1[:], in0=t1[:], in1=t2[:], op=ALU.mult)
    ts(v, t2[:], src_ap, 0.0, ALU.is_lt, -2.0, ALU.mult)
    ts(v, t2[:], t2[:], 1.0, ALU.add)
    v.tensor_tensor(out=t1[:], in0=t1[:], in1=t2[:], op=ALU.mult)
    c = v.tensor_tensor(out=dst_ap.rearrange("p (b e) -> p b e", e=32),
                        in0=view32(t1[:]), in1=bcast32(sc[:]), op=ALU.mult)
    return c



def emit_stage0w(ctx, blk, io):
    nc, E = ctx.nc, ctx.E
    w_d, wdT_d, ident_bf = io["ffn_w"], io["wdT"], io["ident_bf"]
    import contextlib
    with contextlib.ExitStack() as st0:
        E0 = st0.enter_context
        wa = [E0(nc.sbuf_tensor(f"s0_wa{j}", [128, D], F32)) for j in range(2)]
        wq = E0(nc.sbuf_tensor("s0_wq", [128, D], BF16))
        q = {k: E0(nc.sbuf_tensor(f"s0_{k}", [128, D], F32))
             for k in ("t1", "t2", "t3", "t5")}
        q["sm"] = E0(nc.sbuf_tensor("s0_sm", [128, 32], F32))
        q["sc"] = E0(nc.sbuf_tensor("s0_sc", [128, 32], F32))
        q["rsc"] = E0(nc.sbuf_tensor("s0_rsc", [128, 32], F32))
        asm = E0(nc.sbuf_tensor("s0_asm", [128, NCH * D], BF16))
        ps = E0(nc.psum_tensor("s0_ps", [128, D], BF16))
        sw = ctx.make_sems("s0w", 2)

        for r in range(NCH):
            @blk.sync
            def _(qq):
                if r >= 2:
                    ctx.wait(qq, "sv", io[("s0_qd", r - 2)])
                ctx.inc(sw[r % 2], qq.dma_start(
                    out=wa[r % 2][:], in_=w_d[r * 128:(r + 1) * 128, :]), 16)

            @blk.vector
            def _(v):
                ctx.wait(v, sw[r % 2], ctx.val(sw[r % 2]))
                if r >= 1:
                    ctx.wait(v, "st", io[("s0_tr", r - 1)])
                c = emit_quant_chain(ctx, v, wa[r % 2][:], wq[:], q)
                io[("s0_qd", r)] = ctx.inc("sv", c)

            @blk.tensor
            def _(t):
                ctx.wait(t, "sv", io[("s0_qd", r)])
                if r >= 1:
                    ctx.wait(t, "sv", io[("s0_ev", r - 1)])
                for c_ in range(NCH):
                    ins = t.transpose(ps[:, c_ * 128:(c_ + 1) * 128],
                                      wq[:, c_ * 128:(c_ + 1) * 128], ident_bf[:])
                io[("s0_tr", r)] = ctx.inc("st", ins)

            @blk.vector
            def _(v):
                ctx.wait(v, "st", io[("s0_tr", r)])
                dst = asm[:].rearrange("p (c o) -> p c o", o=D)[:, :, r * 128:(r + 1) * 128]
                src = ps[:].rearrange("p (c o) -> p c o", o=128)
                c = v.tensor_copy(dst, src)
                io[("s0_ev", r)] = ctx.inc("sv", c)

        @blk.gpsimd
        def _(g):
            ctx.wait(g, "sv", io[("s0_ev", NCH - 1)])
            src3 = asm[:].rearrange("p (c o) -> p c o", o=D)
            dst3 = wdT_d.rearrange("(c p) o -> p c o", p=128)
            ctx.inc("sgd", g.dma_start(out=dst3, in_=src3), 16)
        io["s0_done_sgd"] = ctx.val("sgd")



def emit_stage4(ctx, blk, io):
    nc, E = ctx.nc, ctx.E
    xdT_d, oT_d, x_d, x1_d = io["xdT"], io["oT"], io["x"], io["x1"]
    wdT_d, woT_d = io["wdT"], io["WoT"]
    fb_row_d, bo_row_d = io["ffn_b_row"], io["bo_row"]

    xdT_sb = E(nc.sbuf_tensor("s4_xdT", [128, NCH * S], BF16))
    oT_sb = E(nc.sbuf_tensor("s4_oT", [128, NCH * S], BF16))
    wd_sb = E(nc.sbuf_tensor("s4_wd", [128, NCH * D], BF16))
    wo_sb = E(nc.sbuf_tensor("s4_wo", [128, NCH * D], BF16))
    fb_row = E(nc.sbuf_tensor("s4_fbr", [1, D], F32))
    bo_row = E(nc.sbuf_tensor("s4_bor", [1, D], F32))
    onesf = E(nc.sbuf_tensor("s4_ones", [1, 128], F32))
    fb_exp = E(nc.sbuf_tensor("s4_fbe", [128, D], F32))
    bo_exp = E(nc.sbuf_tensor("s4_boe", [128, D], F32))
    xa = [E(nc.sbuf_tensor(f"s4_xa{j}", [128, D], F32)) for j in range(2)]
    gt = E(nc.sbuf_tensor("s4_gt", [128, D], F32))
    st_t = E(nc.sbuf_tensor("s4_st", [128, D], F32))
    x1t = [E(nc.sbuf_tensor(f"s4_x1t{j}", [128, D], F32)) for j in range(2)]
    ps_f = [E(nc.psum_tensor(f"s4_psf{j}", [128, 512], F32)) for j in range(2)]
    ps_s = [E(nc.psum_tensor(f"s4_pss{j}", [128, 512], F32)) for j in range(2)]
    ps_b = E(nc.psum_tensor("s4_psb", [128, D], F32))

    sxl = ctx.make_sems("s4xl", 2)
    sxa = ctx.make_sems("s4xa", 2)
    sx1 = ctx.make_sems("s4x1", 2)

    @blk.gpsimd
    def _(g):
        for d_, sb in ((wdT_d, wd_sb), (woT_d, wo_sb)):
            src3 = d_.rearrange("(c p) o -> p c o", p=128)
            dst3 = sb[:].rearrange("p (c o) -> p c o", o=D)
            ctx.inc("sgd", g.dma_start(out=dst3, in_=src3), 16)
    w_ld = ctx.val("sgd")

    @blk.sync
    def _(q):
        ctx.inc("sd", q.dma_start(out=fb_row[:], in_=fb_row_d), 16)
        ctx.inc("sd", q.dma_start(out=bo_row[:], in_=bo_row_d), 16)
    b_ld = ctx.val("sd")

    @blk.vector
    def _(v):
        c = v.memset(onesf[:], 1.0)
        io["_s4_ones"] = ctx.inc("sv", c)

    @blk.tensor
    def _(t):
        ctx.wait(t, "sd", b_ld)
        ctx.wait(t, "sv", io["_s4_ones"])
        for half in range(2):
            c = t.matmul(ps_b[:, half * 512:(half + 1) * 512], onesf[:],
                         fb_row[:, half * 512:(half + 1) * 512],
                         start=True, stop=True)
        io["_s4_fbmm"] = ctx.inc("st", c)

    @blk.vector
    def _(v):
        ctx.wait(v, "st", io["_s4_fbmm"])
        c = v.tensor_copy(fb_exp[:], ps_b[:])
        io["_s4_fbe"] = ctx.inc("sv", c)

    @blk.tensor
    def _(t):
        ctx.wait(t, "sv", io["_s4_fbe"])
        for half in range(2):
            c = t.matmul(ps_b[:, half * 512:(half + 1) * 512], onesf[:],
                         bo_row[:, half * 512:(half + 1) * 512],
                         start=True, stop=True)
        io["_s4_bomm"] = ctx.inc("st", c)

    @blk.vector
    def _(v):
        ctx.wait(v, "st", io["_s4_bomm"])
        c = v.tensor_copy(bo_exp[:], ps_b[:])
        io["_s4_boe"] = ctx.inc("sv", c)

    for s in range(NSEQ):
        @blk.sync
        def _(q):
            if s > 0:
                ctx.wait(q, "st", io["_s4_lastmm"])
            ctx.inc(sxl[0], q.dma_start(
                out=xdT_sb[:].rearrange("p (c t) -> p c t", t=S),
                in_=xdT_d[s].rearrange("(c p) t -> p c t", p=128)), 16)
            ctx.inc(sxl[1], q.dma_start(
                out=oT_sb[:].rearrange("p (c t) -> p c t", t=S),
                in_=oT_d[s].rearrange("(c p) t -> p c t", p=128)), 16)
        xd_ld, oT_ld = ctx.val(sxl[0]), ctx.val(sxl[1])

        for tt in range(NT):
            j = tt % 2

            @blk.sync
            def _(q):
                if s * NT + tt >= 2:
                    ctx.wait(q, "sv", io[("s4_xc", j)])
                ctx.inc(sxa[j], q.dma_start(
                    out=xa[j][:], in_=x_d[s, tt * 128:(tt + 1) * 128, :]), 16)
            xa_ld = ctx.val(sxa[j])

            for half in range(2):
                @blk.tensor
                def _(t):
                    if s == 0 and tt == 0 and half == 0:
                        ctx.wait(t, "sgd", w_ld)
                    ctx.wait(t, sxl[0], xd_ld)
                    if ("s4_fev", j) in io:
                        ctx.wait(t, "sv", io[("s4_fev", j)])
                    for kc in range(NCH):
                        c = t.matmul(
                            ps_f[j][:],
                            xdT_sb[:, kc * S + tt * 128:kc * S + (tt + 1) * 128],
                            wd_sb[:, kc * D + half * 512:kc * D + (half + 1) * 512],
                            start=(kc == 0), stop=(kc == NCH - 1))
                    io[("s4_fmm", j, half)] = ctx.inc("st", c)

                @blk.vector
                def _(v):
                    ctx.wait(v, "st", io[("s4_fmm", j, half)])
                    if s == 0 and tt == 0 and half == 0:
                        ctx.wait(v, "sv", io["_s4_fbe"])
                    if half == 0 and "s4_prev_xc" in io:
                        ctx.wait(v, "sv", io["s4_prev_xc"])
                    c = v.tensor_tensor(out=gt[:, half * 512:(half + 1) * 512],
                                        in0=ps_f[j][:],
                                        in1=fb_exp[:, half * 512:(half + 1) * 512],
                                        op=ALU.add)
                    io[("s4_fev", j)] = ctx.inc("sv", c)

            @blk.vector
            def _(v):
                ctx.wait(v, "sv", io[("s4_fev", j)])
                io["_s4_gadd"] = ctx.val("sv")

            @blk.scalar
            def _(sE):
                ctx.wait(sE, "sv", io["_s4_gadd"])
                c = sE.activation(gt[:], gt[:], AF.Silu, bias=0.0, scale=1.0)
                io["_s4_silu"] = ctx.inc("ss", c)

            for half in range(2):
                @blk.tensor
                def _(t):
                    ctx.wait(t, sxl[1], oT_ld)
                    if ("s4_sev", j) in io:
                        ctx.wait(t, "sv", io[("s4_sev", j)])
                    for kc in range(NCH):
                        c = t.matmul(
                            ps_s[j][:],
                            oT_sb[:, kc * S + tt * 128:kc * S + (tt + 1) * 128],
                            wo_sb[:, kc * D + half * 512:kc * D + (half + 1) * 512],
                            start=(kc == 0), stop=(kc == NCH - 1))
                    io[("s4_smm", j, half)] = ctx.inc("st", c)
                    io["_s4_lastmm"] = ctx.val("st")

                @blk.vector
                def _(v):
                    ctx.wait(v, "st", io[("s4_smm", j, half)])
                    if s == 0 and tt == 0 and half == 0:
                        ctx.wait(v, "sv", io["_s4_boe"])
                    c = v.tensor_tensor(out=st_t[:, half * 512:(half + 1) * 512],
                                        in0=ps_s[j][:],
                                        in1=bo_exp[:, half * 512:(half + 1) * 512],
                                        op=ALU.add)
                    io[("s4_sev", j)] = ctx.inc("sv", c)

            @blk.vector
            def _(v):
                ctx.wait(v, "ss", io["_s4_silu"])
                v.tensor_tensor(out=st_t[:], in0=st_t[:], in1=gt[:], op=ALU.mult)
                ctx.wait(v, sxa[j], xa_ld)
                ctx.wait(v, sx1[j], ctx.val(sx1[j]))
                c = v.tensor_tensor(out=x1t[j][:], in0=st_t[:], in1=xa[j][:],
                                    op=ALU.add)
                io[("s4_xc", j)] = ctx.inc("sv", c)
                io["s4_prev_xc"] = io[("s4_xc", j)]

            @blk.gpsimd
            def _(g):
                ctx.wait(g, "sv", io[("s4_xc", j)])
                ctx.inc(sx1[j], g.dma_start(
                    out=x1_d[s, tt * 128:(tt + 1) * 128, :], in_=x1t[j][:]), 16)

    io["s4_done_sgd"] = ctx.val("sgd")



def emit_stage5(ctx, blk, io):
    nc, E = ctx.nc, ctx.E
    x1_d = io["x1"]
    xn1T_d, xhatT_d = io["xn1T"], io["xhatT"]
    feats_d, mask_d, asel_d = io["feats"], io["mask"], io["asel"]
    centT_d = io["centT"]
    ident_bf, identf_d = io["ident_bf"], io["ident_f32"]

    NBUF = 4
    xa = [E(nc.sbuf_tensor(f"s5_xa{i}", [128, D], F32)) for i in range(NBUF)]
    xn1 = [E(nc.sbuf_tensor(f"s5_xn1_{j}", [128, D], F32)) for j in range(2)]
    t0 = E(nc.sbuf_tensor("s5_t0", [128, D], F32))
    sqdump = E(nc.sbuf_tensor("s5_sqd", [128, D], F32))
    xn1b = [E(nc.sbuf_tensor(f"s5_xn1b{j}", [128, D], BF16)) for j in range(2)]
    xhb = [E(nc.sbuf_tensor(f"s5_xhb{j}", [128, D], BF16)) for j in range(2)]
    xf32T = E(nc.sbuf_tensor("s5_xf32T", [128, D], F32))
    stat = E(nc.sbuf_tensor("s5_stat", [128, 4], F32))
    rstd = E(nc.sbuf_tensor("s5_rstd", [128, 1], F32))
    epsb = E(nc.sbuf_tensor("s5_epsb", [128, 1], F32))
    centT = E(nc.sbuf_tensor("s5_centT", [128, NCH * 8], F32))
    rmax = E(nc.sbuf_tensor("s5_rmax", [128, 1], F32))
    maskt = [E(nc.sbuf_tensor(f"s5_mask{j}", [128, 8], F32)) for j in range(2)]
    alphat = E(nc.sbuf_tensor("s5_alpha", [128, 8], F32))
    aselt = [E(nc.sbuf_tensor(f"s5_asel{j}", [128, 1], F32)) for j in range(2)]
    identf = E(nc.sbuf_tensor("s5_idf", [128, 128], F32))
    asn1 = E(nc.sbuf_tensor("s5_asn1", [128, NCH * S], BF16))
    asx = E(nc.sbuf_tensor("s5_asx", [128, NCH * S], BF16))
    ps_n = E(nc.psum_tensor("s5_psn", [128, D], BF16))
    ps_h = E(nc.psum_tensor("s5_psh", [128, D], BF16))
    ps_f = E(nc.psum_tensor("s5_psf", [128, D], F32))
    ps_aff = E(nc.psum_tensor("s5_aff", [128, 8], F32))

    sx = ctx.make_sems("s5x", NBUF)
    sfo = ctx.make_sems("s5fo", 2)
    smo = ctx.make_sems("s5mo", 2)

    @blk.vector
    def _(v):
        v.memset(epsb[:], 1e-5)

    @blk.sync
    def _(q):
        ctx.inc("sd", q.dma_start(out=identf[:], in_=identf_d), 16)
        src3 = centT_d.rearrange("(c p) e -> p c e", p=128)
        dst3 = centT[:].rearrange("p (c e) -> p c e", e=8)
        ctx.inc("sd", q.dma_start(out=dst3, in_=src3), 16)
    misc_ld = ctx.val("sd")

    tile_done = {}
    trn_done = {}
    trh_done = {}
    trf_done = {}
    evac_done = {}

    def emit_load(i):
        s, t = divmod(i, NT)

        @blk.sync
        def _(q):
            if i >= NBUF:
                ctx.wait(q, "sv", tile_done[i - NBUF])
            ctx.inc(sx[i % NBUF], q.dma_start(
                out=xa[i % NBUF][:], in_=x1_d[s, t * 128:(t + 1) * 128, :]), 16)

    for i in range(NBUF):
        emit_load(i)

    for i in range(NSEQ * NT):
        s, t = divmod(i, NT)
        xt = xa[i % NBUF]
        j = i % 2

        @blk.vector
        def _(v):
            ctx.wait(v, sx[i % NBUF], ctx.val(sx[i % NBUF]))
            c = v.tensor_reduce(out=stat[:, 0:1], in_=xt[:], axis=AX.X, op=ALU.add)
            ctx.wait(v, "sv", ctx.inc("sv", c))
            c = ts(v, stat[:, 1:2], stat[:, 0:1], 1.0 / D, ALU.mult)
            ctx.wait(v, "sv", ctx.inc("sv", c))
            c = ts(v, t0[:], xt[:], stat[:, 1:2], ALU.subtract)
            io["_m5_xc"] = ctx.inc("sv", c)

        @blk.scalar
        def _(sE):
            ctx.wait(sE, "sv", io["_m5_xc"])
            c = sE.activation(sqdump[:], t0[:], AF.Square, bias=0.0, scale=1.0,
                              accum_out=stat[:, 2:3])
            io["_m5_ssq"] = ctx.inc("ss", c)

        @blk.vector
        def _(v):
            ctx.wait(v, "ss", io["_m5_ssq"])
            c = ts(v, stat[:, 3:4], stat[:, 2:3], 1.0 / D, ALU.mult)
            io["_m5_var"] = ctx.inc("sv", c)

        @blk.scalar
        def _(sE):
            ctx.wait(sE, "sv", io["_m5_var"])
            c = sE.activation(rstd[:], stat[:, 3:4], AF.Sqrt, bias=epsb[:], scale=1.0)
            io["_m5_std"] = ctx.inc("ss", c)

        @blk.vector
        def _(v):
            ctx.wait(v, "ss", io["_m5_std"])
            c = v.reciprocal(rstd[:], rstd[:])
            ctx.wait(v, "sv", ctx.inc("sv", c))
            if i >= 2:
                ctx.wait(v, sfo[j], ctx.val(sfo[j]))
                ctx.wait(v, "st", trf_done[i - 2])
            ts(v, xn1[j][:], t0[:], rstd[:], ALU.mult)
            if i >= 2:
                ctx.wait(v, "st", trn_done[i - 2])
            c = ts(v, xn1b[j][:], t0[:], rstd[:], ALU.mult)
            io["_m5_xn1b"] = ctx.inc("sv", c)

        @blk.vector
        def _(v):
            c = v.tensor_reduce(out=stat[:, 0:1], in_=xn1[j][:], axis=AX.X,
                                op=ALU.add)
            ctx.wait(v, "sv", ctx.inc("sv", c))
            c = ts(v, stat[:, 1:2], stat[:, 0:1], 1.0 / D, ALU.mult)
            ctx.wait(v, "sv", ctx.inc("sv", c))
            c = ts(v, t0[:], xn1[j][:], stat[:, 1:2], ALU.subtract)
            io["_m5_xc2"] = ctx.inc("sv", c)

        @blk.scalar
        def _(sE):
            ctx.wait(sE, "sv", io["_m5_xc2"])
            c = sE.activation(sqdump[:], t0[:], AF.Square, bias=0.0, scale=1.0,
                              accum_out=stat[:, 2:3])
            io["_m5_ssq2"] = ctx.inc("ss", c)

        @blk.vector
        def _(v):
            ctx.wait(v, "ss", io["_m5_ssq2"])
            c = ts(v, stat[:, 3:4], stat[:, 2:3], 1.0 / D, ALU.mult)
            io["_m5_var2"] = ctx.inc("sv", c)

        @blk.scalar
        def _(sE):
            ctx.wait(sE, "sv", io["_m5_var2"])
            c = sE.activation(rstd[:], stat[:, 3:4], AF.Sqrt, bias=epsb[:], scale=1.0)
            io["_m5_std2"] = ctx.inc("ss", c)

        @blk.vector
        def _(v):
            ctx.wait(v, "ss", io["_m5_std2"])
            c = v.reciprocal(rstd[:], rstd[:])
            ctx.wait(v, "sv", ctx.inc("sv", c))
            if i >= 2:
                ctx.wait(v, "st", trh_done[i - 2])
            c = ts(v, xhb[j][:], t0[:], rstd[:], ALU.mult)
            io["_m5_xhb"] = ctx.inc("sv", c)

        @blk.tensor
        def _(tE):
            ctx.wait(tE, "sv", io["_m5_xn1b"])
            if i >= 1:
                ctx.wait(tE, "sv", evac_done.get(i - 1, 0))
            for p in range(NCH):
                ins = tE.transpose(ps_n[:, p * 128:(p + 1) * 128],
                                   xn1b[j][:, p * 128:(p + 1) * 128], ident_bf[:])
            trn_done[i] = ctx.inc("st", ins)
            ctx.wait(tE, "sv", io["_m5_xhb"])
            for p in range(NCH):
                ins = tE.transpose(ps_h[:, p * 128:(p + 1) * 128],
                                   xhb[j][:, p * 128:(p + 1) * 128], ident_bf[:])
            trh_done[i] = ctx.inc("st", ins)
            ctx.wait(tE, "sd", misc_ld)
            for p in range(NCH):
                ins = tE.transpose(ps_f[:, p * 128:(p + 1) * 128],
                                   xn1[j][:, p * 128:(p + 1) * 128], identf[:])
            trf_done[i] = ctx.inc("st", ins)

        @blk.vector
        def _(v):
            dstn = asn1[:].rearrange("p (c q) -> p c q", q=S)[:, :, t * 128:(t + 1) * 128]
            dsth = asx[:].rearrange("p (c q) -> p c q", q=S)[:, :, t * 128:(t + 1) * 128]
            ctx.wait(v, "st", trn_done[i])
            v.tensor_copy(dstn, ps_n[:].rearrange("p (c q) -> p c q", q=128))
            ctx.wait(v, "st", trh_done[i])
            v.tensor_copy(dsth, ps_h[:].rearrange("p (c q) -> p c q", q=128))
            ctx.wait(v, "st", trf_done[i])
            c = v.tensor_copy(xf32T[:], ps_f[:])
            evac_done[i] = ctx.inc("sv", c)

        @blk.tensor
        def _(tE):
            ctx.wait(tE, "sv", evac_done[i])
            if ("s5_affr",) in io:
                ctx.wait(tE, "ss", io[("s5_affr",)])
            for kc in range(NCH):
                c = tE.matmul(ps_aff[:], xf32T[:, kc * 128:(kc + 1) * 128],
                              centT[:, kc * 8:(kc + 1) * 8],
                              start=(kc == 0), stop=(kc == NCH - 1))
            io["_m5_aff"] = ctx.inc("st", c)

        @blk.vector
        def _(v):
            ctx.wait(v, "st", io["_m5_aff"])
            c = v.tensor_reduce(out=rmax[:], in_=ps_aff[:], axis=AX.X, op=ALU.max)
            ctx.wait(v, "sv", ctx.inc("sv", c))
            ctx.wait(v, smo[j], ctx.val(smo[j]))
            c = ts(v, maskt[j][:], ps_aff[:], rmax[:], ALU.is_equal)
            io["_m5_mask"] = ctx.inc("sv", c)

        @blk.scalar
        def _(sE):
            ctx.wait(sE, "sv", io["_m5_mask"])
            c = sE.activation(alphat[:], ps_aff[:], AF.Sigmoid, bias=0.0, scale=1.0)
            io[("s5_affr",)] = ctx.inc("ss", c)

        @blk.vector
        def _(v):
            ctx.wait(v, "ss", io[("s5_affr",)])
            c = v.tensor_tensor(out=alphat[:], in0=alphat[:], in1=maskt[j][:],
                                op=ALU.mult)
            ctx.wait(v, "sv", ctx.inc("sv", c))
            c = v.tensor_reduce(out=aselt[j][:], in_=alphat[:], axis=AX.X,
                                op=ALU.add)
            tile_done[i] = ctx.inc("sv", c)
            io["_m5_tile"] = tile_done[i]

        @blk.gpsimd
        def _(g):
            ctx.wait(g, "sv", tile_done[i])
            ctx.inc(sfo[j], g.dma_start(
                out=feats_d[s, t * 128:(t + 1) * 128, :], in_=xn1[j][:]), 16)
            ctx.inc(smo[j], g.dma_start(
                out=mask_d[s, t * 128:(t + 1) * 128, :], in_=maskt[j][:]), 16)
            ctx.inc(smo[j], g.dma_start(
                out=asel_d[s, t * 128:(t + 1) * 128, :], in_=aselt[j][:]), 16)

        if i + NBUF < NSEQ * NT:
            emit_load(i + NBUF)

        if t == NT - 1:
            @blk.gpsimd
            def _(g):
                ctx.wait(g, "sv", tile_done[i])
                for p in range(NCH):
                    ctx.inc("sgd", g.dma_start(
                        out=xn1T_d[s, p * 128:(p + 1) * 128, :],
                        in_=asn1[:, p * S:(p + 1) * S]), 16)
                    ctx.inc("sgd", g.dma_start(
                        out=xhatT_d[s, p * 128:(p + 1) * 128, :],
                        in_=asx[:, p * S:(p + 1) * S]), 16)
            io["_m5_store"] = ctx.val("sgd")
            if s == 0:
                @blk.vector
                def _(v):
                    ctx.wait(v, "sgd", io["_m5_store"])

    io["s5_done_sgd"] = ctx.val("sgd")



def emit_stage6a(ctx, blk, io):
    nc, E = ctx.nc, ctx.E
    xn1T_d = io["xn1T"]
    qTl_d, kTl_d, vl_d = io["qTl"], io["kTl"], io["vl"]
    ktm_d = io["ktm"]
    wq_d, wk_d, wv_d = io["L1Wq"], io["L1Wk"], io["L1Wv"]
    bq_d, bk_d = io["l1bq_pc"], io["l1bk_pc"]
    bv_d = io["l1bv_row"]

    xnT_sb = E(nc.sbuf_tensor("s6a_xnT", [128, NCH * S], BF16))
    wsb = {}
    for nm in ("wq", "wk", "wv"):
        wsb[nm] = E(nc.sbuf_tensor(f"s6a_{nm}", [128, NCH * D], BF16))
    bq_sb = E(nc.sbuf_tensor("s6a_bq", [128, NCH], F32))
    bk_sb = E(nc.sbuf_tensor("s6a_bk", [128, NCH], F32))
    bv_sb = E(nc.sbuf_tensor("s6a_bv", [1, D], BF16))
    bkr_sb = E(nc.sbuf_tensor("s6a_bkr", [1, D], BF16))
    ones_bf = E(nc.sbuf_tensor("s6a_ones1", [1, 128], BF16))
    cbuf = {"q": [E(nc.sbuf_tensor(f"s6a_cq{j}", [128, S], BF16)) for j in range(2)],
            "k": [E(nc.sbuf_tensor(f"s6a_ck{j}", [128, S], BF16)) for j in range(2)]}
    v_sb = E(nc.sbuf_tensor("s6a_v", [128, NT * D], BF16))
    ps_a = [E(nc.psum_tensor(f"s6a_psa{j}", [128, 512], F32)) for j in range(2)]
    ps_v = [E(nc.psum_tensor(f"s6a_psv{j}", [128, 512], F32)) for j in range(2)]

    sxl = ctx.make_sems("s6axl", 1)[0]
    scb = ctx.make_sems("s6acb", 4)

    @blk.gpsimd
    def _(g):
        for nm, d_ in (("wq", wq_d), ("wk", wk_d), ("wv", wv_d)):
            src3 = d_.rearrange("(c p) o -> p c o", p=128)
            dst3 = wsb[nm][:].rearrange("p (c o) -> p c o", o=D)
            ctx.inc("sgd", g.dma_start(out=dst3, in_=src3), 16)
        ctx.inc("sgd", g.dma_start(out=bv_sb[:], in_=bv_d), 16)
        ctx.inc("sgd", g.dma_start(out=bkr_sb[:], in_=io["l1bk_row"]), 16)
    w_loaded = ctx.val("sgd")

    @blk.sync
    def _(q):
        ctx.inc("sd", q.dma_start(out=bq_sb[:], in_=bq_d), 16)
        ctx.inc("sd", q.dma_start(out=bk_sb[:], in_=bk_d), 16)
    misc_loaded = ctx.val("sd")

    @blk.vector
    def _(v):
        c0 = v.memset(ones_bf[:], 1.0)
        io["_s6a_ones"] = ctx.inc("sv", c0)

    for s in range(NSEQ):
        @blk.sync
        def _(q):
            if s > 0:
                ctx.wait(q, "st", io["_s6a_vmm"])
            src3 = xn1T_d[s].rearrange("(c p) t -> p c t", p=128)
            dst3 = xnT_sb[:].rearrange("p (c t) -> p c t", t=S)
            ctx.inc(sxl, q.dma_start(out=dst3, in_=src3), 16)
        xn_ready = ctx.val(sxl)

        for kind, wa, ba, dram in (("q", "wq", bq_sb, qTl_d),
                                   ("k", "wk", bk_sb, kTl_d)):
            for c in range(NCH):
                slot = c % 2
                for g_ in range(4):
                    jj = g_ % 2

                    @blk.tensor
                    def _(t):
                        if s == 0 and kind == "q" and c == 0 and g_ == 0:
                            ctx.wait(t, "sgd", w_loaded)
                            ctx.wait(t, "sd", misc_loaded)
                        ctx.wait(t, sxl, xn_ready)
                        if ("s6a_ev", jj) in io:
                            ctx.wait(t, "sv", io[("s6a_ev", jj)])
                        for kc in range(NCH):
                            ins = t.matmul(
                                ps_a[jj][:],
                                wsb[wa][:, kc * D + c * 128:kc * D + (c + 1) * 128],
                                xnT_sb[:, kc * S + g_ * 512:kc * S + g_ * 512 + 512],
                                start=(kc == 0), stop=(kc == NCH - 1))
                        io["_s6a_mm"] = ctx.inc("st", ins)

                    @blk.vector
                    def _(v):
                        ctx.wait(v, "st", io["_s6a_mm"])
                        if g_ == 0:
                            sem = scb[(0 if kind == "q" else 2) + slot]
                            ctx.wait(v, sem, ctx.val(sem))
                        c2 = ts(v, cbuf[kind][slot][:, g_ * 512:(g_ + 1) * 512],
                                ps_a[jj][:], ba[:, c:c + 1], ALU.add)
                        io[("s6a_ev", jj)] = ctx.inc("sv", c2)
                        io["_s6a_evl"] = io[("s6a_ev", jj)]

                @blk.gpsimd
                def _(g):
                    ctx.wait(g, "sv", io["_s6a_evl"])
                    sem = scb[(0 if kind == "q" else 2) + slot]
                    ctx.inc(sem, g.dma_start(
                        out=dram[s, c * 128:(c + 1) * 128, :],
                        in_=cbuf[kind][slot][:]), 16)

        for kind2, wname, brow, dram2 in (("v", "wv", bv_sb, vl_d),
                                          ("ktm", "wk", bkr_sb, ktm_d)):
            for tt in range(NT):
                for hf in range(2):
                    j = (tt * 2 + hf) % 2

                    @blk.tensor
                    def _(t):
                        if s == 0 and kind2 == "v" and tt == 0 and hf == 0:
                            ctx.wait(t, "sv", io["_s6a_ones"])
                        if ("s6a_vev", j) in io:
                            ctx.wait(t, "sv", io[("s6a_vev", j)])
                        for kc in range(NCH):
                            t.matmul(ps_v[j][:],
                                     xnT_sb[:, kc * S + tt * 128:kc * S + (tt + 1) * 128],
                                     wsb[wname][:, kc * D + hf * 512:kc * D + (hf + 1) * 512],
                                     start=(kc == 0), stop=False)
                        ins = t.matmul(ps_v[j][:], ones_bf[:],
                                       brow[:, hf * 512:(hf + 1) * 512],
                                       start=False, stop=True)
                        io["_s6a_vmm"] = ctx.inc("st", ins)

                    @blk.vector
                    def _(vE):
                        ctx.wait(vE, "st", io["_s6a_vmm"])
                        c2 = vE.tensor_copy(
                            v_sb[:, tt * D + hf * 512:tt * D + (hf + 1) * 512],
                            ps_v[j][:])
                        io[("s6a_vev", j)] = ctx.inc("sv", c2)
                        io["_s6a_vevl"] = io[("s6a_vev", j)]

            @blk.gpsimd
            def _(g):
                ctx.wait(g, "sv", io[("s6a_vev", 0)])
                ctx.wait(g, "sv", io[("s6a_vev", 1)])
                dst3 = dram2[s].rearrange("(tt p) f -> p tt f", p=128)
                src3 = v_sb[:].rearrange("p (tt f) -> p tt f", f=D)
                ctx.inc("sgd", g.dma_start(out=dst3, in_=src3), 16)
            io["_s6a_vstore"] = ctx.val("sgd")

            @blk.vector
            def _(vE):
                ctx.wait(vE, "sgd", io["_s6a_vstore"])

    io["s6a_done_sgd"] = ctx.val("sgd")



def emit_stage6b(ctx, blk, io):
    nc, E = ctx.nc, ctx.E
    qTl_d, kTl_d, vl_d, oTl_d = io["qTl"], io["kTl"], io["vl"], io["oTl"]
    ktm_d = io["ktm"]
    mt_d = io["decay_mt"]
    dup2_d = io["dup2"]
    wdec_d = io["wdec"]
    lamq_d = io["lamq"]
    LAM128 = 0.95 ** 128
    ident_bf = io["ident_bf"]

    qT_sb = E(nc.sbuf_tensor("s6b_qT", [128, NCH * S], BF16))
    kT_sb = E(nc.sbuf_tensor("s6b_kT", [128, NCH * S], BF16))
    v_sb = E(nc.sbuf_tensor("s6b_v", [128, NT * D], BF16))
    ktm_sb = E(nc.sbuf_tensor("s6b_ktm", [128, NT * D], BF16))
    mt_sb = E(nc.sbuf_tensor("s6b_mt", [128, 128], F32))
    wdec = E(nc.sbuf_tensor("s6b_wdec", [128, 1], F32))
    lamq = E(nc.sbuf_tensor("s6b_lamq", [128, 1], F32))
    Am = [E(nc.sbuf_tensor(f"s6b_am{j}", [128, 128], BF16)) for j in range(2)]
    Kd = E(nc.sbuf_tensor("s6b_kd", [128, 64], BF16))
    G = E(nc.sbuf_tensor("s6b_G", [64, 64], F32))
    Gb = E(nc.sbuf_tensor("s6b_Gb", [128, 64], BF16))
    dup2 = E(nc.sbuf_tensor("s6b_dup2", [64, 128], F32))
    tb = E(nc.sbuf_tensor("s6b_tb", [128, 64], F32))
    Ob = E(nc.sbuf_tensor("s6b_Ob", [128, 64], BF16))
    oTl_asm = E(nc.sbuf_tensor("s6b_asm", [128, NCH * S], BF16))
    ps_sA = E(nc.psum_tensor("s6b_psA", [128, 128], F32))
    ps_oA = E(nc.psum_tensor("s6b_poA", [128, 64], F32))
    ps_g = E(nc.psum_tensor("s6b_pg", [64, 64], F32))
    ps_oB = E(nc.psum_tensor("s6b_poB", [128, 64], F32))
    ps_gb = E(nc.psum_tensor("s6b_pgb", [128, 64], F32))
    ps_ot = E(nc.psum_tensor("s6b_pot", [64, 128], BF16))

    sld = ctx.make_sems("s6bld", 4)

    @blk.sync
    def _(q):
        ctx.inc("sd", q.dma_start(out=mt_sb[:], in_=mt_d), 16)
        ctx.inc("sd", q.dma_start(out=wdec[:], in_=wdec_d), 16)
        ctx.inc("sd", q.dma_start(out=lamq[:], in_=lamq_d), 16)
        ctx.inc("sd", q.dma_start(out=dup2[:], in_=dup2_d), 16)
    misc_ld = ctx.val("sd")

    for s in range(NSEQ):
        @blk.sync
        def _(q):
            if s > 0:
                ctx.wait(q, "sgd", io["_s6b_store"])
            for dsem, d_, sb, il in ((sld[0], qTl_d, qT_sb, "(c p) t -> p c t"),
                                     (sld[1], kTl_d, kT_sb, "(c p) t -> p c t"),
                                     (sld[2], vl_d, v_sb, "(tt p) f -> p tt f"),
                                     (sld[3], ktm_d, ktm_sb, "(tt p) f -> p tt f")):
                src3 = d_[s].rearrange(il, p=128)
                if sb is v_sb or sb is ktm_sb:
                    dst3 = sb[:].rearrange("p (tt f) -> p tt f", f=D)
                else:
                    dst3 = sb[:].rearrange("p (c t) -> p c t", t=S)
                ctx.inc(dsem, q.dma_start(out=dst3, in_=src3), 16)
        lds = [ctx.val(x) for x in sld]

        for h in range(16):
            ch, po = h // 2, (h % 2) * 64

            @blk.vector
            def _(v):
                if s == 0 and h == 0:
                    ctx.wait(v, "sd", misc_ld)
                c = v.memset(G[:], 0.0)
                io["_s6b_G0"] = ctx.inc("sv", c)

            for ct in range(NT):
                ja = ct % 2

                @blk.tensor
                def _(t):
                    ctx.wait(t, sld[0], lds[0])
                    ctx.wait(t, sld[1], lds[1])
                    ctx.wait(t, sld[2], lds[2])
                    ctx.wait(t, sld[3], lds[3])
                    if ("s6b_amL",) in io:
                        ctx.wait(t, "sv", io[("s6b_amL",)])
                    c = t.matmul(ps_sA[:],
                                 kT_sb[po:po + 64, ch * S + ct * 128:ch * S + (ct + 1) * 128],
                                 qT_sb[po:po + 64, ch * S + ct * 128:ch * S + (ct + 1) * 128],
                                 start=True, stop=True)
                    io["_s6b_sA"] = ctx.inc("st", c)

                @blk.vector
                def _(v):
                    ctx.wait(v, "st", io["_s6b_sA"])
                    c = v.tensor_tensor(out=Am[ja][:], in0=ps_sA[:], in1=mt_sb[:],
                                        op=ALU.mult)
                    io[("s6b_am", ja)] = ctx.inc("sv", c)
                    io[("s6b_amL",)] = io[("s6b_am", ja)]
                    if ("s6b_gmm",) in io:
                        ctx.wait(v, "st", io[("s6b_gmm",)])
                    c = ts(v, Kd[:], ktm_sb[:, ct * D + h * 64:ct * D + (h + 1) * 64],
                           wdec[:], ALU.mult)
                    io[("s6b_kd",)] = ctx.inc("sv", c)

                @blk.tensor
                def _(t):
                    ctx.wait(t, "sv", io[("s6b_am", ja)])
                    if ("s6b_obc",) in io:
                        ctx.wait(t, "sv", io[("s6b_obc",)])
                    c = t.matmul(ps_oA[:], Am[ja][:],
                                 v_sb[:, ct * D + h * 64:ct * D + (h + 1) * 64],
                                 start=True, stop=True)
                    io["_s6b_oA"] = ctx.inc("st", c)
                    ctx.wait(t, "sv", io[("s6b_kd",)])
                    if ("s6b_gup",) in io:
                        ctx.wait(t, "sv", io[("s6b_gup",)])
                    c = t.matmul(ps_g[:], Kd[:],
                                 v_sb[:, ct * D + h * 64:ct * D + (h + 1) * 64],
                                 start=True, stop=True)
                    io["_s6b_g"] = ctx.inc("st", c)
                    io[("s6b_gmm",)] = io["_s6b_g"]
                    if ct > 0:
                        c = t.matmul(ps_oB[:],
                                     qT_sb[po:po + 64, ch * S + ct * 128:ch * S + (ct + 1) * 128],
                                     Gb[po:po + 64, :], start=True, stop=True)
                        io[("s6b_oB",)] = ctx.inc("st", c)

                @blk.vector
                def _(v):
                    ctx.wait(v, "st", io["_s6b_oA"])
                    if ct > 0:
                        ctx.wait(v, "st", io[("s6b_oB",)])
                        c = ts(v, tb[:], ps_oB[:], lamq[:], ALU.mult)
                        ctx.wait(v, "sv", ctx.inc("sv", c))
                        c = v.tensor_tensor(out=Ob[:], in0=ps_oA[:], in1=tb[:],
                                            op=ALU.add)
                    else:
                        c = v.tensor_copy(Ob[:], ps_oA[:])
                    io["_s6b_ob"] = ctx.inc("sv", c)
                    ctx.wait(v, "st", io["_s6b_g"])
                    c = v.scalar_tensor_tensor(out=G[:], in0=G[:], scalar=LAM128,
                                               in1=ps_g[:], op0=ALU.mult,
                                               op1=ALU.add)
                    io[("s6b_gup",)] = ctx.inc("sv", c)

                @blk.tensor
                def _(t):
                    ctx.wait(t, "sv", io["_s6b_ob"])
                    if ("s6b_otc",) in io:
                        ctx.wait(t, "sv", io[("s6b_otc",)])
                    c = t.transpose(ps_ot[:], Ob[:], ident_bf[:])
                    io["_s6b_ot"] = ctx.inc("st", c)
                    ctx.wait(t, "sv", io[("s6b_gup",)])
                    c = t.matmul(ps_gb[:], dup2[:], G[:], start=True, stop=True)
                    io["_s6b_gbm"] = ctx.inc("st", c)

                @blk.vector
                def _(v):
                    ctx.wait(v, "st", io["_s6b_gbm"])
                    if ct > 0:
                        ctx.wait(v, "st", io[("s6b_oB",)])
                    c = v.tensor_copy(Gb[:], ps_gb[:])
                    io["_s6b_gb"] = ctx.inc("sv", c)

                @blk.tensor
                def _(t):
                    if ct < NT - 1:
                        ctx.wait(t, "sv", io["_s6b_gb"])

                @blk.vector
                def _(v):
                    ctx.wait(v, "st", io["_s6b_ot"])
                    dst = oTl_asm[po:po + 64, ch * S + ct * 128:ch * S + (ct + 1) * 128]
                    c = v.tensor_copy(dst, ps_ot[:])
                    io[("s6b_otc",)] = ctx.inc("sv", c)

        @blk.gpsimd
        def _(g):
            ctx.wait(g, "sv", io[("s6b_otc",)])
            for p in range(NCH):
                ctx.inc("sgd", g.dma_start(
                    out=oTl_d[s, p * 128:(p + 1) * 128, :],
                    in_=oTl_asm[:, p * S:(p + 1) * S]), 16)
        io["_s6b_store"] = ctx.val("sgd")

    io["s6b_done_sgd"] = ctx.val("sgd")



def emit_stage6c(ctx, blk, io):
    nc, E = ctx.nc, ctx.E
    oTl_d, s1_d = io["oTl"], io["s1"]
    wo_d = io["L1Wo"]
    bo_d = io["l1bo_row"]
    gam_d, bet_d = io["l1gam_row"], io["l1bet_row"]

    oT_sb = E(nc.sbuf_tensor("s6c_oT", [128, NCH * S], BF16))
    wo_sb = E(nc.sbuf_tensor("s6c_wo", [128, NCH * D], BF16))
    bo_row = E(nc.sbuf_tensor("s6c_bor", [1, D], F32))
    gam_row = E(nc.sbuf_tensor("s6c_gam", [1, D], F32))
    bet_row = E(nc.sbuf_tensor("s6c_bet", [1, D], F32))
    onesf = E(nc.sbuf_tensor("s6c_ones", [1, 128], F32))
    onesc = E(nc.sbuf_tensor("s6c_onesc", [128, 1], F32))
    bo_exp = E(nc.sbuf_tensor("s6c_boe", [128, D], F32))
    y_sb = E(nc.sbuf_tensor("s6c_y", [128, NT * D], F32))
    sq_scr = E(nc.sbuf_tensor("s6c_sq", [128, D], F32))
    acc_s = E(nc.sbuf_tensor("s6c_accs", [128, 16], F32))
    acc_q = E(nc.sbuf_tensor("s6c_accq", [128, 16], F32))
    srow = E(nc.sbuf_tensor("s6c_srow", [1, 16], F32))
    qrow = E(nc.sbuf_tensor("s6c_qrow", [1, 16], F32))
    mrow = E(nc.sbuf_tensor("s6c_mrow", [1, 16], F32))
    vrow = E(nc.sbuf_tensor("s6c_vrow", [1, 16], F32))
    rrow = E(nc.sbuf_tensor("s6c_rrow", [1, 16], F32))
    arow = E(nc.sbuf_tensor("s6c_arow", [1, D], F32))
    brow = E(nc.sbuf_tensor("s6c_brow", [1, D], F32))
    epsr = E(nc.sbuf_tensor("s6c_epsr", [1, 16], F32))
    a_exp = E(nc.sbuf_tensor("s6c_aexp", [128, D], F32))
    b_exp = E(nc.sbuf_tensor("s6c_bexp", [128, D], F32))
    s1t = [E(nc.sbuf_tensor(f"s6c_s1t{j}", [128, D], F32)) for j in range(2)]
    ps_s = [E(nc.psum_tensor(f"s6c_pss{j}", [128, 512], F32)) for j in range(2)]
    ps_b = E(nc.psum_tensor("s6c_psb", [128, D], F32))
    ps_r = E(nc.psum_tensor("s6c_psr", [1, 16], F32))

    sxl = ctx.make_sems("s6cxl", 1)[0]
    so1 = ctx.make_sems("s6co", 2)

    @blk.gpsimd
    def _(g):
        src3 = wo_d.rearrange("(c p) o -> p c o", p=128)
        ctx.inc("sgd", g.dma_start(
            out=wo_sb[:].rearrange("p (c o) -> p c o", o=D), in_=src3), 16)
    w_ld = ctx.val("sgd")

    @blk.sync
    def _(q):
        ctx.inc("sd", q.dma_start(out=bo_row[:], in_=bo_d), 16)
        ctx.inc("sd", q.dma_start(out=gam_row[:], in_=gam_d), 16)
        ctx.inc("sd", q.dma_start(out=bet_row[:], in_=bet_d), 16)
    b_ld = ctx.val("sd")

    @blk.vector
    def _(v):
        v.memset(onesf[:], 1.0)
        v.memset(onesc[:], 1.0)
        c = v.memset(epsr[:], 1e-5)
        io["_s6c_ones"] = ctx.inc("sv", c)

    @blk.tensor
    def _(t):
        ctx.wait(t, "sd", b_ld)
        ctx.wait(t, "sv", io["_s6c_ones"])
        for half in range(2):
            c = t.matmul(ps_b[:, half * 512:(half + 1) * 512], onesf[:],
                         bo_row[:, half * 512:(half + 1) * 512],
                         start=True, stop=True)
        io["_s6c_bomm"] = ctx.inc("st", c)

    @blk.vector
    def _(v):
        ctx.wait(v, "st", io["_s6c_bomm"])
        c = v.tensor_copy(bo_exp[:], ps_b[:])
        io["_s6c_boe"] = ctx.inc("sv", c)

    for s in range(NSEQ):
        @blk.sync
        def _(q):
            if s > 0:
                ctx.wait(q, "sv", io["_s6c_applydone"])
            src3 = oTl_d[s].rearrange("(c p) t -> p c t", p=128)
            ctx.inc(sxl, q.dma_start(
                out=oT_sb[:].rearrange("p (c t) -> p c t", t=S), in_=src3), 16)
        oT_ld = ctx.val(sxl)

        @blk.vector
        def _(v):
            if s > 0:
                ctx.wait(v, "sv", io["_s6c_applydone"])
            v.memset(acc_s[:], 0.0)
            c = v.memset(acc_q[:], 0.0)
            io["_s6c_acc0"] = ctx.inc("sv", c)

        for tt in range(NT):
            j = tt % 2
            for half in range(2):
                @blk.tensor
                def _(t):
                    if s == 0 and tt == 0 and half == 0:
                        ctx.wait(t, "sgd", w_ld)
                    ctx.wait(t, sxl, oT_ld)
                    if ("s6c_sev", j) in io:
                        ctx.wait(t, "sv", io[("s6c_sev", j)])
                    for kc in range(NCH):
                        c = t.matmul(
                            ps_s[j][:],
                            oT_sb[:, kc * S + tt * 128:kc * S + (tt + 1) * 128],
                            wo_sb[:, kc * D + half * 512:kc * D + (half + 1) * 512],
                            start=(kc == 0), stop=(kc == NCH - 1))
                    io[("s6c_smm", j, half)] = ctx.inc("st", c)

                @blk.vector
                def _(v):
                    ctx.wait(v, "st", io[("s6c_smm", j, half)])
                    if s == 0 and tt == 0 and half == 0:
                        ctx.wait(v, "sv", io["_s6c_boe"])
                    c = v.tensor_tensor(
                        out=y_sb[:, tt * D + half * 512:tt * D + (half + 1) * 512],
                        in0=ps_s[j][:],
                        in1=bo_exp[:, half * 512:(half + 1) * 512], op=ALU.add)
                    io[("s6c_sev", j)] = ctx.inc("sv", c)

            @blk.vector
            def _(v):
                ctx.wait(v, "sv", io["_s6c_acc0"])
                yv = y_sb[:, tt * D:(tt + 1) * D].rearrange("p (g e) -> p g e", e=64)
                c = v.tensor_reduce(out=sq_scr[:, 0:16], in_=yv, axis=AX.X,
                                    op=ALU.add)
                ctx.wait(v, "sv", ctx.inc("sv", c))
                c = v.tensor_tensor(out=acc_s[:], in0=acc_s[:],
                                    in1=sq_scr[:, 0:16], op=ALU.add)
                io["_s6c_sacc"] = ctx.inc("sv", c)

            @blk.scalar
            def _(sE):
                ctx.wait(sE, "sv", io["_s6c_sacc"])
                c = sE.activation(sq_scr[:], y_sb[:, tt * D:(tt + 1) * D],
                                  AF.Square, bias=0.0, scale=1.0)
                io["_s6c_sq"] = ctx.inc("ss", c)

            @blk.vector
            def _(v):
                ctx.wait(v, "ss", io["_s6c_sq"])
                sv3 = sq_scr[:].rearrange("p (g e) -> p g e", e=64)
                c = v.tensor_reduce(out=sq_scr[:, 0:16], in_=sv3, axis=AX.X,
                                    op=ALU.add)
                ctx.wait(v, "sv", ctx.inc("sv", c))
                c = v.tensor_tensor(out=acc_q[:], in0=acc_q[:],
                                    in1=sq_scr[:, 0:16], op=ALU.add)
                io["_s6c_qacc"] = ctx.inc("sv", c)

        @blk.tensor
        def _(t):
            ctx.wait(t, "sv", io["_s6c_qacc"])
            c = t.matmul(ps_r[:], onesc[:], acc_s[:], start=True, stop=True)
            io["_s6c_psum1"] = ctx.inc("st", c)

        @blk.vector
        def _(v):
            ctx.wait(v, "st", io["_s6c_psum1"])
            c = v.tensor_copy(srow[:], ps_r[:])
            io["_s6c_srow"] = ctx.inc("sv", c)

        @blk.tensor
        def _(t):
            ctx.wait(t, "sv", io["_s6c_srow"])
            c = t.matmul(ps_r[:], onesc[:], acc_q[:], start=True, stop=True)
            io["_s6c_psum2"] = ctx.inc("st", c)

        @blk.vector
        def _(v):
            ctx.wait(v, "st", io["_s6c_psum2"])
            c = v.tensor_copy(qrow[:], ps_r[:])
            ctx.wait(v, "sv", ctx.inc("sv", c))
            NTOT = 64.0 * S
            c = ts(v, mrow[:], srow[:], 1.0 / NTOT, ALU.mult)
            ctx.wait(v, "sv", ctx.inc("sv", c))
            c = v.tensor_tensor(out=vrow[:], in0=mrow[:], in1=mrow[:], op=ALU.mult)
            ctx.wait(v, "sv", ctx.inc("sv", c))
            c = v.scalar_tensor_tensor(out=vrow[:], in0=qrow[:], scalar=1.0 / NTOT,
                                       in1=vrow[:], op0=ALU.mult, op1=ALU.subtract)
            ctx.wait(v, "sv", ctx.inc("sv", c))
            c = v.tensor_tensor(out=vrow[:], in0=vrow[:], in1=epsr[:], op=ALU.add)
            io["_s6c_var"] = ctx.inc("sv", c)

        @blk.scalar
        def _(sE):
            ctx.wait(sE, "sv", io["_s6c_var"])
            c = sE.activation(rrow[:], vrow[:], AF.Sqrt, bias=0.0, scale=1.0)
            io["_s6c_sd"] = ctx.inc("ss", c)

        @blk.vector
        def _(v):
            ctx.wait(v, "ss", io["_s6c_sd"])
            c = v.reciprocal(rrow[:], rrow[:])
            ctx.wait(v, "sv", ctx.inc("sv", c))
            gv = gam_row[:].rearrange("p (g e) -> p g e", e=64)
            rv = AP(rrow.ap().tensor, rrow.ap().offset,
                    [list(rrow.ap().ap[0]), list(rrow.ap().ap[1]), [0, 64]])
            av = arow[:].rearrange("p (g e) -> p g e", e=64)
            c = v.tensor_tensor(out=av, in0=gv, in1=rv, op=ALU.mult)
            ctx.wait(v, "sv", ctx.inc("sv", c))
            mv = AP(mrow.ap().tensor, mrow.ap().offset,
                    [list(mrow.ap().ap[0]), list(mrow.ap().ap[1]), [0, 64]])
            bv3 = brow[:].rearrange("p (g e) -> p g e", e=64)
            c = v.tensor_tensor(out=bv3, in0=arow[:].rearrange("p (g e) -> p g e", e=64),
                                in1=mv, op=ALU.mult)
            ctx.wait(v, "sv", ctx.inc("sv", c))
            c = v.tensor_tensor(out=brow[:], in0=bet_row[:], in1=brow[:],
                                op=ALU.subtract)
            io["_s6c_ab"] = ctx.inc("sv", c)

        @blk.tensor
        def _(t):
            ctx.wait(t, "sv", io["_s6c_ab"])
            for half in range(2):
                c = t.matmul(ps_b[:, half * 512:(half + 1) * 512], onesf[:],
                             arow[:, half * 512:(half + 1) * 512],
                             start=True, stop=True)
            io["_s6c_amm"] = ctx.inc("st", c)

        @blk.vector
        def _(v):
            ctx.wait(v, "st", io["_s6c_amm"])
            c = v.tensor_copy(a_exp[:], ps_b[:])
            io["_s6c_aexp"] = ctx.inc("sv", c)

        @blk.tensor
        def _(t):
            ctx.wait(t, "sv", io["_s6c_aexp"])
            for half in range(2):
                c = t.matmul(ps_b[:, half * 512:(half + 1) * 512], onesf[:],
                             brow[:, half * 512:(half + 1) * 512],
                             start=True, stop=True)
            io["_s6c_bmm"] = ctx.inc("st", c)

        @blk.vector
        def _(v):
            ctx.wait(v, "st", io["_s6c_bmm"])
            c = v.tensor_copy(b_exp[:], ps_b[:])
            io["_s6c_bexp"] = ctx.inc("sv", c)

        for tt in range(NT):
            j = tt % 2

            @blk.vector
            def _(v):
                if tt == 0:
                    ctx.wait(v, "sv", io["_s6c_bexp"])
                ctx.wait(v, so1[j], ctx.val(so1[j]))
                c = v.tensor_tensor(out=s1t[j][:], in0=y_sb[:, tt * D:(tt + 1) * D],
                                    in1=a_exp[:], op=ALU.mult)
                ctx.wait(v, "sv", ctx.inc("sv", c))
                c = v.tensor_tensor(out=s1t[j][:], in0=s1t[j][:], in1=b_exp[:],
                                    op=ALU.add)
                io[("s6c_app", j)] = ctx.inc("sv", c)
                io["_s6c_applydone"] = io[("s6c_app", j)]

            @blk.gpsimd
            def _(g):
                ctx.wait(g, "sv", io[("s6c_app", j)])
                ctx.inc(so1[j], g.dma_start(
                    out=s1_d[s, tt * 128:(tt + 1) * 128, :], in_=s1t[j][:]), 16)

    io["s6c_done_sgd"] = ctx.val("sgd")



def emit_stage7(ctx, blk, io):
    nc, E = ctx.nc, ctx.E
    xhatT_d, mask_d, acc_d = io["xhatT"], io["mask"], io["acc"]
    w1_d, w2_d = io["w1"], io["w2"]
    ga_d, be_d = io["ga_pc"], io["be_pc"]
    b1_d = io["b1_pc"]
    b2_d = io["b2_rows"]
    FF = 4096

    w1_sb = E(nc.sbuf_tensor("s7_w1", [128, NCH * FF], BF16))
    w2_sb = E(nc.sbuf_tensor("s7_w2", [128, (FF // 128) * D], BF16))
    hT_sb = E(nc.sbuf_tensor("s7_hT", [128, NCH * S], BF16))
    stage = [E(nc.sbuf_tensor(f"s7_stg{j}", [128, S], BF16)) for j in range(2)]
    ga_sb = E(nc.sbuf_tensor("s7_ga", [128, 64], F32))
    be_sb = E(nc.sbuf_tensor("s7_be", [128, 64], F32))
    b1_sb = E(nc.sbuf_tensor("s7_b1", [128, 8 * 32], F32))
    b2_row = E(nc.sbuf_tensor("s7_b2r", [1, D], F32))
    b2_exp = E(nc.sbuf_tensor("s7_b2e", [128, D], F32))
    onesf = E(nc.sbuf_tensor("s7_ones", [1, 128], F32))
    mask_sb = E(nc.sbuf_tensor("s7_mask", [128, NT * 8], F32))
    aT_sb = [E(nc.sbuf_tensor(f"s7_aT{j}", [128, 4 * 256], BF16)) for j in range(2)]
    zt = [E(nc.sbuf_tensor(f"s7_zt{j}", [128, D], F32)) for j in range(2)]
    ps_a = [E(nc.psum_tensor(f"s7_psa{j}", [128, 256], F32)) for j in range(2)]
    ps_z = [E(nc.psum_tensor(f"s7_psz{j}", [128, D], F32)) for j in range(2)]
    ps_b = E(nc.psum_tensor("s7_psb", [128, D], F32))

    sw = ctx.make_sems("s7w", 2)
    sxh = ctx.make_sems("s7xh", 1)[0]
    smk = ctx.make_sems("s7mk", 1)[0]
    szo = ctx.make_sems("s7zo", 2)

    @blk.sync
    def _(q):
        ctx.inc("sd", q.dma_start(out=ga_sb[:], in_=ga_d), 16)
        ctx.inc("sd", q.dma_start(out=be_sb[:], in_=be_d), 16)
        ctx.inc("sd", q.dma_start(out=b1_sb[:], in_=b1_d), 16)
    misc_ld = ctx.val("sd")

    @blk.vector
    def _(v):
        c = v.memset(onesf[:], 1.0)
        io["_s7_ones"] = ctx.inc("sv", c)

    for e in range(8):
        @blk.gpsimd
        def _(g):
            if e > 0:
                ctx.wait(g, "st", io["_s7_lastz"])
            src1 = w1_d[e].rearrange("(c p) f -> p c f", p=128)
            ctx.inc(sw[0], g.dma_start(
                out=w1_sb[:].rearrange("p (c f) -> p c f", f=FF), in_=src1), 16)
            src2 = w2_d[e].rearrange("(c p) f -> p c f", p=128)
            ctx.inc(sw[1], g.dma_start(
                out=w2_sb[:].rearrange("p (c f) -> p c f", f=D), in_=src2), 16)
        w_ld = [ctx.val(sw[0]), ctx.val(sw[1])]

        @blk.sync
        def _(q):
            if e > 0:
                ctx.wait(q, "sv", io["_s7_lastzev"])
            ctx.inc("sd", q.dma_start(out=b2_row[:], in_=b2_d[e:e + 1, :]), 16)
        b2_ld = ctx.val("sd")

        @blk.tensor
        def _(t):
            ctx.wait(t, "sd", b2_ld)
            ctx.wait(t, "sv", io["_s7_ones"])
            if ("s7_b2e",) in io:
                ctx.wait(t, "sv", io[("s7_b2e",)])
            for half in range(2):
                c = t.matmul(ps_b[:, half * 512:(half + 1) * 512], onesf[:],
                             b2_row[:, half * 512:(half + 1) * 512],
                             start=True, stop=True)
            io["_s7_b2mm"] = ctx.inc("st", c)

        @blk.vector
        def _(v):
            ctx.wait(v, "st", io["_s7_b2mm"])
            c = v.tensor_copy(b2_exp[:], ps_b[:])
            io[("s7_b2e",)] = ctx.inc("sv", c)

        for s in range(NSEQ):
            @blk.sync
            def _(q):
                if "_s7_lastamm" in io:
                    ctx.wait(q, "st", io["_s7_lastamm"])
                if "_s7_lastzev" in io:
                    ctx.wait(q, "sv", io["_s7_lastzev"])
                src3 = xhatT_d[s].rearrange("(c p) t -> p c t", p=128)
                ctx.inc(sxh, q.dma_start(
                    out=stage[0][:].rearrange("p (c t) -> p c t", t=S)
                    if False else hT_sb[:].rearrange("p (c t) -> p c t", t=S),
                    in_=src3), 16)
                src4 = mask_d[s].rearrange("(tt p) e -> p tt e", p=128)
                ctx.inc(smk, q.dma_start(
                    out=mask_sb[:].rearrange("p (tt e) -> p tt e", e=8),
                    in_=src4), 16)
            xh_ld, mk_ld = ctx.val(sxh), ctx.val(smk)

            @blk.vector
            def _(v):
                ctx.wait(v, sxh, xh_ld)
                ctx.wait(v, "sd", misc_ld)
                if "_s7_lastamm" in io:
                    ctx.wait(v, "st", io["_s7_lastamm"])
                for c_ in range(NCH):
                    c = ts(v, hT_sb[:, c_ * S:(c_ + 1) * S],
                           hT_sb[:, c_ * S:(c_ + 1) * S],
                           ga_sb[:, e * 8 + c_:e * 8 + c_ + 1],
                           ALU.mult, be_sb[:, e * 8 + c_:e * 8 + c_ + 1], ALU.add)
                io["_s7_hT"] = ctx.inc("sv", c)
                io["_s7_hTdone"] = io["_s7_hT"]

            for grp in range(8):
                gt0 = grp * 256
                for f128 in range(32):
                    jA = f128 % 2

                    @blk.tensor
                    def _(t):
                        if f128 == 0 and grp == 0:
                            ctx.wait(t, sw[0], w_ld[0])
                            ctx.wait(t, sw[1], w_ld[1])
                            ctx.wait(t, "sv", io["_s7_hT"])
                        if ("s7_rel", jA) in io:
                            ctx.wait(t, "sv", io[("s7_rel", jA)])
                        for dc in range(NCH):
                            c = t.matmul(
                                ps_a[jA][:],
                                w1_sb[:, dc * FF + f128 * 128:dc * FF + (f128 + 1) * 128],
                                hT_sb[:, dc * S + gt0:dc * S + gt0 + 256],
                                start=(dc == 0), stop=(dc == NCH - 1))
                        io[("s7_amm", jA)] = ctx.inc("st", c)
                        io["_s7_lastamm"] = io[("s7_amm", jA)]

                    @blk.vector
                    def _(v):
                        ctx.wait(v, "st", io[("s7_amm", jA)])
                        if ("s7_zmm", jA) in io:
                            ctx.wait(v, "st", io[("s7_zmm", jA)])
                        c = ts(v, aT_sb[jA][:, (f128 % 4) * 256:(f128 % 4 + 1) * 256],
                               ps_a[jA][:],
                               b1_sb[:, e * 32 + f128:e * 32 + f128 + 1],
                               ALU.add, 0.0, ALU.max)
                        io[("s7_rel", jA)] = ctx.inc("sv", c)

                    @blk.tensor
                    def _(t):
                        ctx.wait(t, "sv", io[("s7_rel", jA)])
                        if f128 < 2 and "_s7_lastzev" in io:
                            ctx.wait(t, "sv", io["_s7_lastzev"])
                        for tk in range(2):
                            for half in range(2):
                                c = t.matmul(
                                    ps_z[tk][:, half * 512:(half + 1) * 512],
                                    aT_sb[jA][:, (f128 % 4) * 256 + tk * 128:
                                               (f128 % 4) * 256 + (tk + 1) * 128],
                                    w2_sb[:, f128 * D + half * 512:
                                          f128 * D + (half + 1) * 512],
                                    start=(f128 == 0), stop=(f128 == 31))
                        io[("s7_zmm", jA)] = ctx.inc("st", c)
                        io[("s7_mmdone",)] = io[("s7_zmm", jA)]
                        io["_s7_lastz"] = io[("s7_zmm", jA)]

                for tk in range(2):
                    jz = tk

                    @blk.vector
                    def _(v):
                        ctx.wait(v, "st", io[("s7_zmm", 1)])
                        ctx.wait(v, "st", io[("s7_zmm", 0)])
                        ctx.wait(v, smk, mk_ld)
                        ctx.wait(v, "sv", io[("s7_b2e",)])
                        ctx.wait(v, szo[jz], ctx.val(szo[jz]))
                        c = v.tensor_tensor(out=zt[jz][:], in0=ps_z[tk][:],
                                            in1=b2_exp[:], op=ALU.add)
                        ctx.wait(v, "sv", ctx.inc("sv", c))
                        tidx = grp * 2 + tk
                        c = ts(v, zt[jz][:], zt[jz][:],
                               mask_sb[:, tidx * 8 + e:tidx * 8 + e + 1], ALU.mult)
                        io[("s7_zev", grp % 2)] = ctx.inc("sv", c)
                        io["_s7_lastzev"] = io[("s7_zev", grp % 2)]

                    @blk.gpsimd
                    def _(g):
                        ctx.wait(g, "sv", io[("s7_zev", grp % 2)])
                        tidx = grp * 2 + tk
                        if e == 0:
                            ctx.inc(szo[jz], g.dma_start(
                                out=acc_d[s, tidx * 128:(tidx + 1) * 128, :],
                                in_=zt[jz][:]), 16)
                        else:
                            ctx.inc(szo[jz], g.dma_start(
                                out=acc_d[s, tidx * 128:(tidx + 1) * 128, :],
                                in_=zt[jz][:], accum_op=ALU.add), 16)

    io["s7_done_sgd"] = ctx.val("sgd")



def emit_stage8(ctx, blk, io):
    nc, E = ctx.nc, ctx.E
    feats_d, acc_d, asel_d = io["feats"], io["acc"], io["asel"]
    s1_d, x1_d, out_d = io["s1"], io["x1"], io["out"]

    ft = [E(nc.sbuf_tensor(f"s8_ft{j}", [128, D], F32)) for j in range(2)]
    at = [E(nc.sbuf_tensor(f"s8_at{j}", [128, D], F32)) for j in range(2)]
    st_ = [E(nc.sbuf_tensor(f"s8_st{j}", [128, D], F32)) for j in range(2)]
    x1t = [E(nc.sbuf_tensor(f"s8_x1t{j}", [128, D], F32)) for j in range(2)]
    al = [E(nc.sbuf_tensor(f"s8_al{j}", [128, 1], F32)) for j in range(2)]
    ot = [E(nc.sbuf_tensor(f"s8_ot{j}", [128, D], F32)) for j in range(2)]
    sld = ctx.make_sems("s8ld", 2)

    for i in range(NSEQ * NT):
        s, t = divmod(i, NT)
        j = i % 2

        @blk.sync
        def _(q):
            if i >= 2:
                ctx.wait(q, "so", io[("s8_out", j)])
            sl = slice(t * 128, (t + 1) * 128)
            ctx.inc(sld[j], q.dma_start(out=ft[j][:], in_=feats_d[s, sl, :]), 16)
            ctx.inc(sld[j], q.dma_start(out=at[j][:], in_=acc_d[s, sl, :]), 16)
            ctx.inc(sld[j], q.dma_start(out=st_[j][:], in_=s1_d[s, sl, :]), 16)
            ctx.inc(sld[j], q.dma_start(out=x1t[j][:], in_=x1_d[s, sl, :]), 16)
            ctx.inc(sld[j], q.dma_start(out=al[j][:], in_=asel_d[s, sl, :]), 16)
        ld = ctx.val(sld[j])

        @blk.vector
        def _(v):
            ctx.wait(v, sld[j], ld)
            c = ts(v, at[j][:], at[j][:], al[j][:], ALU.mult)
            ctx.wait(v, "sv", ctx.inc("sv", c))
            c = v.tensor_tensor(out=at[j][:], in0=at[j][:], in1=ft[j][:],
                                op=ALU.add)
            io["_s8_pre"] = ctx.inc("sv", c)

        @blk.scalar
        def _(sE):
            ctx.wait(sE, "sv", io["_s8_pre"])
            c = sE.activation(at[j][:], at[j][:], AF.Silu, bias=0.0, scale=1.0)
            io["_s8_silu"] = ctx.inc("ss", c)

        @blk.vector
        def _(v):
            ctx.wait(v, "ss", io["_s8_silu"])
            c = v.tensor_tensor(out=ot[j][:], in0=at[j][:], in1=st_[j][:],
                                op=ALU.mult)
            ctx.wait(v, "sv", ctx.inc("sv", c))
            c = v.tensor_tensor(out=ot[j][:], in0=ot[j][:], in1=x1t[j][:],
                                op=ALU.add)
            io[("s8_o", j)] = ctx.inc("sv", c)

        @blk.gpsimd
        def _(g):
            ctx.wait(g, "sv", io[("s8_o", j)])
            io[("s8_out", j)] = ctx.inc("so", g.dma_start(
                out=out_d[s, t * 128:(t + 1) * 128, :], in_=ot[j][:]), 16)

    @blk.sync
    def _(q):
        ctx.wait(q, "so", ctx.val("so"))



def _host_prep(inputs):
    f = {}
    f32 = np.float32
    perm = np.arange(1024).reshape(-1, 2)[:, ::-1].reshape(-1)
    for nm in ("Wq", "Wk"):
        W = np.ascontiguousarray(inputs[f"L0_attn_{nm}"].astype(f32))
        f[f"{nm}T"] = np.ascontiguousarray(W.T)
        f[f"{nm}Ts"] = np.ascontiguousarray(W.T[:, perm])
    for bn in ("bq", "bk"):
        b = inputs[f"L0_attn_{bn}"].astype(f32)
        f[f"{bn}_pc"] = np.ascontiguousarray(b.reshape(8, 128).T)
        f[f"{bn}s_pc"] = np.ascontiguousarray(b[perm].reshape(8, 128).T)
    f["WvT"] = np.ascontiguousarray(inputs["L0_attn_Wv"].astype(f32).T)
    f["bv_row"] = inputs["L0_attn_bv"].astype(f32).reshape(1, D)
    f["WoT"] = np.ascontiguousarray(inputs["L0_attn_Wo"].astype(f32).T)
    f["bo_row"] = inputs["L0_attn_bo"].astype(f32).reshape(1, D)
    f["ffn_w"] = np.ascontiguousarray(inputs["L0_ffn_weight"].astype(f32))
    f["ffn_b_row"] = inputs["L0_ffn_bias"].astype(f32).reshape(1, D)
    for nm in ("Wq", "Wk", "Wv", "Wo"):
        f[f"L1{nm}"] = np.ascontiguousarray(inputs[f"L1_attn_{nm}"].astype(f32))
    for bn in ("bq", "bk"):
        b = inputs[f"L1_attn_{bn}"].astype(f32)
        f[f"l1{bn}_pc"] = np.ascontiguousarray(b.reshape(8, 128).T)
    f["l1bv_row"] = inputs["L1_attn_bv"].astype(f32).reshape(1, D)
    f["l1bk_row"] = inputs["L1_attn_bk"].astype(f32).reshape(1, D)
    f["l1bo_row"] = inputs["L1_attn_bo"].astype(f32).reshape(1, D)
    f["l1gam_row"] = inputs["L1_attn_gamma"].astype(f32).reshape(1, D)
    f["l1bet_row"] = inputs["L1_attn_beta"].astype(f32).reshape(1, D)
    f["centT"] = np.ascontiguousarray(inputs["L1_ffn_centroids"].astype(f32).T)
    gam = inputs["L1_ffn_gamma"].astype(f32)
    bet = inputs["L1_ffn_beta"].astype(f32)
    ga_pc = np.empty((128, 64), f32)
    be_pc = np.empty((128, 64), f32)
    for e in range(8):
        ga_pc[:, e * 8:(e + 1) * 8] = gam[e].reshape(8, 128).T
        be_pc[:, e * 8:(e + 1) * 8] = bet[e].reshape(8, 128).T
    f["ga_pc"], f["be_pc"] = ga_pc, be_pc
    b1 = inputs["L1_ffn_b1"].astype(f32)
    b1_pc = np.empty((128, 8 * 32), f32)
    for e in range(8):
        b1_pc[:, e * 32:(e + 1) * 32] = b1[e].reshape(32, 128).T
    f["b1_pc"] = b1_pc
    f["b2_rows"] = np.ascontiguousarray(inputs["L1_ffn_b2"].astype(f32))
    f["w1"] = np.ascontiguousarray(inputs["L1_ffn_w1"].astype(f32))
    f["w2"] = np.ascontiguousarray(inputs["L1_ffn_w2"].astype(f32))
    f["ident_bf"] = np.eye(128, dtype=ml_dtypes.bfloat16)
    f["ident_f32"] = np.eye(128, dtype=np.float32)
    d = 64
    inv_freq = (1.0 / 10000.0 ** (np.arange(0, d, 2, dtype=f32) / d))
    pos = np.arange(S, dtype=f32)
    th = pos[None, :] * inv_freq[:, None]
    cos = np.cos(th).astype(f32)
    sin = np.sin(th).astype(f32)
    cos64 = np.repeat(cos, 2, axis=0)
    sin64 = np.empty((64, S), f32)
    sin64[0::2] = -sin
    sin64[1::2] = sin
    f["cos_ext"] = np.ascontiguousarray(np.concatenate([cos64, cos64], axis=0))
    f["sin_ext"] = np.ascontiguousarray(np.concatenate([sin64, sin64], axis=0))
    e16 = np.zeros((128, 128), f32)
    for c in range(8):
        for r_ in range(128):
            hh = 2 * c + (1 if r_ >= 64 else 0)
            e16[r_, c * 16 + hh] = 1.0
    f["e16"] = e16
    lam = 0.95
    q_ = np.arange(128)
    f["decay_mt"] = np.where(q_[None, :] - q_[:, None] >= 0,
                             lam ** (q_[None, :] - q_[:, None]), 0.0).astype(f32)
    f["wdec"] = (lam ** (127 - q_)).astype(f32).reshape(128, 1)
    f["lamq"] = (lam ** (q_ + 1)).astype(f32).reshape(128, 1)
    d2 = np.zeros((64, 128), f32)
    for i in range(128):
        d2[i % 64, i] = 1.0
    f["dup2"] = d2
    return f


_IN_SPECS = [
    ("x", [NSEQ, S, D]), ("WqT", [D, D]), ("WqTs", [D, D]), ("WkT", [D, D]),
    ("WkTs", [D, D]), ("WvT", [D, D]), ("bq_pc", [128, 8]), ("bqs_pc", [128, 8]),
    ("bk_pc", [128, 8]), ("bks_pc", [128, 8]), ("bv_row", [1, D]),
    ("WoT", [D, D]), ("bo_row", [1, D]), ("ffn_w", [D, D]), ("ffn_b_row", [1, D]),
    ("L1Wq", [D, D]), ("L1Wk", [D, D]), ("L1Wv", [D, D]), ("L1Wo", [D, D]),
    ("l1bq_pc", [128, 8]), ("l1bk_pc", [128, 8]), ("l1bv_row", [1, D]),
    ("l1bk_row", [1, D]), ("l1bo_row", [1, D]), ("l1gam_row", [1, D]),
    ("l1bet_row", [1, D]), ("centT", [D, 8]), ("ga_pc", [128, 64]),
    ("be_pc", [128, 64]), ("b1_pc", [128, 256]), ("b2_rows", [8, D]),
    ("ident_f32", [128, 128]), ("cos_ext", [128, S]), ("sin_ext", [128, S]),
    ("e16", [128, 128]), ("decay_mt", [128, 128]), ("wdec", [128, 1]),
    ("lamq", [128, 1]), ("dup2", [64, 128]),
]
_IN_SPECS_W1W2 = [("w1", [8, D, 4096]), ("w2", [8, 4096, D])]

_L0_INTER = [("wdT", [D, D], "BF16"), ("xnT", [NSEQ, D, S], "BF16"),
             ("xdT", [NSEQ, D, S], "BF16"), ("qT", [NSEQ, D, S], "BF16"),
             ("kT", [NSEQ, D, S], "BF16"), ("v", [NSEQ, S, D], "BF16"),
             ("oT", [NSEQ, D, S], "BF16"), ("x1", [NSEQ, S, D], "F32")]
_L1_INTER = [("xn1T", [NSEQ, D, S], "BF16"), ("xhatT", [NSEQ, D, S], "BF16"),
             ("feats", [NSEQ, S, D], "F32"), ("mask", [NSEQ, S, 8], "F32"),
             ("asel", [NSEQ, S, 1], "F32"), ("qTl", [NSEQ, D, S], "BF16"),
             ("kTl", [NSEQ, D, S], "BF16"), ("ktm", [NSEQ, S, D], "BF16"),
             ("vl", [NSEQ, S, D], "BF16"), ("oTl", [NSEQ, D, S], "BF16"),
             ("s1", [NSEQ, S, D], "F32"), ("acc", [NSEQ, S, D], "F32")]


def _declare(nc, io, specs_in, inter, x1_kind=None):
    for nm, shape in specs_in:
        io[nm] = nc.dram_tensor(nm, shape, F32, kind="ExternalInput").ap()
    for nm, shape, dt in inter:
        k = "Internal"
        if x1_kind and nm == "x1":
            k = x1_kind
        io[nm] = nc.dram_tensor(nm, shape, F32 if dt == "F32" else BF16,
                                kind=k).ap()


def _preamble(ctx, blk, nc, io):
    ident_d = nc.dram_tensor("ident_bf", [128, 128], BF16,
                             kind="ExternalInput").ap()
    identb = ctx.outer_E(nc.sbuf_tensor("identb", [128, 128], BF16))

    @blk.sync
    def _(q):
        ctx.inc("sd", q.dma_start(out=identb[:], in_=ident_d), 16)
    ld = ctx.val("sd")

    @blk.vector
    def _(v):
        ctx.wait(v, "sd", ld)

    @blk.tensor
    def _(t):
        ctx.wait(t, "sd", ld)

    @blk.scalar
    def _(sE):
        ctx.wait(sE, "sd", ld)

    @blk.gpsimd
    def _(g):
        ctx.wait(g, "sd", ld)
    io["ident_bf"] = identb


def build_full():
    nc = bass.Bass("TRN2", target_bir_lowering=False)
    io = {}
    _declare(nc, io, _IN_SPECS + _IN_SPECS_W1W2, _L0_INTER + _L1_INTER)
    io["out"] = nc.dram_tensor("out", [NSEQ, S, D], F32,
                               kind="ExternalOutput").ap()
    with ExitStack() as stack:
        ctx = Ctx(nc, stack)
        blk = ctx.E(nc.Block())
        _preamble(ctx, blk, nc, io)
        for em in (emit_stage0w, emit_stage1, emit_stage2, emit_stage3,
                   emit_stage4, emit_stage5, emit_stage6a, emit_stage6b,
                   emit_stage6c, emit_stage7, emit_stage8):
            with ctx.stage():
                em(ctx, blk, io)
            ctx.barrier(blk)
    return nc


def build_l0():
    nc = bass.Bass("TRN2", target_bir_lowering=False)
    io = {}
    _declare(nc, io, _IN_SPECS, _L0_INTER, x1_kind="ExternalOutput")
    with ExitStack() as stack:
        ctx = Ctx(nc, stack)
        blk = ctx.E(nc.Block())
        _preamble(ctx, blk, nc, io)
        for em in (emit_stage0w, emit_stage1, emit_stage2, emit_stage3,
                   emit_stage4):
            with ctx.stage():
                em(ctx, blk, io)
            ctx.barrier(blk)
        @blk.sync
        def _(q):
            for nm in ("s4x10", "s4x11"):
                ctx.wait(q, nm, ctx.val(nm))
    return nc


def build_l1():
    nc = bass.Bass("TRN2", target_bir_lowering=False)
    io = {}
    specs = [s for s in _IN_SPECS if s[0] not in
             ("WqT", "WqTs", "WkT", "WkTs", "WvT", "bq_pc", "bqs_pc", "bk_pc",
              "bks_pc", "bv_row", "WoT", "bo_row", "ffn_w", "ffn_b_row",
              "cos_ext", "sin_ext", "e16", "x")]
    specs = [("x1", [NSEQ, S, D])] + specs
    _declare(nc, io, specs + _IN_SPECS_W1W2, _L1_INTER)
    io["out"] = nc.dram_tensor("out", [NSEQ, S, D], F32,
                               kind="ExternalOutput").ap()
    with ExitStack() as stack:
        ctx = Ctx(nc, stack)
        blk = ctx.E(nc.Block())
        _preamble(ctx, blk, nc, io)
        for em in (emit_stage5, emit_stage6a, emit_stage6b, emit_stage6c,
                   emit_stage7, emit_stage8):
            with ctx.stage():
                em(ctx, blk, io)
            ctx.barrier(blk)
    return nc


FUSED = True


def kernel(**inputs):
    f = _host_prep(inputs)
    x = np.ascontiguousarray(inputs["L0_attn_x"].astype(np.float32))
    ncores = 8
    if FUSED:
        nc = build_full()
        in_maps = []
        for ci in range(ncores):
            m = dict(f)
            m["x"] = np.ascontiguousarray(x[2 * ci:2 * ci + 2])
            in_maps.append(m)
        res = run_bass_kernel_spmd(nc, in_maps, core_ids=list(range(ncores)))
        out = np.concatenate([np.asarray(r["out"]) for r in res.results], axis=0)
        return out
    nc0 = build_l0()
    keys0 = {s[0] for s in _IN_SPECS} | {"ident_bf"}
    in_maps = []
    for ci in range(ncores):
        m = {k: v for k, v in f.items() if k in keys0}
        m["x"] = np.ascontiguousarray(x[2 * ci:2 * ci + 2])
        in_maps.append(m)
    res0 = run_bass_kernel_spmd(nc0, in_maps, core_ids=list(range(ncores)))
    nc1 = build_l1()
    keys1 = {"L1Wq", "L1Wk", "L1Wv", "L1Wo", "l1bq_pc", "l1bk_pc", "l1bv_row",
             "l1bk_row", "l1bo_row", "l1gam_row", "l1bet_row", "centT",
             "ga_pc", "be_pc", "b1_pc", "b2_rows", "w1", "w2", "ident_f32",
             "decay_mt", "wdec", "lamq", "dup2", "ident_bf"}
    in_maps = []
    for ci in range(ncores):
        m = {k: v for k, v in f.items() if k in keys1}
        m["x1"] = np.asarray(res0.results[ci]["x1"])
        in_maps.append(m)
    res1 = run_bass_kernel_spmd(nc1, in_maps, core_ids=list(range(ncores)))
    return np.concatenate([np.asarray(r["out"]) for r in res1.results], axis=0)


# version 09
# speedup vs baseline: 1.3921x; 1.3921x over previous
import math
from contextlib import ExitStack, contextmanager
import numpy as np
import ml_dtypes
import concourse.bass as bass
import concourse.mybir as mybir
from concourse.bass import AP, IndirectOffsetOnAxis
from concourse.bass_utils import run_bass_kernel_spmd

F32 = mybir.dt.float32
BF16 = mybir.dt.bfloat16
I32 = mybir.dt.int32
AF = mybir.ActivationFunctionType
ALU = mybir.AluOpType
AX = mybir.AxisListType

NSEQ = 2
S = 2048
D = 1024
NT = S // 128
NCH = D // 128


class Ctx:
    def __init__(self, nc, stack):
        self.nc = nc
        self.outer_E = stack.enter_context
        self.E = stack.enter_context
        self.counts = {}
        self.sems = {}
        for name in ("sd", "sgd", "sv", "st", "ss", "sg", "so"):
            self.sems[name] = self.E(nc.semaphore(name))
            self.counts[name] = 0

    def inc(self, name, instr, amt=1):
        instr.then_inc(self.sems[name], amt)
        self.counts[name] += amt
        return self.counts[name]

    def val(self, name):
        return self.counts[name]

    def wait(self, eng, name, val):
        if val > 0:
            eng.wait_ge(self.sems[name], val)

    def barrier(self, blk):
        snap = {k: v for k, v in self.counts.items() if v > 0}
        for eng_name in ("sync", "scalar", "vector", "tensor", "gpsimd"):
            sec = getattr(blk, eng_name)

            def _mk(engine):
                for k, v in snap.items():
                    engine.wait_ge(self.sems[k], v)
            sec(_mk)

    def make_sems(self, prefix, n):
        for k in range(n):
            name = f"{prefix}{k}"
            self.sems[name] = self.outer_E(self.nc.semaphore(name))
            self.counts[name] = 0
        return [f"{prefix}{k}" for k in range(n)]

    @contextmanager
    def stage(self):
        st = ExitStack()
        old = self.E
        self.E = st.enter_context
        try:
            yield
        finally:
            st.close()
            self.E = old


def ts(eng, out, in0, s1, op0, s2=None, op1=ALU.bypass, accum_out=None):
    return eng.tensor_scalar(out=out, in0=in0, scalar1=s1, scalar2=s2,
                             op0=op0, op1=op1, accum_out=accum_out)


def bcast32(tile_ap):
    return AP(tile_ap.tensor, tile_ap.offset,
              [list(tile_ap.ap[0]), list(tile_ap.ap[1]), [0, 32]])


def view32(ap2d):
    return ap2d.rearrange("p (b e) -> p b e", e=32)



def emit_stage1(ctx, blk, io):
    nc, E = ctx.nc, ctx.E
    x_in, xnT, xdT, ident_bf = io["x"], io["xnT"], io["xdT"], io["ident_bf"]

    NBUF = 4
    xa = [E(nc.sbuf_tensor(f"s1_xa{i}", [128, 1024], F32)) for i in range(NBUF)]
    t0 = E(nc.sbuf_tensor("s1_t0", [128, 1024], F32))
    t1 = E(nc.sbuf_tensor("s1_t1", [128, 1024], F32))
    t2 = E(nc.sbuf_tensor("s1_t2", [128, 1024], F32))
    t3 = E(nc.sbuf_tensor("s1_t3", [128, 1024], F32))
    t5 = E(nc.sbuf_tensor("s1_t5", [128, 1024], F32))
    xn = E(nc.sbuf_tensor("s1_xn", [128, 1024], F32))
    sqdump = E(nc.sbuf_tensor("s1_sqdump", [128, 1024], F32))
    xnb = [E(nc.sbuf_tensor(f"s1_xnb{i}", [128, 1024], BF16)) for i in range(2)]
    xdb = [E(nc.sbuf_tensor(f"s1_xdb{i}", [128, 1024], BF16)) for i in range(2)]
    sm = E(nc.sbuf_tensor("s1_sm", [128, 32], F32))
    sc = E(nc.sbuf_tensor("s1_sc", [128, 32], F32))
    rsc = E(nc.sbuf_tensor("s1_rsc", [128, 32], F32))
    stat = E(nc.sbuf_tensor("s1_stat", [128, 4], F32))
    rstd = E(nc.sbuf_tensor("s1_rstd", [128, 1], F32))
    epsb = E(nc.sbuf_tensor("s1_epsb", [128, 1], F32))
    asn = E(nc.sbuf_tensor("s1_asn", [128, NCH * S], BF16))
    asd = E(nc.sbuf_tensor("s1_asd", [128, NCH * S], BF16))
    ps_n = E(nc.psum_tensor("s1_psn", [128, 1024], BF16))
    ps_d = E(nc.psum_tensor("s1_psd", [128, 1024], BF16))

    tile_done = {}
    evac_done = {}
    trn_done = {}
    trd_done = {}
    sx = ctx.make_sems("s1x", NBUF)

    @blk.vector
    def _(v):
        v.memset(epsb[:], 1e-5)

    def emit_load(i):
        s, t = divmod(i, NT)

        @blk.sync
        def _(q):
            if i >= NBUF:
                ctx.wait(q, "sv", tile_done[i - NBUF])
            ctx.inc(sx[i % NBUF], q.dma_start(
                out=xa[i % NBUF][:],
                in_=x_in[s, t * 128:(t + 1) * 128, :]), 16)

    for i in range(NBUF):
        emit_load(i)

    for i in range(NSEQ * NT):
        s, t = divmod(i, NT)
        xt = xa[i % NBUF]

        @blk.vector
        def _(v):
            ctx.wait(v, sx[i % NBUF], ctx.val(sx[i % NBUF]))
            c = v.tensor_reduce(out=stat[:, 0:1], in_=xt[:], axis=AX.X, op=ALU.add)
            ctx.wait(v, "sv", ctx.inc("sv", c))
            c = ts(v, stat[:, 1:2], stat[:, 0:1], 1.0 / D, ALU.mult)
            ctx.wait(v, "sv", ctx.inc("sv", c))
            c = ts(v, t0[:], xt[:], stat[:, 1:2], ALU.subtract)
            io["_m_xc"] = ctx.inc("sv", c)

        @blk.scalar
        def _(sE):
            ctx.wait(sE, "sv", io["_m_xc"])
            c = sE.activation(sqdump[:], t0[:], AF.Square, bias=0.0, scale=1.0,
                              accum_out=stat[:, 2:3])
            io["_m_ssq"] = ctx.inc("ss", c)

        @blk.vector
        def _(v):
            ctx.wait(v, "ss", io["_m_ssq"])
            c = ts(v, stat[:, 3:4], stat[:, 2:3], 1.0 / D, ALU.mult)
            io["_m_var"] = ctx.inc("sv", c)

        @blk.scalar
        def _(sE):
            ctx.wait(sE, "sv", io["_m_var"])
            c = sE.activation(rstd[:], stat[:, 3:4], AF.Sqrt, bias=epsb[:], scale=1.0)
            io["_m_std"] = ctx.inc("ss", c)

        @blk.vector
        def _(v):
            ctx.wait(v, "ss", io["_m_std"])
            c = v.reciprocal(rstd[:], rstd[:])
            ctx.wait(v, "sv", ctx.inc("sv", c))
            ts(v, xn[:], t0[:], rstd[:], ALU.mult)
            if i >= 2:
                ctx.wait(v, "st", trn_done[i - 2])
            c = ts(v, xnb[i % 2][:], t0[:], rstd[:], ALU.mult)
            io["_m_xnb"] = ctx.inc("sv", c)

        @blk.tensor
        def _(tE):
            ctx.wait(tE, "sv", io["_m_xnb"])
            if i >= 1:
                ctx.wait(tE, "sv", evac_done.get(i - 1, 0))
            for p in range(NCH):
                ins = tE.transpose(ps_n[:, p * 128:(p + 1) * 128],
                                   xnb[i % 2][:, p * 128:(p + 1) * 128],
                                   ident_bf[:])
            trn_done[i] = ctx.inc("st", ins)

        @blk.vector
        def _(v):
            c = v.tensor_reduce(out=sm[:], in_=view32(xn[:]), axis=AX.X,
                                op=ALU.max, apply_absolute_value=True)
            ctx.wait(v, "sv", ctx.inc("sv", c))
            c = ts(v, sc[:], sm[:], 1.0 / 7.5, ALU.mult, 1e-30, ALU.max)
            ctx.wait(v, "sv", ctx.inc("sv", c))
            c = v.reciprocal(rsc[:], sc[:])
            ctx.wait(v, "sv", ctx.inc("sv", c))
            v.tensor_tensor(out=view32(t1[:]), in0=view32(xn[:]),
                            in1=bcast32(rsc[:]), op=ALU.mult)
            v.scalar_tensor_tensor(out=t2[:], in0=t1[:], scalar=-1.0,
                                   in1=t1[:], op0=ALU.mult, op1=ALU.max)
            ts(v, t3[:], t2[:], 2.0, ALU.is_ge, -4.0, ALU.mult)
            ts(v, t1[:], t2[:], 4.0, ALU.is_ge, -2.0, ALU.mult)
            v.scalar_tensor_tensor(out=t3[:], in0=t3[:], scalar=8.0,
                                   in1=t1[:], op0=ALU.add, op1=ALU.add)
            v.tensor_tensor(out=t1[:], in0=t2[:], in1=t3[:], op=ALU.mult)
            ts(v, t2[:], t1[:], 15.5, ALU.is_ge, 16.0, ALU.mult)
            v.tensor_tensor(out=t5[:], in0=t1[:], in1=t2[:], op=ALU.subtract)
            ts(v, t2[:], t5[:], 7.5, ALU.is_ge, 8.0, ALU.mult)
            v.tensor_tensor(out=t5[:], in0=t5[:], in1=t2[:], op=ALU.subtract)
            ts(v, t2[:], t5[:], 3.5, ALU.is_ge, 4.0, ALU.mult)
            v.tensor_tensor(out=t5[:], in0=t5[:], in1=t2[:], op=ALU.subtract)
            ts(v, t2[:], t5[:], 1.5, ALU.is_ge, 2.0, ALU.mult)
            v.tensor_tensor(out=t5[:], in0=t5[:], in1=t2[:], op=ALU.subtract)
            ts(v, t2[:], t5[:], 0.5, ALU.is_ge, 1.0, ALU.mult)
            v.tensor_tensor(out=t5[:], in0=t5[:], in1=t2[:], op=ALU.subtract)
            v.tensor_tensor(out=t1[:], in0=t1[:], in1=t5[:], op=ALU.subtract)
            v.reciprocal(t2[:], t3[:])
            v.tensor_tensor(out=t1[:], in0=t1[:], in1=t2[:], op=ALU.mult)
            ts(v, t2[:], xn[:], 0.0, ALU.is_lt, -2.0, ALU.mult)
            ts(v, t2[:], t2[:], 1.0, ALU.add)
            v.tensor_tensor(out=t1[:], in0=t1[:], in1=t2[:], op=ALU.mult)
            if i >= 2:
                ctx.wait(v, "st", trd_done[i - 2])
            c = v.tensor_tensor(out=view32(xdb[i % 2][:]), in0=view32(t1[:]),
                                in1=bcast32(sc[:]), op=ALU.mult)
            io["_m_xdb"] = ctx.inc("sv", c)

        @blk.tensor
        def _(tE):
            ctx.wait(tE, "sv", io["_m_xdb"])
            for p in range(NCH):
                ins = tE.transpose(ps_d[:, p * 128:(p + 1) * 128],
                                   xdb[i % 2][:, p * 128:(p + 1) * 128],
                                   ident_bf[:])
            trd_done[i] = ctx.inc("st", ins)

        @blk.vector
        def _(v):
            dstn = asn[:].rearrange("p (c q) -> p c q", q=S)[:, :, t * 128:(t + 1) * 128]
            dstd = asd[:].rearrange("p (c q) -> p c q", q=S)[:, :, t * 128:(t + 1) * 128]
            srcn = ps_n[:].rearrange("p (c q) -> p c q", q=128)
            srcd = ps_d[:].rearrange("p (c q) -> p c q", q=128)
            ctx.wait(v, "st", trn_done[i])
            v.tensor_copy(dstn, srcn)
            ctx.wait(v, "st", trd_done[i])
            c = v.tensor_copy(dstd, srcd)
            evac_done[i] = ctx.inc("sv", c)
            tile_done[i] = evac_done[i]

        if i + NBUF < NSEQ * NT:
            emit_load(i + NBUF)

        if t == NT - 1:
            @blk.gpsimd
            def _(g):
                ctx.wait(g, "sv", tile_done[i])
                for p in range(NCH):
                    ctx.inc("sgd", g.dma_start(
                        out=xnT[s, p * 128:(p + 1) * 128, :],
                        in_=asn[:, p * S:(p + 1) * S]), 16)
                    ctx.inc("sgd", g.dma_start(
                        out=xdT[s, p * 128:(p + 1) * 128, :],
                        in_=asd[:, p * S:(p + 1) * S]), 16)
            io["_m_s1_store"] = ctx.val("sgd")
            if s == 0:
                @blk.vector
                def _(v):
                    ctx.wait(v, "sgd", io["_m_s1_store"])

    io["s1_done_sgd"] = ctx.val("sgd")



def host_constants():
    c = {}
    c["ident_bf"] = np.eye(128, dtype=ml_dtypes.bfloat16)
    c["ident_f32"] = np.eye(128, dtype=np.float32)
    c["ones_row"] = np.ones((1, 128), dtype=np.float32)
    d = 64
    inv_freq = (1.0 / 10000.0 ** (np.arange(0, d, 2, dtype=np.float32) / d))
    pos = np.arange(S, dtype=np.float32)
    th = pos[None, :] * inv_freq[:, None]
    cos = np.cos(th).astype(np.float32)
    sin = np.sin(th).astype(np.float32)
    cos64 = np.repeat(cos, 2, axis=0)
    sin64 = np.empty((64, S), dtype=np.float32)
    sin64[0::2] = -sin
    sin64[1::2] = sin
    c["cos_ext"] = np.concatenate([cos64, cos64], axis=0)
    c["sin_ext"] = np.concatenate([sin64, sin64], axis=0)
    return c



def emit_stage2(ctx, blk, io):
    nc, E = ctx.nc, ctx.E
    xnT_d = io["xnT"]
    qT_d, kT_d, v_d = io["qT"], io["kT"], io["v"]
    wq_d, wqs_d = io["WqT"], io["WqTs"]
    wk_d, wks_d = io["WkT"], io["WkTs"]
    wv_d = io["WvT"]
    bq_d, bqs_d = io["bq_pc"], io["bqs_pc"]
    bk_d, bks_d = io["bk_pc"], io["bks_pc"]
    bv_d = io["bv_row"]
    cos_d, sin_d = io["cos_ext"], io["sin_ext"]

    xnT_sb = E(nc.sbuf_tensor("s2_xnT", [128, NCH * S], BF16))
    wsb = {}
    for nm in ("wq", "wqs", "wk", "wks", "wv"):
        wsb[nm] = E(nc.sbuf_tensor(f"s2_{nm}", [128, NCH * D], BF16))
    cos_sb = E(nc.sbuf_tensor("s2_cos", [128, S], F32))
    sin_sb = E(nc.sbuf_tensor("s2_sin", [128, S], F32))
    bqs_sb = E(nc.sbuf_tensor("s2_bq", [128, NCH], F32))
    bqss_sb = E(nc.sbuf_tensor("s2_bqs", [128, NCH], F32))
    bks_sb = E(nc.sbuf_tensor("s2_bk", [128, NCH], F32))
    bkss_sb = E(nc.sbuf_tensor("s2_bks", [128, NCH], F32))
    bv_sb = E(nc.sbuf_tensor("s2_bv", [1, D], BF16))
    ones_bf = E(nc.sbuf_tensor("s2_ones1", [1, 128], BF16))
    tq = E(nc.sbuf_tensor("s2_tq", [128, 512], F32))
    tq2 = E(nc.sbuf_tensor("s2_tq2", [128, 512], F32))
    cbuf = {"q": [E(nc.sbuf_tensor(f"s2_cq{j}", [128, S], BF16)) for j in range(2)],
            "k": [E(nc.sbuf_tensor(f"s2_ck{j}", [128, S], BF16)) for j in range(2)]}
    v_sb = E(nc.sbuf_tensor("s2_v", [128, NT * D], BF16))
    ps_a = E(nc.psum_tensor("s2_psa", [128, 512], F32))
    ps_b = E(nc.psum_tensor("s2_psb", [128, 512], F32))
    ps_v = [E(nc.psum_tensor(f"s2_psv{j}", [128, 512], F32)) for j in range(2)]

    sxl = ctx.make_sems("s2xl", 1)[0]
    scb = ctx.make_sems("s2cb", 4)

    @blk.gpsimd
    def _(g):
        for nm, d_ in (("wq", wq_d), ("wqs", wqs_d), ("wk", wk_d),
                       ("wks", wks_d), ("wv", wv_d)):
            src3 = d_.rearrange("(c p) o -> p c o", p=128)
            dst3 = wsb[nm][:].rearrange("p (c o) -> p c o", o=D)
            ctx.inc("sgd", g.dma_start(out=dst3, in_=src3), 16)
        ctx.inc("sgd", g.dma_start(out=bv_sb[:], in_=bv_d), 16)
    w_loaded = ctx.val("sgd")

    @blk.sync
    def _(q):
        ctx.inc("sd", q.dma_start(out=cos_sb[:], in_=cos_d), 16)
        ctx.inc("sd", q.dma_start(out=sin_sb[:], in_=sin_d), 16)
        for bd, bs_ in ((bq_d, bqs_sb), (bqs_d, bqss_sb), (bk_d, bks_sb),
                        (bks_d, bkss_sb)):
            ctx.inc("sd", q.dma_start(out=bs_[:], in_=bd), 16)
    misc_loaded = ctx.val("sd")

    @blk.vector
    def _(v):
        c0 = v.memset(ones_bf[:], 1.0)
        io["_s2_ones"] = ctx.inc("sv", c0)

    for s in range(NSEQ):
        @blk.sync
        def _(q):
            if s > 0:
                ctx.wait(q, "st", io["_s2_vmm"])
            src3 = xnT_d[s].rearrange("(c p) t -> p c t", p=128)
            dst3 = xnT_sb[:].rearrange("p (c t) -> p c t", t=S)
            ctx.inc(sxl, q.dma_start(out=dst3, in_=src3), 16)
        xn_ready = ctx.val(sxl)

        for kind, wa, wb_, ba, bb_, dram in (
                ("q", "wq", "wqs", bqs_sb, bqss_sb, qT_d),
                ("k", "wk", "wks", bks_sb, bkss_sb, kT_d)):
            for c in range(NCH):
                slot = c % 2
                for g_ in range(4):
                    @blk.tensor
                    def _(t):
                        if s == 0 and kind == "q" and c == 0 and g_ == 0:
                            ctx.wait(t, "sgd", w_loaded)
                            ctx.wait(t, "sd", misc_loaded)
                        ctx.wait(t, sxl, xn_ready)
                        if not (c == 0 and g_ == 0):
                            ctx.wait(t, "sv", io["_s2_evac"])
                        elif s > 0 or kind == "k":
                            ctx.wait(t, "sv", io.get("_s2_evac", 0))
                        for kc in range(NCH):
                            t.matmul(ps_a[:],
                                     wsb[wa][:, kc * D + c * 128:kc * D + (c + 1) * 128],
                                     xnT_sb[:, kc * S + g_ * 512:kc * S + g_ * 512 + 512],
                                     start=(kc == 0), stop=(kc == NCH - 1))
                        for kc in range(NCH):
                            ins = t.matmul(ps_b[:],
                                     wsb[wb_][:, kc * D + c * 128:kc * D + (c + 1) * 128],
                                     xnT_sb[:, kc * S + g_ * 512:kc * S + g_ * 512 + 512],
                                     start=(kc == 0), stop=(kc == NCH - 1))
                        io["_s2_mm"] = ctx.inc("st", ins)
                        io["_s2_lastmm"] = io["_s2_mm"]

                    @blk.vector
                    def _(v):
                        ctx.wait(v, "st", io["_s2_mm"])
                        if g_ == 0:
                            ctx.wait(v, scb[(0 if kind == "q" else 2) + slot],
                                     ctx.val(scb[(0 if kind == "q" else 2) + slot]))
                        cs = cos_sb[:, g_ * 512:(g_ + 1) * 512]
                        sn = sin_sb[:, g_ * 512:(g_ + 1) * 512]
                        v.scalar_tensor_tensor(out=tq[:], in0=ps_a[:],
                                               scalar=ba[:, c:c + 1], in1=cs,
                                               op0=ALU.add, op1=ALU.mult)
                        v.scalar_tensor_tensor(out=tq2[:], in0=ps_b[:],
                                               scalar=bb_[:, c:c + 1], in1=sn,
                                               op0=ALU.add, op1=ALU.mult)
                        c2 = v.tensor_tensor(
                            out=cbuf[kind][slot][:, g_ * 512:(g_ + 1) * 512],
                            in0=tq[:], in1=tq2[:], op=ALU.add)
                        io["_s2_evac"] = ctx.inc("sv", c2)

                @blk.gpsimd
                def _(g):
                    ctx.wait(g, "sv", io["_s2_evac"])
                    ctx.inc(scb[(0 if kind == "q" else 2) + slot],
                            g.dma_start(out=dram[s, c * 128:(c + 1) * 128, :],
                                        in_=cbuf[kind][slot][:]), 16)

        for tt in range(NT):
            for hf in range(2):
                j = (tt * 2 + hf) % 2
                @blk.tensor
                def _(t):
                    if s == 0 and tt == 0 and hf == 0:
                        ctx.wait(t, "sv", io["_s2_ones"])
                    if tt * 2 + hf >= 2:
                        ctx.wait(t, "sv", io["_s2_vevac"][j])
                    for kc in range(NCH):
                        t.matmul(ps_v[j][:],
                                 xnT_sb[:, kc * S + tt * 128:kc * S + (tt + 1) * 128],
                                 wsb["wv"][:, kc * D + hf * 512:kc * D + (hf + 1) * 512],
                                 start=(kc == 0), stop=False)
                    ins = t.matmul(ps_v[j][:], ones_bf[:],
                                   bv_sb[:, hf * 512:(hf + 1) * 512],
                                   start=False, stop=True)
                    io["_s2_vmm"] = ctx.inc("st", ins)

                @blk.vector
                def _(vE):
                    ctx.wait(vE, "st", io["_s2_vmm"])
                    c2 = vE.tensor_copy(
                        v_sb[:, tt * D + hf * 512:tt * D + (hf + 1) * 512],
                        ps_v[j][:])
                    if "_s2_vevac" not in io or tt * 2 + hf < 2:
                        io.setdefault("_s2_vevac", {})
                    io["_s2_vevac"][j] = ctx.inc("sv", c2)

        @blk.gpsimd
        def _(g):
            ctx.wait(g, "sv", io["_s2_vevac"][0])
            ctx.wait(g, "sv", io["_s2_vevac"][1])
            dst3 = v_d[s].rearrange("(tt p) f -> p tt f", p=128)
            src3 = v_sb[:].rearrange("p (tt f) -> p tt f", f=D)
            ctx.inc("sgd", g.dma_start(out=dst3, in_=src3), 16)
        io["_s2_vstore"] = ctx.val("sgd")
        if s == 0:
            @blk.vector
            def _(vE):
                ctx.wait(vE, "sgd", io["_s2_vstore"])

    io["s2_done_sgd"] = ctx.val("sgd")



def emit_stage3(ctx, blk, io):
    nc, E = ctx.nc, ctx.E
    qT_d, kT_d, v_d, oT_d = io["qT"], io["kT"], io["v"], io["oT"]
    e16_d = io["e16"]
    identf_d = io["ident_f32"]

    qT_sb = E(nc.sbuf_tensor("s3_qT", [128, NCH * S], BF16))
    kT_sb = E(nc.sbuf_tensor("s3_kT", [128, NCH * S], BF16))
    v_sb = E(nc.sbuf_tensor("s3_v", [128, NT * D], BF16))
    v65 = [E(nc.sbuf_tensor(f"s3_v65_{j}", [128, NT * 65], BF16)) for j in range(2)]
    pT = [E(nc.sbuf_tensor(f"s3_pT{j}", [128, NT * 512], BF16)) for j in range(2)]
    sq_scr = E(nc.sbuf_tensor("s3_sq", [128, NCH * 512], F32))
    e16_sb = E(nc.sbuf_tensor("s3_e16", [128, NCH * 16], F32))
    identf = E(nc.sbuf_tensor("s3_idf", [128, 128], F32))
    onesf = E(nc.sbuf_tensor("s3_onesf", [1, 128], F32))
    bnd = E(nc.sbuf_tensor("s3_bnd", [16, 8], F32))
    nrow = E(nc.sbuf_tensor("s3_nrow", [1, 16], F32))
    bias16 = E(nc.sbuf_tensor("s3_bias16", [128, 16], F32))
    l_sb = E(nc.sbuf_tensor("s3_l", [1, 512], F32))
    linv = E(nc.sbuf_tensor("s3_linv", [1, 512], F32))
    lexp = E(nc.sbuf_tensor("s3_lexp", [64, 512], F32))
    ot_stage = [E(nc.sbuf_tensor(f"s3_ot{j}", [64, 512], BF16)) for j in range(2)]

    ps_s = [E(nc.psum_tensor(f"s3_pss{j}", [128, 512], F32)) for j in range(4)]
    ps_o = [E(nc.psum_tensor(f"s3_pso{j}", [128, 512], F32)) for j in range(2)]
    ps_l = E(nc.psum_tensor("s3_psl", [64, 512], F32))
    ps_n1 = ps_o[1][0:16, 0:512]
    ps_r = ps_o[0][0:128, 0:16]

    sq_ld = ctx.make_sems("s3ld", 3)
    s_ot = ctx.make_sems("s3ot", 2)

    @blk.sync
    def _(q):
        ctx.inc("sd", q.dma_start(out=e16_sb[:], in_=e16_d), 16)
        ctx.inc("sd", q.dma_start(out=identf[:], in_=identf_d), 16)
    misc_ld = ctx.val("sd")

    @blk.vector
    def _(v):
        c = v.memset(onesf[:], 1.0)
        io["_s3_ones"] = ctx.inc("sv", c)

    exp_done = {}
    av_done = {}
    tt_done = {}

    for s in range(NSEQ):
        @blk.sync
        def _(q):
            if s > 0:
                ctx.wait(q, "sv", io["_s3_lastTT"])
            src_q = qT_d[s].rearrange("(c p) t -> p c t", p=128)
            ctx.inc(sq_ld[0], q.dma_start(
                out=qT_sb[:].rearrange("p (c t) -> p c t", t=S), in_=src_q), 16)
            src_k = kT_d[s].rearrange("(c p) t -> p c t", p=128)
            ctx.inc(sq_ld[1], q.dma_start(
                out=kT_sb[:].rearrange("p (c t) -> p c t", t=S), in_=src_k), 16)
            src_v = v_d[s].rearrange("(tt p) f -> p tt f", p=128)
            ctx.inc(sq_ld[2], q.dma_start(
                out=v_sb[:].rearrange("p (tt f) -> p tt f", f=D), in_=src_v), 16)
        qld, kld, vld = ctx.val(sq_ld[0]), ctx.val(sq_ld[1]), ctx.val(sq_ld[2])

        for pi, (src_sb, ldsem, ldval, col) in enumerate(
                ((qT_sb, sq_ld[0], qld, 2), (kT_sb, sq_ld[1], kld, 3))):
            @blk.vector
            def _(v):
                c2 = v.memset(bnd[:, 0:1], 0.0)
                io["_s3_bacc"] = ctx.inc("sv", c2)

            for g_ in range(4):
                for c in range(NCH):
                    @blk.scalar
                    def _(sE):
                        ctx.wait(sE, ldsem, ldval)
                        if s > 0 and pi == 0 and g_ == 0 and c == 0:
                            ctx.wait(sE, "sv", io["_s3_lastTT"])
                        c2 = sE.activation(
                            sq_scr[:, c * 512:(c + 1) * 512],
                            src_sb[:, c * S + g_ * 512:c * S + (g_ + 1) * 512],
                            AF.Square, bias=0.0, scale=1.0)
                        io["_s3_sq"] = ctx.inc("ss", c2)

                    @blk.tensor
                    def _(t):
                        ctx.wait(t, "ss", io["_s3_sq"])
                        ctx.wait(t, "sd", misc_ld)
                        if c == 0 and "_s3_red" in io:
                            ctx.wait(t, "sv", io["_s3_red"])
                        c2 = t.matmul(ps_n1, e16_sb[:, c * 16:(c + 1) * 16],
                                      sq_scr[:, c * 512:(c + 1) * 512],
                                      start=(c == 0), stop=(c == NCH - 1))
                        if c == NCH - 1:
                            io["_s3_nmm"] = ctx.inc("st", c2)

                @blk.vector
                def _(v):
                    ctx.wait(v, "st", io["_s3_nmm"])
                    if g_ == 0:
                        ctx.wait(v, "sv", io["_s3_bacc"])
                    c2 = v.tensor_reduce(out=bnd[:, 1:2], in_=ps_n1, axis=AX.X,
                                         op=ALU.max)
                    ctx.wait(v, "sv", ctx.inc("sv", c2))
                    c2 = v.tensor_tensor(out=bnd[:, 0:1], in0=bnd[:, 0:1],
                                         in1=bnd[:, 1:2], op=ALU.max)
                    ctx.wait(v, "sv", ctx.inc("sv", c2))
                    io["_s3_red"] = ctx.val("sv")

            @blk.scalar
            def _(sE):
                ctx.wait(sE, "sv", io["_s3_red"])
                c2 = sE.activation(bnd[:, col:col + 1], bnd[:, 0:1], AF.Sqrt,
                                   bias=0.0, scale=1.0)
                io["_s3_sqrt"] = ctx.inc("ss", c2)

        @blk.vector
        def _(v):
            ctx.wait(v, "ss", io["_s3_sqrt"])
            c2 = v.tensor_tensor(out=bnd[:, 4:5], in0=bnd[:, 2:3],
                                 in1=bnd[:, 3:4], op=ALU.mult)
            ctx.wait(v, "sv", ctx.inc("sv", c2))
            c2 = ts(v, bnd[:, 4:5], bnd[:, 4:5], -0.125, ALU.mult)
            io["_s3_negc"] = ctx.inc("sv", c2)

        @blk.tensor
        def _(t):
            ctx.wait(t, "sv", io["_s3_negc"])
            c2 = t.matmul(ps_r[0:1, 0:16], bnd[0:16, 4:5], identf[0:16, 0:16],
                          start=True, stop=True)
            io["_s3_rowmm"] = ctx.inc("st", c2)

        @blk.vector
        def _(v):
            ctx.wait(v, "st", io["_s3_rowmm"])
            c2 = v.tensor_copy(nrow[:], ps_r[0:1, 0:16])
            io["_s3_nrowc"] = ctx.inc("sv", c2)

        @blk.tensor
        def _(t):
            ctx.wait(t, "sv", io["_s3_nrowc"])
            c2 = t.matmul(ps_r, onesf[:], nrow[:], start=True, stop=True)
            io["_s3_bmm"] = ctx.inc("st", c2)

        @blk.vector
        def _(v):
            ctx.wait(v, "st", io["_s3_bmm"])
            ctx.wait(v, "sv", io["_s3_ones"])
            c2 = v.tensor_copy(bias16[:], ps_r)
            io["_s3_bias"] = ctx.inc("sv", c2)

        for h in range(16):
            hb = h % 2
            ch = h // 2
            po = (h % 2) * 64
            @blk.vector
            def _(v):
                ctx.wait(v, sq_ld[2], vld)
                if (s, h - 2, 3) in av_done:
                    ctx.wait(v, "st", av_done[(s, h - 2, 3)])
                src = v_sb[:].rearrange("p (tt f) -> p tt f", f=D)[
                    :, :, h * 64:(h + 1) * 64]
                dst = v65[hb][:].rearrange("p (tt f) -> p tt f", f=65)[:, :, 0:64]
                v.tensor_copy(dst, src)
                onescol = v65[hb][:].rearrange("p (tt f) -> p tt f", f=65)[
                    :, :, 64:65]
                c2 = v.memset(onescol, 1.0)
                io["_s3_v65"] = ctx.inc("sv", c2)
            v65_ready = io["_s3_v65"]

            for g_ in range(4):
                u = (s, h, g_)
                up = (h * 4 + g_) % 2
                for g4 in range(4):
                    @blk.tensor
                    def _(t):
                        if g4 == 0 and g_ == 0 and h == 0:
                            ctx.wait(t, sq_ld[0], qld)
                            ctx.wait(t, sq_ld[1], kld)
                            ctx.wait(t, "sv", io["_s3_bias"])
                        for kc in range(4 * g4, 4 * g4 + 4):
                            bank = kc % 4
                            key = ("expb", bank)
                            if key in io:
                                ctx.wait(t, "ss", io[key])
                            c2 = t.matmul(
                                ps_s[bank][:],
                                kT_sb[po:po + 64, ch * S + kc * 128:ch * S + (kc + 1) * 128],
                                qT_sb[po:po + 64, ch * S + g_ * 512:ch * S + (g_ + 1) * 512],
                                start=True, stop=True)
                            io[("mm", kc)] = ctx.inc("st", c2)

                    @blk.scalar
                    def _(sE):
                        if g4 == 0 and g_ == 0 and h == 0:
                            ctx.wait(sE, "sv", io["_s3_bias"])
                        if g4 == 0:
                            key = ("avd", up)
                            if key in io:
                                ctx.wait(sE, "st", io[key])
                        for kc in range(4 * g4, 4 * g4 + 4):
                            ctx.wait(sE, "st", io[("mm", kc)])
                            c2 = sE.activation(pT[up][:, kc * 512:(kc + 1) * 512],
                                               ps_s[kc % 4][:], AF.Exp,
                                               bias=bias16[:, h:h + 1], scale=0.125)
                            io[("expb", kc % 4)] = ctx.inc("ss", c2)
                            io[("exp", kc)] = io[("expb", kc % 4)]

                ob = (h * 4 + g_) % 2
                @blk.tensor
                def _(t):
                    ctx.wait(t, "sv", v65_ready)
                    key = ("ttd", ob)
                    if key in io:
                        ctx.wait(t, "sv", io[key])
                    for kc in range(NT):
                        ctx.wait(t, "ss", io[("exp", kc)])
                        c2 = t.matmul(ps_o[ob][0:65, :],
                                      v65[hb][:, kc * 65:(kc + 1) * 65],
                                      pT[up][:, kc * 512:(kc + 1) * 512],
                                      start=(kc == 0), stop=(kc == NT - 1))
                    io[("avd", up)] = ctx.inc("st", c2)
                    av_done[(s, h, g_)] = io[("avd", up)]

                @blk.vector
                def _(v):
                    ctx.wait(v, "st", io[("avd", up)])
                    c2 = v.tensor_copy(l_sb[:], ps_o[ob][64:65, :])
                    ctx.wait(v, "sv", ctx.inc("sv", c2))
                    c2 = v.reciprocal(linv[:], l_sb[:])
                    io["_s3_linv"] = ctx.inc("sv", c2)

                @blk.tensor
                def _(t):
                    ctx.wait(t, "sv", io["_s3_linv"])
                    c2 = t.matmul(ps_l[:], onesf[:, 0:64], linv[:],
                                  start=True, stop=True)
                    io["_s3_lexp"] = ctx.inc("st", c2)

                @blk.vector
                def _(v):
                    ctx.wait(v, "st", io["_s3_lexp"])
                    v.tensor_copy(lexp[:], ps_l[:])
                    jj = (h * 4 + g_) % 2
                    ctx.wait(v, s_ot[jj], ctx.val(s_ot[jj]))
                    c2 = v.tensor_tensor(out=ot_stage[jj][:], in0=ps_o[ob][0:64, :],
                                         in1=lexp[:], op=ALU.mult)
                    io[("ttd", ob)] = ctx.inc("sv", c2)
                    io["_s3_lastTT"] = io[("ttd", ob)]
                    tt_done[u] = io[("ttd", ob)]

                @blk.gpsimd
                def _(g):
                    jj = (h * 4 + g_) % 2
                    ctx.wait(g, "sv", tt_done[u])
                    ctx.inc(s_ot[jj], g.dma_start(
                        out=oT_d[s, h * 64:(h + 1) * 64, g_ * 512:(g_ + 1) * 512],
                        in_=ot_stage[jj][:]), 16)

    io["s3_done_sgd"] = ctx.val("sgd")


def emit_quant_chain(ctx, v, src_ap, dst_ap, q):
    t1, t2, t3, t5 = q["t1"], q["t2"], q["t3"], q["t5"]
    sm, sc, rsc = q["sm"], q["sc"], q["rsc"]
    c = v.tensor_reduce(out=sm[:], in_=src_ap.rearrange("p (b e) -> p b e", e=32),
                        axis=AX.X, op=ALU.max, apply_absolute_value=True)
    ctx.wait(v, "sv", ctx.inc("sv", c))
    c = ts(v, sc[:], sm[:], 1.0 / 7.5, ALU.mult, 1e-30, ALU.max)
    ctx.wait(v, "sv", ctx.inc("sv", c))
    c = v.reciprocal(rsc[:], sc[:])
    ctx.wait(v, "sv", ctx.inc("sv", c))
    v.tensor_tensor(out=view32(t1[:]), in0=src_ap.rearrange("p (b e) -> p b e", e=32),
                    in1=bcast32(rsc[:]), op=ALU.mult)
    v.scalar_tensor_tensor(out=t2[:], in0=t1[:], scalar=-1.0,
                           in1=t1[:], op0=ALU.mult, op1=ALU.max)
    ts(v, t3[:], t2[:], 2.0, ALU.is_ge, -4.0, ALU.mult)
    ts(v, t1[:], t2[:], 4.0, ALU.is_ge, -2.0, ALU.mult)
    v.scalar_tensor_tensor(out=t3[:], in0=t3[:], scalar=8.0,
                           in1=t1[:], op0=ALU.add, op1=ALU.add)
    v.tensor_tensor(out=t1[:], in0=t2[:], in1=t3[:], op=ALU.mult)
    ts(v, t2[:], t1[:], 15.5, ALU.is_ge, 16.0, ALU.mult)
    v.tensor_tensor(out=t5[:], in0=t1[:], in1=t2[:], op=ALU.subtract)
    ts(v, t2[:], t5[:], 7.5, ALU.is_ge, 8.0, ALU.mult)
    v.tensor_tensor(out=t5[:], in0=t5[:], in1=t2[:], op=ALU.subtract)
    ts(v, t2[:], t5[:], 3.5, ALU.is_ge, 4.0, ALU.mult)
    v.tensor_tensor(out=t5[:], in0=t5[:], in1=t2[:], op=ALU.subtract)
    ts(v, t2[:], t5[:], 1.5, ALU.is_ge, 2.0, ALU.mult)
    v.tensor_tensor(out=t5[:], in0=t5[:], in1=t2[:], op=ALU.subtract)
    ts(v, t2[:], t5[:], 0.5, ALU.is_ge, 1.0, ALU.mult)
    v.tensor_tensor(out=t5[:], in0=t5[:], in1=t2[:], op=ALU.subtract)
    v.tensor_tensor(out=t1[:], in0=t1[:], in1=t5[:], op=ALU.subtract)
    v.reciprocal(t2[:], t3[:])
    v.tensor_tensor(out=t1[:], in0=t1[:], in1=t2[:], op=ALU.mult)
    ts(v, t2[:], src_ap, 0.0, ALU.is_lt, -2.0, ALU.mult)
    ts(v, t2[:], t2[:], 1.0, ALU.add)
    v.tensor_tensor(out=t1[:], in0=t1[:], in1=t2[:], op=ALU.mult)
    c = v.tensor_tensor(out=dst_ap.rearrange("p (b e) -> p b e", e=32),
                        in0=view32(t1[:]), in1=bcast32(sc[:]), op=ALU.mult)
    return c



def emit_stage0w(ctx, blk, io):
    nc, E = ctx.nc, ctx.E
    w_d, wdT_d, ident_bf = io["ffn_w"], io["wdT"], io["ident_bf"]
    import contextlib
    with contextlib.ExitStack() as st0:
        E0 = st0.enter_context
        wa = [E0(nc.sbuf_tensor(f"s0_wa{j}", [128, D], F32)) for j in range(2)]
        wq = E0(nc.sbuf_tensor("s0_wq", [128, D], BF16))
        q = {k: E0(nc.sbuf_tensor(f"s0_{k}", [128, D], F32))
             for k in ("t1", "t2", "t3", "t5")}
        q["sm"] = E0(nc.sbuf_tensor("s0_sm", [128, 32], F32))
        q["sc"] = E0(nc.sbuf_tensor("s0_sc", [128, 32], F32))
        q["rsc"] = E0(nc.sbuf_tensor("s0_rsc", [128, 32], F32))
        asm = E0(nc.sbuf_tensor("s0_asm", [128, NCH * D], BF16))
        ps = E0(nc.psum_tensor("s0_ps", [128, D], BF16))
        sw = ctx.make_sems("s0w", 2)

        for r in range(NCH):
            @blk.sync
            def _(qq):
                if r >= 2:
                    ctx.wait(qq, "sv", io[("s0_qd", r - 2)])
                ctx.inc(sw[r % 2], qq.dma_start(
                    out=wa[r % 2][:], in_=w_d[r * 128:(r + 1) * 128, :]), 16)

            @blk.vector
            def _(v):
                ctx.wait(v, sw[r % 2], ctx.val(sw[r % 2]))
                if r >= 1:
                    ctx.wait(v, "st", io[("s0_tr", r - 1)])
                c = emit_quant_chain(ctx, v, wa[r % 2][:], wq[:], q)
                io[("s0_qd", r)] = ctx.inc("sv", c)

            @blk.tensor
            def _(t):
                ctx.wait(t, "sv", io[("s0_qd", r)])
                if r >= 1:
                    ctx.wait(t, "sv", io[("s0_ev", r - 1)])
                for c_ in range(NCH):
                    ins = t.transpose(ps[:, c_ * 128:(c_ + 1) * 128],
                                      wq[:, c_ * 128:(c_ + 1) * 128], ident_bf[:])
                io[("s0_tr", r)] = ctx.inc("st", ins)

            @blk.vector
            def _(v):
                ctx.wait(v, "st", io[("s0_tr", r)])
                dst = asm[:].rearrange("p (c o) -> p c o", o=D)[:, :, r * 128:(r + 1) * 128]
                src = ps[:].rearrange("p (c o) -> p c o", o=128)
                c = v.tensor_copy(dst, src)
                io[("s0_ev", r)] = ctx.inc("sv", c)

        @blk.gpsimd
        def _(g):
            ctx.wait(g, "sv", io[("s0_ev", NCH - 1)])
            src3 = asm[:].rearrange("p (c o) -> p c o", o=D)
            dst3 = wdT_d.rearrange("(c p) o -> p c o", p=128)
            ctx.inc("sgd", g.dma_start(out=dst3, in_=src3), 16)
        io["s0_done_sgd"] = ctx.val("sgd")



def emit_stage4(ctx, blk, io):
    nc, E = ctx.nc, ctx.E
    xdT_d, oT_d, x_d, x1_d = io["xdT"], io["oT"], io["x"], io["x1"]
    wdT_d, woT_d = io["wdT"], io["WoT"]
    fb_row_d, bo_row_d = io["ffn_b_row"], io["bo_row"]

    xdT_sb = E(nc.sbuf_tensor("s4_xdT", [128, NCH * S], BF16))
    oT_sb = E(nc.sbuf_tensor("s4_oT", [128, NCH * S], BF16))
    wd_sb = E(nc.sbuf_tensor("s4_wd", [128, NCH * D], BF16))
    wo_sb = E(nc.sbuf_tensor("s4_wo", [128, NCH * D], BF16))
    fb_row = E(nc.sbuf_tensor("s4_fbr", [1, D], F32))
    bo_row = E(nc.sbuf_tensor("s4_bor", [1, D], F32))
    onesf = E(nc.sbuf_tensor("s4_ones", [1, 128], F32))
    fb_exp = E(nc.sbuf_tensor("s4_fbe", [128, D], F32))
    bo_exp = E(nc.sbuf_tensor("s4_boe", [128, D], F32))
    xa = [E(nc.sbuf_tensor(f"s4_xa{j}", [128, D], F32)) for j in range(2)]
    gt = E(nc.sbuf_tensor("s4_gt", [128, D], F32))
    st_t = E(nc.sbuf_tensor("s4_st", [128, D], F32))
    x1t = [E(nc.sbuf_tensor(f"s4_x1t{j}", [128, D], F32)) for j in range(2)]
    ps_f = [E(nc.psum_tensor(f"s4_psf{j}", [128, 512], F32)) for j in range(2)]
    ps_s = [E(nc.psum_tensor(f"s4_pss{j}", [128, 512], F32)) for j in range(2)]
    ps_b = E(nc.psum_tensor("s4_psb", [128, D], F32))

    sxl = ctx.make_sems("s4xl", 2)
    sxa = ctx.make_sems("s4xa", 2)
    sx1 = ctx.make_sems("s4x1", 2)

    @blk.gpsimd
    def _(g):
        for d_, sb in ((wdT_d, wd_sb), (woT_d, wo_sb)):
            src3 = d_.rearrange("(c p) o -> p c o", p=128)
            dst3 = sb[:].rearrange("p (c o) -> p c o", o=D)
            ctx.inc("sgd", g.dma_start(out=dst3, in_=src3), 16)
    w_ld = ctx.val("sgd")

    @blk.sync
    def _(q):
        ctx.inc("sd", q.dma_start(out=fb_row[:], in_=fb_row_d), 16)
        ctx.inc("sd", q.dma_start(out=bo_row[:], in_=bo_row_d), 16)
    b_ld = ctx.val("sd")

    @blk.vector
    def _(v):
        c = v.memset(onesf[:], 1.0)
        io["_s4_ones"] = ctx.inc("sv", c)

    @blk.tensor
    def _(t):
        ctx.wait(t, "sd", b_ld)
        ctx.wait(t, "sv", io["_s4_ones"])
        for half in range(2):
            c = t.matmul(ps_b[:, half * 512:(half + 1) * 512], onesf[:],
                         fb_row[:, half * 512:(half + 1) * 512],
                         start=True, stop=True)
        io["_s4_fbmm"] = ctx.inc("st", c)

    @blk.vector
    def _(v):
        ctx.wait(v, "st", io["_s4_fbmm"])
        c = v.tensor_copy(fb_exp[:], ps_b[:])
        io["_s4_fbe"] = ctx.inc("sv", c)

    @blk.tensor
    def _(t):
        ctx.wait(t, "sv", io["_s4_fbe"])
        for half in range(2):
            c = t.matmul(ps_b[:, half * 512:(half + 1) * 512], onesf[:],
                         bo_row[:, half * 512:(half + 1) * 512],
                         start=True, stop=True)
        io["_s4_bomm"] = ctx.inc("st", c)

    @blk.vector
    def _(v):
        ctx.wait(v, "st", io["_s4_bomm"])
        c = v.tensor_copy(bo_exp[:], ps_b[:])
        io["_s4_boe"] = ctx.inc("sv", c)

    for s in range(NSEQ):
        @blk.sync
        def _(q):
            if s > 0:
                ctx.wait(q, "st", io["_s4_lastmm"])
            ctx.inc(sxl[0], q.dma_start(
                out=xdT_sb[:].rearrange("p (c t) -> p c t", t=S),
                in_=xdT_d[s].rearrange("(c p) t -> p c t", p=128)), 16)
            ctx.inc(sxl[1], q.dma_start(
                out=oT_sb[:].rearrange("p (c t) -> p c t", t=S),
                in_=oT_d[s].rearrange("(c p) t -> p c t", p=128)), 16)
        xd_ld, oT_ld = ctx.val(sxl[0]), ctx.val(sxl[1])

        for tt in range(NT):
            j = tt % 2

            @blk.sync
            def _(q):
                if s * NT + tt >= 2:
                    ctx.wait(q, "sv", io[("s4_xc", j)])
                ctx.inc(sxa[j], q.dma_start(
                    out=xa[j][:], in_=x_d[s, tt * 128:(tt + 1) * 128, :]), 16)
            xa_ld = ctx.val(sxa[j])

            for half in range(2):
                @blk.tensor
                def _(t):
                    if s == 0 and tt == 0 and half == 0:
                        ctx.wait(t, "sgd", w_ld)
                    ctx.wait(t, sxl[0], xd_ld)
                    if ("s4_fev", j) in io:
                        ctx.wait(t, "sv", io[("s4_fev", j)])
                    for kc in range(NCH):
                        c = t.matmul(
                            ps_f[j][:],
                            xdT_sb[:, kc * S + tt * 128:kc * S + (tt + 1) * 128],
                            wd_sb[:, kc * D + half * 512:kc * D + (half + 1) * 512],
                            start=(kc == 0), stop=(kc == NCH - 1))
                    io[("s4_fmm", j, half)] = ctx.inc("st", c)

                @blk.vector
                def _(v):
                    ctx.wait(v, "st", io[("s4_fmm", j, half)])
                    if s == 0 and tt == 0 and half == 0:
                        ctx.wait(v, "sv", io["_s4_fbe"])
                    if half == 0 and "s4_prev_xc" in io:
                        ctx.wait(v, "sv", io["s4_prev_xc"])
                    c = v.tensor_tensor(out=gt[:, half * 512:(half + 1) * 512],
                                        in0=ps_f[j][:],
                                        in1=fb_exp[:, half * 512:(half + 1) * 512],
                                        op=ALU.add)
                    io[("s4_fev", j)] = ctx.inc("sv", c)

            @blk.vector
            def _(v):
                ctx.wait(v, "sv", io[("s4_fev", j)])
                io["_s4_gadd"] = ctx.val("sv")

            @blk.scalar
            def _(sE):
                ctx.wait(sE, "sv", io["_s4_gadd"])
                c = sE.activation(gt[:], gt[:], AF.Silu, bias=0.0, scale=1.0)
                io["_s4_silu"] = ctx.inc("ss", c)

            for half in range(2):
                @blk.tensor
                def _(t):
                    ctx.wait(t, sxl[1], oT_ld)
                    if ("s4_sev", j) in io:
                        ctx.wait(t, "sv", io[("s4_sev", j)])
                    for kc in range(NCH):
                        c = t.matmul(
                            ps_s[j][:],
                            oT_sb[:, kc * S + tt * 128:kc * S + (tt + 1) * 128],
                            wo_sb[:, kc * D + half * 512:kc * D + (half + 1) * 512],
                            start=(kc == 0), stop=(kc == NCH - 1))
                    io[("s4_smm", j, half)] = ctx.inc("st", c)
                    io["_s4_lastmm"] = ctx.val("st")

                @blk.vector
                def _(v):
                    ctx.wait(v, "st", io[("s4_smm", j, half)])
                    if s == 0 and tt == 0 and half == 0:
                        ctx.wait(v, "sv", io["_s4_boe"])
                    c = v.tensor_tensor(out=st_t[:, half * 512:(half + 1) * 512],
                                        in0=ps_s[j][:],
                                        in1=bo_exp[:, half * 512:(half + 1) * 512],
                                        op=ALU.add)
                    io[("s4_sev", j)] = ctx.inc("sv", c)

            @blk.vector
            def _(v):
                ctx.wait(v, "ss", io["_s4_silu"])
                v.tensor_tensor(out=st_t[:], in0=st_t[:], in1=gt[:], op=ALU.mult)
                ctx.wait(v, sxa[j], xa_ld)
                ctx.wait(v, sx1[j], ctx.val(sx1[j]))
                c = v.tensor_tensor(out=x1t[j][:], in0=st_t[:], in1=xa[j][:],
                                    op=ALU.add)
                io[("s4_xc", j)] = ctx.inc("sv", c)
                io["s4_prev_xc"] = io[("s4_xc", j)]

            @blk.gpsimd
            def _(g):
                ctx.wait(g, "sv", io[("s4_xc", j)])
                ctx.inc(sx1[j], g.dma_start(
                    out=x1_d[s, tt * 128:(tt + 1) * 128, :], in_=x1t[j][:]), 16)

    io["s4_done_sgd"] = ctx.val("sgd")



def emit_stage5(ctx, blk, io):
    nc, E = ctx.nc, ctx.E
    x1_d = io["x1"]
    xn1T_d, xhatT_d = io["xn1T"], io["xhatT"]
    feats_d, mask_d, asel_d = io["feats"], io["mask"], io["asel"]
    centT_d = io["centT"]
    ident_bf, identf_d = io["ident_bf"], io["ident_f32"]
    xhat_tm_d = io["xhat_tm"]
    dest_d = io["dest"]
    destf_d = io["destf"]
    ltri_d = io["ltri"]
    er512_d = io["er512"]

    NBUF = 4
    xa = [E(nc.sbuf_tensor(f"s5_xa{i}", [128, D], F32)) for i in range(NBUF)]
    xn1 = [E(nc.sbuf_tensor(f"s5_xn1_{j}", [128, D], F32)) for j in range(2)]
    t0 = E(nc.sbuf_tensor("s5_t0", [128, D], F32))
    sqdump = E(nc.sbuf_tensor("s5_sqd", [128, D], F32))
    xn1b = [E(nc.sbuf_tensor(f"s5_xn1b{j}", [128, D], BF16)) for j in range(2)]
    xhb = [E(nc.sbuf_tensor(f"s5_xhb{j}", [128, D], BF16)) for j in range(2)]
    xf32T = E(nc.sbuf_tensor("s5_xf32T", [128, D], F32))
    stat = E(nc.sbuf_tensor("s5_stat", [128, 4], F32))
    rstd = E(nc.sbuf_tensor("s5_rstd", [128, 1], F32))
    epsb = E(nc.sbuf_tensor("s5_epsb", [128, 1], F32))
    centT = E(nc.sbuf_tensor("s5_centT", [128, NCH * 8], F32))
    rmax = E(nc.sbuf_tensor("s5_rmax", [128, 1], F32))
    maskt = [E(nc.sbuf_tensor(f"s5_mask{j}", [128, 8], F32)) for j in range(2)]
    alphat = E(nc.sbuf_tensor("s5_alpha", [128, 8], F32))
    aselt = [E(nc.sbuf_tensor(f"s5_asel{j}", [128, 1], F32)) for j in range(2)]
    identf = E(nc.sbuf_tensor("s5_idf", [128, 128], F32))
    ltri = E(nc.sbuf_tensor("s5_ltri", [128, 128], F32))
    er512 = E(nc.sbuf_tensor("s5_er512", [128, 8], F32))
    onesfr = E(nc.sbuf_tensor("s5_onesfr", [1, 128], F32))
    onescl = E(nc.sbuf_tensor("s5_onescl", [128, 1], F32))
    runrow = E(nc.sbuf_tensor("s5_runrow", [1, 8], F32))
    dst0 = E(nc.sbuf_tensor("s5_dst0", [128, 1], F32))
    osel = E(nc.sbuf_tensor("s5_osel", [128, 1], F32))
    rq = E(nc.sbuf_tensor("s5_rq", [128, 8], F32))
    desti = [E(nc.sbuf_tensor(f"s5_desti{j}", [128, 1], mybir.dt.uint32))
             for j in range(2)]
    asn1 = E(nc.sbuf_tensor("s5_asn1", [128, NCH * S], BF16))
    asx = E(nc.sbuf_tensor("s5_asx", [128, NCH * S], BF16))
    ps_n = E(nc.psum_tensor("s5_psn", [128, D], BF16))
    ps_h = E(nc.psum_tensor("s5_psh", [128, D], BF16))
    ps_f = E(nc.psum_tensor("s5_psf", [128, D], F32))
    ps_aff = E(nc.psum_tensor("s5_aff", [128, 8], F32))
    ps_off = E(nc.psum_tensor("s5_off", [128, 8], F32))
    ps_cs = E(nc.psum_tensor("s5_cs", [1, 8], F32))

    sx = ctx.make_sems("s5x", NBUF)
    sfo = ctx.make_sems("s5fo", 2)
    smo = ctx.make_sems("s5mo", 2)
    sho = ctx.make_sems("s5ho", 2)
    sso = ctx.make_sems("s5so", 2)

    @blk.vector
    def _(v):
        v.memset(epsb[:], 1e-5)

    @blk.sync
    def _(q):
        ctx.inc("sd", q.dma_start(out=identf[:], in_=identf_d), 16)
        src3 = centT_d.rearrange("(c p) e -> p c e", p=128)
        dst3 = centT[:].rearrange("p (c e) -> p c e", e=8)
        ctx.inc("sd", q.dma_start(out=dst3, in_=src3), 16)
        ctx.inc("sd", q.dma_start(out=ltri[:], in_=ltri_d), 16)
        ctx.inc("sd", q.dma_start(out=er512[:], in_=er512_d), 16)
    misc_ld = ctx.val("sd")

    @blk.vector
    def _(v):
        v.memset(onesfr[:], 1.0)
        v.memset(onescl[:], 1.0)
        c = v.memset(runrow[:], 0.0)
        io["_s5_rr"] = ctx.inc("sv", c)

    tile_done = {}
    trn_done = {}
    trh_done = {}
    trf_done = {}
    evac_done = {}

    def emit_load(i):
        s, t = divmod(i, NT)

        @blk.sync
        def _(q):
            if i >= NBUF:
                ctx.wait(q, "sv", tile_done[i - NBUF])
            ctx.inc(sx[i % NBUF], q.dma_start(
                out=xa[i % NBUF][:], in_=x1_d[s, t * 128:(t + 1) * 128, :]), 16)

    for i in range(NBUF):
        emit_load(i)

    for i in range(NSEQ * NT):
        s, t = divmod(i, NT)
        xt = xa[i % NBUF]
        j = i % 2

        @blk.vector
        def _(v):
            ctx.wait(v, sx[i % NBUF], ctx.val(sx[i % NBUF]))
            c = v.tensor_reduce(out=stat[:, 0:1], in_=xt[:], axis=AX.X, op=ALU.add)
            ctx.wait(v, "sv", ctx.inc("sv", c))
            c = ts(v, stat[:, 1:2], stat[:, 0:1], 1.0 / D, ALU.mult)
            ctx.wait(v, "sv", ctx.inc("sv", c))
            c = ts(v, t0[:], xt[:], stat[:, 1:2], ALU.subtract)
            io["_m5_xc"] = ctx.inc("sv", c)

        @blk.scalar
        def _(sE):
            ctx.wait(sE, "sv", io["_m5_xc"])
            c = sE.activation(sqdump[:], t0[:], AF.Square, bias=0.0, scale=1.0,
                              accum_out=stat[:, 2:3])
            io["_m5_ssq"] = ctx.inc("ss", c)

        @blk.vector
        def _(v):
            ctx.wait(v, "ss", io["_m5_ssq"])
            c = ts(v, stat[:, 3:4], stat[:, 2:3], 1.0 / D, ALU.mult)
            io["_m5_var"] = ctx.inc("sv", c)

        @blk.scalar
        def _(sE):
            ctx.wait(sE, "sv", io["_m5_var"])
            c = sE.activation(rstd[:], stat[:, 3:4], AF.Sqrt, bias=epsb[:], scale=1.0)
            io["_m5_std"] = ctx.inc("ss", c)

        @blk.vector
        def _(v):
            ctx.wait(v, "ss", io["_m5_std"])
            c = v.reciprocal(rstd[:], rstd[:])
            ctx.wait(v, "sv", ctx.inc("sv", c))
            if i >= 2:
                ctx.wait(v, sfo[j], ctx.val(sfo[j]))
                ctx.wait(v, "st", trf_done[i - 2])
            ts(v, xn1[j][:], t0[:], rstd[:], ALU.mult)
            if i >= 2:
                ctx.wait(v, "st", trn_done[i - 2])
            c = ts(v, xn1b[j][:], t0[:], rstd[:], ALU.mult)
            io["_m5_xn1b"] = ctx.inc("sv", c)

        @blk.vector
        def _(v):
            c = v.tensor_reduce(out=stat[:, 0:1], in_=xn1[j][:], axis=AX.X,
                                op=ALU.add)
            ctx.wait(v, "sv", ctx.inc("sv", c))
            c = ts(v, stat[:, 1:2], stat[:, 0:1], 1.0 / D, ALU.mult)
            ctx.wait(v, "sv", ctx.inc("sv", c))
            c = ts(v, t0[:], xn1[j][:], stat[:, 1:2], ALU.subtract)
            io["_m5_xc2"] = ctx.inc("sv", c)

        @blk.scalar
        def _(sE):
            ctx.wait(sE, "sv", io["_m5_xc2"])
            c = sE.activation(sqdump[:], t0[:], AF.Square, bias=0.0, scale=1.0,
                              accum_out=stat[:, 2:3])
            io["_m5_ssq2"] = ctx.inc("ss", c)

        @blk.vector
        def _(v):
            ctx.wait(v, "ss", io["_m5_ssq2"])
            c = ts(v, stat[:, 3:4], stat[:, 2:3], 1.0 / D, ALU.mult)
            io["_m5_var2"] = ctx.inc("sv", c)

        @blk.scalar
        def _(sE):
            ctx.wait(sE, "sv", io["_m5_var2"])
            c = sE.activation(rstd[:], stat[:, 3:4], AF.Sqrt, bias=epsb[:], scale=1.0)
            io["_m5_std2"] = ctx.inc("ss", c)

        @blk.vector
        def _(v):
            ctx.wait(v, "ss", io["_m5_std2"])
            c = v.reciprocal(rstd[:], rstd[:])
            ctx.wait(v, "sv", ctx.inc("sv", c))
            if i >= 2:
                ctx.wait(v, "st", trh_done[i - 2])
                ctx.wait(v, sho[j], ctx.val(sho[j]))
            c = ts(v, xhb[j][:], t0[:], rstd[:], ALU.mult)
            io["_m5_xhb"] = ctx.inc("sv", c)

        @blk.tensor
        def _(tE):
            ctx.wait(tE, "sv", io["_m5_xn1b"])
            if i >= 1:
                ctx.wait(tE, "sv", evac_done.get(i - 1, 0))
            for p in range(NCH):
                ins = tE.transpose(ps_n[:, p * 128:(p + 1) * 128],
                                   xn1b[j][:, p * 128:(p + 1) * 128], ident_bf[:])
            trn_done[i] = ctx.inc("st", ins)
            ctx.wait(tE, "sv", io["_m5_xhb"])
            for p in range(NCH):
                ins = tE.transpose(ps_h[:, p * 128:(p + 1) * 128],
                                   xhb[j][:, p * 128:(p + 1) * 128], ident_bf[:])
            trh_done[i] = ctx.inc("st", ins)
            ctx.wait(tE, "sd", misc_ld)
            for p in range(NCH):
                ins = tE.transpose(ps_f[:, p * 128:(p + 1) * 128],
                                   xn1[j][:, p * 128:(p + 1) * 128], identf[:])
            trf_done[i] = ctx.inc("st", ins)

        @blk.vector
        def _(v):
            dstn = asn1[:].rearrange("p (c q) -> p c q", q=S)[:, :, t * 128:(t + 1) * 128]
            dsth = asx[:].rearrange("p (c q) -> p c q", q=S)[:, :, t * 128:(t + 1) * 128]
            ctx.wait(v, "st", trn_done[i])
            v.tensor_copy(dstn, ps_n[:].rearrange("p (c q) -> p c q", q=128))
            ctx.wait(v, "st", trh_done[i])
            v.tensor_copy(dsth, ps_h[:].rearrange("p (c q) -> p c q", q=128))
            ctx.wait(v, "st", trf_done[i])
            c = v.tensor_copy(xf32T[:], ps_f[:])
            evac_done[i] = ctx.inc("sv", c)

        @blk.tensor
        def _(tE):
            ctx.wait(tE, "sv", evac_done[i])
            if ("s5_affr",) in io:
                ctx.wait(tE, "ss", io[("s5_affr",)])
            for kc in range(NCH):
                c = tE.matmul(ps_aff[:], xf32T[:, kc * 128:(kc + 1) * 128],
                              centT[:, kc * 8:(kc + 1) * 8],
                              start=(kc == 0), stop=(kc == NCH - 1))
            io["_m5_aff"] = ctx.inc("st", c)

        @blk.vector
        def _(v):
            ctx.wait(v, "st", io["_m5_aff"])
            c = v.tensor_reduce(out=rmax[:], in_=ps_aff[:], axis=AX.X, op=ALU.max)
            ctx.wait(v, "sv", ctx.inc("sv", c))
            ctx.wait(v, smo[j], ctx.val(smo[j]))
            if "_s5_csmm" in io:
                ctx.wait(v, "st", io["_s5_csmm"])
            c = ts(v, maskt[j][:], ps_aff[:], rmax[:], ALU.is_equal)
            io["_m5_mask"] = ctx.inc("sv", c)

        @blk.scalar
        def _(sE):
            ctx.wait(sE, "sv", io["_m5_mask"])
            c = sE.activation(alphat[:], ps_aff[:], AF.Sigmoid, bias=0.0, scale=1.0)
            io[("s5_affr",)] = ctx.inc("ss", c)

        @blk.vector
        def _(v):
            ctx.wait(v, "ss", io[("s5_affr",)])
            c = v.tensor_tensor(out=alphat[:], in0=alphat[:], in1=maskt[j][:],
                                op=ALU.mult)
            ctx.wait(v, "sv", ctx.inc("sv", c))
            c = v.tensor_reduce(out=aselt[j][:], in_=alphat[:], axis=AX.X,
                                op=ALU.add)
            tile_done[i] = ctx.inc("sv", c)
            io["_m5_tile"] = tile_done[i]

        if t == 0:
            @blk.vector
            def _(v):
                if "_s5_offmm" in io:
                    ctx.wait(v, "st", io["_s5_offmm"])
                c = v.memset(runrow[:], 0.0)
                io["_s5_rrup"] = ctx.inc("sv", c)

        @blk.tensor
        def _(tE):
            ctx.wait(tE, "sv", io["_m5_mask"])
            ctx.wait(tE, "sv", io["_s5_rrup"])
            if "_s5_dest" in io:
                ctx.wait(tE, "sv", io["_s5_dest"])
            tE.matmul(ps_off[:], ltri[:], maskt[j][:], start=True, stop=False)
            c = tE.matmul(ps_off[:], onesfr[:], runrow[:], start=False, stop=True)
            io["_s5_offmm"] = ctx.inc("st", c)
            c = tE.matmul(ps_cs[:], onescl[:], maskt[j][:], start=True, stop=True)
            io["_s5_csmm"] = ctx.inc("st", c)

        @blk.vector
        def _(v):
            ctx.wait(v, "st", io["_s5_offmm"])
            c = v.tensor_tensor(out=rq[:], in0=ps_off[:], in1=er512[:], op=ALU.add)
            ctx.wait(v, "sv", ctx.inc("sv", c))
            c = v.tensor_tensor(out=rq[:], in0=rq[:], in1=maskt[j][:], op=ALU.mult)
            ctx.wait(v, "sv", ctx.inc("sv", c))
            c = v.tensor_reduce(out=dst0[:], in_=rq[:], axis=AX.X, op=ALU.add)
            ctx.wait(v, "sv", ctx.inc("sv", c))
            c = v.tensor_tensor(out=rq[:], in0=ps_off[:], in1=maskt[j][:], op=ALU.mult)
            ctx.wait(v, "sv", ctx.inc("sv", c))
            c = v.tensor_reduce(out=osel[:], in_=rq[:], axis=AX.X, op=ALU.add)
            ctx.wait(v, "sv", ctx.inc("sv", c))
            c = ts(v, osel[:], osel[:], 512.0, ALU.is_ge, 33554432.0, ALU.mult)
            ctx.wait(v, "sv", ctx.inc("sv", c))
            ctx.wait(v, sso[j], ctx.val(sso[j]))
            c = v.tensor_tensor(out=dst0[:], in0=dst0[:], in1=osel[:], op=ALU.add)
            io["_s5_dest"] = ctx.inc("sv", c)
            ctx.wait(v, "st", io["_s5_csmm"])
            c = v.tensor_tensor(out=runrow[:], in0=runrow[:], in1=ps_cs[:],
                                op=ALU.add)
            io["_s5_rrup"] = ctx.inc("sv", c)

        @blk.scalar
        def _(sE):
            ctx.wait(sE, "sv", io["_s5_dest"])
            if i >= 2:
                ctx.wait(sE, sso[j], ctx.val(sso[j]))
            c = sE.activation(desti[j][:], dst0[:], AF.Copy, bias=0.0, scale=1.0)
            io["_s5_desti"] = ctx.inc("ss", c)

        @blk.gpsimd
        def _(g):
            ctx.wait(g, "ss", io["_s5_desti"])
            ctx.inc(sso[j], g.dma_start(
                out=dest_d[s, :, t:t + 1], in_=desti[j][:],
                allow_slow_non_contiguous=True), 16)
            ctx.inc(sso[j], g.dma_start(
                out=destf_d[s, :, t:t + 1], in_=dst0[:],
                allow_slow_non_contiguous=True), 16)

        @blk.gpsimd
        def _(g):
            ctx.wait(g, "sv", tile_done[i])
            ctx.inc(sfo[j], g.dma_start(
                out=feats_d[s, t * 128:(t + 1) * 128, :], in_=xn1[j][:]), 16)
            ctx.inc(smo[j], g.dma_start(
                out=mask_d[s, t * 128:(t + 1) * 128, :], in_=maskt[j][:]), 16)
            ctx.inc(smo[j], g.dma_start(
                out=asel_d[s, t * 128:(t + 1) * 128, :], in_=aselt[j][:]), 16)
            ctx.wait(g, "sv", io["_m5_xhb"])
            ctx.inc(sho[j], g.dma_start(
                out=xhat_tm_d[s, t * 128:(t + 1) * 128, :], in_=xhb[j][:]), 16)

        if i + NBUF < NSEQ * NT:
            emit_load(i + NBUF)

        if t == NT - 1:
            @blk.gpsimd
            def _(g):
                ctx.wait(g, "sv", tile_done[i])
                for p in range(NCH):
                    ctx.inc("sgd", g.dma_start(
                        out=xn1T_d[s, p * 128:(p + 1) * 128, :],
                        in_=asn1[:, p * S:(p + 1) * S]), 16)
                    ctx.inc("sgd", g.dma_start(
                        out=xhatT_d[s, p * 128:(p + 1) * 128, :],
                        in_=asx[:, p * S:(p + 1) * S]), 16)
            io["_m5_store"] = ctx.val("sgd")
            if s == 0:
                @blk.vector
                def _(v):
                    ctx.wait(v, "sgd", io["_m5_store"])

    io["s5_done_sgd"] = ctx.val("sgd")



def emit_stage6a(ctx, blk, io):
    nc, E = ctx.nc, ctx.E
    xn1T_d = io["xn1T"]
    qTl_d, kTl_d, vl_d = io["qTl"], io["kTl"], io["vl"]
    ktm_d = io["ktm"]
    wq_d, wk_d, wv_d = io["L1Wq"], io["L1Wk"], io["L1Wv"]
    bq_d, bk_d = io["l1bq_pc"], io["l1bk_pc"]
    bv_d = io["l1bv_row"]

    xnT_sb = E(nc.sbuf_tensor("s6a_xnT", [128, NCH * S], BF16))
    wsb = {}
    for nm in ("wq", "wk", "wv"):
        wsb[nm] = E(nc.sbuf_tensor(f"s6a_{nm}", [128, NCH * D], BF16))
    bq_sb = E(nc.sbuf_tensor("s6a_bq", [128, NCH], F32))
    bk_sb = E(nc.sbuf_tensor("s6a_bk", [128, NCH], F32))
    bv_sb = E(nc.sbuf_tensor("s6a_bv", [1, D], BF16))
    bkr_sb = E(nc.sbuf_tensor("s6a_bkr", [1, D], BF16))
    ones_bf = E(nc.sbuf_tensor("s6a_ones1", [1, 128], BF16))
    cbuf = {"q": [E(nc.sbuf_tensor(f"s6a_cq{j}", [128, S], BF16)) for j in range(2)],
            "k": [E(nc.sbuf_tensor(f"s6a_ck{j}", [128, S], BF16)) for j in range(2)]}
    v_sb = E(nc.sbuf_tensor("s6a_v", [128, NT * D], BF16))
    ps_a = [E(nc.psum_tensor(f"s6a_psa{j}", [128, 512], F32)) for j in range(2)]
    ps_v = [E(nc.psum_tensor(f"s6a_psv{j}", [128, 512], F32)) for j in range(2)]

    sxl = ctx.make_sems("s6axl", 1)[0]
    scb = ctx.make_sems("s6acb", 4)

    @blk.gpsimd
    def _(g):
        for nm, d_ in (("wq", wq_d), ("wk", wk_d), ("wv", wv_d)):
            src3 = d_.rearrange("(c p) o -> p c o", p=128)
            dst3 = wsb[nm][:].rearrange("p (c o) -> p c o", o=D)
            ctx.inc("sgd", g.dma_start(out=dst3, in_=src3), 16)
        ctx.inc("sgd", g.dma_start(out=bv_sb[:], in_=bv_d), 16)
        ctx.inc("sgd", g.dma_start(out=bkr_sb[:], in_=io["l1bk_row"]), 16)
    w_loaded = ctx.val("sgd")

    @blk.sync
    def _(q):
        ctx.inc("sd", q.dma_start(out=bq_sb[:], in_=bq_d), 16)
        ctx.inc("sd", q.dma_start(out=bk_sb[:], in_=bk_d), 16)
    misc_loaded = ctx.val("sd")

    @blk.vector
    def _(v):
        c0 = v.memset(ones_bf[:], 1.0)
        io["_s6a_ones"] = ctx.inc("sv", c0)

    for s in range(NSEQ):
        @blk.sync
        def _(q):
            if s > 0:
                ctx.wait(q, "st", io["_s6a_vmm"])
            src3 = xn1T_d[s].rearrange("(c p) t -> p c t", p=128)
            dst3 = xnT_sb[:].rearrange("p (c t) -> p c t", t=S)
            ctx.inc(sxl, q.dma_start(out=dst3, in_=src3), 16)
        xn_ready = ctx.val(sxl)

        for kind, wa, ba, dram in (("q", "wq", bq_sb, qTl_d),
                                   ("k", "wk", bk_sb, kTl_d)):
            for c in range(NCH):
                slot = c % 2
                for g_ in range(4):
                    jj = g_ % 2

                    @blk.tensor
                    def _(t):
                        if s == 0 and kind == "q" and c == 0 and g_ == 0:
                            ctx.wait(t, "sgd", w_loaded)
                            ctx.wait(t, "sd", misc_loaded)
                        ctx.wait(t, sxl, xn_ready)
                        if ("s6a_ev", jj) in io:
                            ctx.wait(t, "sv", io[("s6a_ev", jj)])
                        for kc in range(NCH):
                            ins = t.matmul(
                                ps_a[jj][:],
                                wsb[wa][:, kc * D + c * 128:kc * D + (c + 1) * 128],
                                xnT_sb[:, kc * S + g_ * 512:kc * S + g_ * 512 + 512],
                                start=(kc == 0), stop=(kc == NCH - 1))
                        io["_s6a_mm"] = ctx.inc("st", ins)

                    @blk.vector
                    def _(v):
                        ctx.wait(v, "st", io["_s6a_mm"])
                        if g_ == 0:
                            sem = scb[(0 if kind == "q" else 2) + slot]
                            ctx.wait(v, sem, ctx.val(sem))
                        c2 = ts(v, cbuf[kind][slot][:, g_ * 512:(g_ + 1) * 512],
                                ps_a[jj][:], ba[:, c:c + 1], ALU.add)
                        io[("s6a_ev", jj)] = ctx.inc("sv", c2)
                        io["_s6a_evl"] = io[("s6a_ev", jj)]

                @blk.gpsimd
                def _(g):
                    ctx.wait(g, "sv", io["_s6a_evl"])
                    sem = scb[(0 if kind == "q" else 2) + slot]
                    ctx.inc(sem, g.dma_start(
                        out=dram[s, c * 128:(c + 1) * 128, :],
                        in_=cbuf[kind][slot][:]), 16)

        for kind2, wname, brow, dram2 in (("v", "wv", bv_sb, vl_d),
                                          ("ktm", "wk", bkr_sb, ktm_d)):
            for tt in range(NT):
                for hf in range(2):
                    j = (tt * 2 + hf) % 2

                    @blk.tensor
                    def _(t):
                        if s == 0 and kind2 == "v" and tt == 0 and hf == 0:
                            ctx.wait(t, "sv", io["_s6a_ones"])
                        if ("s6a_vev", j) in io:
                            ctx.wait(t, "sv", io[("s6a_vev", j)])
                        for kc in range(NCH):
                            t.matmul(ps_v[j][:],
                                     xnT_sb[:, kc * S + tt * 128:kc * S + (tt + 1) * 128],
                                     wsb[wname][:, kc * D + hf * 512:kc * D + (hf + 1) * 512],
                                     start=(kc == 0), stop=False)
                        ins = t.matmul(ps_v[j][:], ones_bf[:],
                                       brow[:, hf * 512:(hf + 1) * 512],
                                       start=False, stop=True)
                        io["_s6a_vmm"] = ctx.inc("st", ins)

                    @blk.vector
                    def _(vE):
                        ctx.wait(vE, "st", io["_s6a_vmm"])
                        c2 = vE.tensor_copy(
                            v_sb[:, tt * D + hf * 512:tt * D + (hf + 1) * 512],
                            ps_v[j][:])
                        io[("s6a_vev", j)] = ctx.inc("sv", c2)
                        io["_s6a_vevl"] = io[("s6a_vev", j)]

            @blk.gpsimd
            def _(g):
                ctx.wait(g, "sv", io[("s6a_vev", 0)])
                ctx.wait(g, "sv", io[("s6a_vev", 1)])
                dst3 = dram2[s].rearrange("(tt p) f -> p tt f", p=128)
                src3 = v_sb[:].rearrange("p (tt f) -> p tt f", f=D)
                ctx.inc("sgd", g.dma_start(out=dst3, in_=src3), 16)
            io["_s6a_vstore"] = ctx.val("sgd")

            @blk.vector
            def _(vE):
                ctx.wait(vE, "sgd", io["_s6a_vstore"])

    io["s6a_done_sgd"] = ctx.val("sgd")



def emit_stage6b(ctx, blk, io):
    nc, E = ctx.nc, ctx.E
    qTl_d, kTl_d, vl_d, oTl_d = io["qTl"], io["kTl"], io["vl"], io["oTl"]
    ktm_d = io["ktm"]
    mt_d = io["decay_mt"]
    dup2_d = io["dup2"]
    wdec_d = io["wdec"]
    lamq_d = io["lamq"]
    LAM128 = 0.95 ** 128
    ident_bf = io["ident_bf"]

    qT_sb = E(nc.sbuf_tensor("s6b_qT", [128, NCH * S], BF16))
    kT_sb = E(nc.sbuf_tensor("s6b_kT", [128, NCH * S], BF16))
    v_sb = E(nc.sbuf_tensor("s6b_v", [128, NT * D], BF16))
    ktm_sb = E(nc.sbuf_tensor("s6b_ktm", [128, NT * D], BF16))
    mt_sb = E(nc.sbuf_tensor("s6b_mt", [128, 128], F32))
    wdec = E(nc.sbuf_tensor("s6b_wdec", [128, 1], F32))
    lamq = E(nc.sbuf_tensor("s6b_lamq", [128, 1], F32))
    Am = [E(nc.sbuf_tensor(f"s6b_am{j}", [128, 128], BF16)) for j in range(2)]
    Kd = E(nc.sbuf_tensor("s6b_kd", [128, 64], BF16))
    G = E(nc.sbuf_tensor("s6b_G", [64, 64], F32))
    Gb = E(nc.sbuf_tensor("s6b_Gb", [128, 64], BF16))
    dup2 = E(nc.sbuf_tensor("s6b_dup2", [64, 128], F32))
    tb = E(nc.sbuf_tensor("s6b_tb", [128, 64], F32))
    Ob = E(nc.sbuf_tensor("s6b_Ob", [128, 64], BF16))
    oTl_asm = E(nc.sbuf_tensor("s6b_asm", [128, NCH * S], BF16))
    ps_sA = E(nc.psum_tensor("s6b_psA", [128, 128], F32))
    ps_oA = E(nc.psum_tensor("s6b_poA", [128, 64], F32))
    ps_g = E(nc.psum_tensor("s6b_pg", [64, 64], F32))
    ps_oB = E(nc.psum_tensor("s6b_poB", [128, 64], F32))
    ps_gb = E(nc.psum_tensor("s6b_pgb", [128, 64], F32))
    ps_ot = E(nc.psum_tensor("s6b_pot", [64, 128], BF16))

    sld = ctx.make_sems("s6bld", 4)

    @blk.sync
    def _(q):
        ctx.inc("sd", q.dma_start(out=mt_sb[:], in_=mt_d), 16)
        ctx.inc("sd", q.dma_start(out=wdec[:], in_=wdec_d), 16)
        ctx.inc("sd", q.dma_start(out=lamq[:], in_=lamq_d), 16)
        ctx.inc("sd", q.dma_start(out=dup2[:], in_=dup2_d), 16)
    misc_ld = ctx.val("sd")

    for s in range(NSEQ):
        @blk.sync
        def _(q):
            if s > 0:
                ctx.wait(q, "sgd", io["_s6b_store"])
            for dsem, d_, sb, il in ((sld[0], qTl_d, qT_sb, "(c p) t -> p c t"),
                                     (sld[1], kTl_d, kT_sb, "(c p) t -> p c t"),
                                     (sld[2], vl_d, v_sb, "(tt p) f -> p tt f"),
                                     (sld[3], ktm_d, ktm_sb, "(tt p) f -> p tt f")):
                src3 = d_[s].rearrange(il, p=128)
                if sb is v_sb or sb is ktm_sb:
                    dst3 = sb[:].rearrange("p (tt f) -> p tt f", f=D)
                else:
                    dst3 = sb[:].rearrange("p (c t) -> p c t", t=S)
                ctx.inc(dsem, q.dma_start(out=dst3, in_=src3), 16)
        lds = [ctx.val(x) for x in sld]

        for h in range(16):
            ch, po = h // 2, (h % 2) * 64

            @blk.vector
            def _(v):
                if s == 0 and h == 0:
                    ctx.wait(v, "sd", misc_ld)
                c = v.memset(G[:], 0.0)
                io["_s6b_G0"] = ctx.inc("sv", c)

            for ct in range(NT):
                ja = ct % 2

                @blk.tensor
                def _(t):
                    ctx.wait(t, sld[0], lds[0])
                    ctx.wait(t, sld[1], lds[1])
                    ctx.wait(t, sld[2], lds[2])
                    ctx.wait(t, sld[3], lds[3])
                    if ("s6b_amL",) in io:
                        ctx.wait(t, "sv", io[("s6b_amL",)])
                    c = t.matmul(ps_sA[:],
                                 kT_sb[po:po + 64, ch * S + ct * 128:ch * S + (ct + 1) * 128],
                                 qT_sb[po:po + 64, ch * S + ct * 128:ch * S + (ct + 1) * 128],
                                 start=True, stop=True)
                    io["_s6b_sA"] = ctx.inc("st", c)

                @blk.vector
                def _(v):
                    ctx.wait(v, "st", io["_s6b_sA"])
                    c = v.tensor_tensor(out=Am[ja][:], in0=ps_sA[:], in1=mt_sb[:],
                                        op=ALU.mult)
                    io[("s6b_am", ja)] = ctx.inc("sv", c)
                    io[("s6b_amL",)] = io[("s6b_am", ja)]
                    if ("s6b_gmm",) in io:
                        ctx.wait(v, "st", io[("s6b_gmm",)])
                    c = ts(v, Kd[:], ktm_sb[:, ct * D + h * 64:ct * D + (h + 1) * 64],
                           wdec[:], ALU.mult)
                    io[("s6b_kd",)] = ctx.inc("sv", c)

                @blk.tensor
                def _(t):
                    ctx.wait(t, "sv", io[("s6b_am", ja)])
                    if ("s6b_obc",) in io:
                        ctx.wait(t, "sv", io[("s6b_obc",)])
                    c = t.matmul(ps_oA[:], Am[ja][:],
                                 v_sb[:, ct * D + h * 64:ct * D + (h + 1) * 64],
                                 start=True, stop=True)
                    io["_s6b_oA"] = ctx.inc("st", c)
                    ctx.wait(t, "sv", io[("s6b_kd",)])
                    if ("s6b_gup",) in io:
                        ctx.wait(t, "sv", io[("s6b_gup",)])
                    c = t.matmul(ps_g[:], Kd[:],
                                 v_sb[:, ct * D + h * 64:ct * D + (h + 1) * 64],
                                 start=True, stop=True)
                    io["_s6b_g"] = ctx.inc("st", c)
                    io[("s6b_gmm",)] = io["_s6b_g"]
                    if ct > 0:
                        c = t.matmul(ps_oB[:],
                                     qT_sb[po:po + 64, ch * S + ct * 128:ch * S + (ct + 1) * 128],
                                     Gb[po:po + 64, :], start=True, stop=True)
                        io[("s6b_oB",)] = ctx.inc("st", c)

                @blk.vector
                def _(v):
                    ctx.wait(v, "st", io["_s6b_oA"])
                    if ct > 0:
                        ctx.wait(v, "st", io[("s6b_oB",)])
                        c = ts(v, tb[:], ps_oB[:], lamq[:], ALU.mult)
                        ctx.wait(v, "sv", ctx.inc("sv", c))
                        c = v.tensor_tensor(out=Ob[:], in0=ps_oA[:], in1=tb[:],
                                            op=ALU.add)
                    else:
                        c = v.tensor_copy(Ob[:], ps_oA[:])
                    io["_s6b_ob"] = ctx.inc("sv", c)
                    ctx.wait(v, "st", io["_s6b_g"])
                    c = v.scalar_tensor_tensor(out=G[:], in0=G[:], scalar=LAM128,
                                               in1=ps_g[:], op0=ALU.mult,
                                               op1=ALU.add)
                    io[("s6b_gup",)] = ctx.inc("sv", c)

                @blk.tensor
                def _(t):
                    ctx.wait(t, "sv", io["_s6b_ob"])
                    if ("s6b_otc",) in io:
                        ctx.wait(t, "sv", io[("s6b_otc",)])
                    c = t.transpose(ps_ot[:], Ob[:], ident_bf[:])
                    io["_s6b_ot"] = ctx.inc("st", c)
                    ctx.wait(t, "sv", io[("s6b_gup",)])
                    c = t.matmul(ps_gb[:], dup2[:], G[:], start=True, stop=True)
                    io["_s6b_gbm"] = ctx.inc("st", c)

                @blk.vector
                def _(v):
                    ctx.wait(v, "st", io["_s6b_gbm"])
                    if ct > 0:
                        ctx.wait(v, "st", io[("s6b_oB",)])
                    c = v.tensor_copy(Gb[:], ps_gb[:])
                    io["_s6b_gb"] = ctx.inc("sv", c)

                @blk.tensor
                def _(t):
                    if ct < NT - 1:
                        ctx.wait(t, "sv", io["_s6b_gb"])

                @blk.vector
                def _(v):
                    ctx.wait(v, "st", io["_s6b_ot"])
                    dst = oTl_asm[po:po + 64, ch * S + ct * 128:ch * S + (ct + 1) * 128]
                    c = v.tensor_copy(dst, ps_ot[:])
                    io[("s6b_otc",)] = ctx.inc("sv", c)

        @blk.gpsimd
        def _(g):
            ctx.wait(g, "sv", io[("s6b_otc",)])
            for p in range(NCH):
                ctx.inc("sgd", g.dma_start(
                    out=oTl_d[s, p * 128:(p + 1) * 128, :],
                    in_=oTl_asm[:, p * S:(p + 1) * S]), 16)
        io["_s6b_store"] = ctx.val("sgd")

    io["s6b_done_sgd"] = ctx.val("sgd")



def emit_stage6c(ctx, blk, io):
    nc, E = ctx.nc, ctx.E
    oTl_d, s1_d = io["oTl"], io["s1"]
    wo_d = io["L1Wo"]
    bo_d = io["l1bo_row"]
    gam_d, bet_d = io["l1gam_row"], io["l1bet_row"]

    oT_sb = E(nc.sbuf_tensor("s6c_oT", [128, NCH * S], BF16))
    wo_sb = E(nc.sbuf_tensor("s6c_wo", [128, NCH * D], BF16))
    bo_row = E(nc.sbuf_tensor("s6c_bor", [1, D], F32))
    gam_row = E(nc.sbuf_tensor("s6c_gam", [1, D], F32))
    bet_row = E(nc.sbuf_tensor("s6c_bet", [1, D], F32))
    onesf = E(nc.sbuf_tensor("s6c_ones", [1, 128], F32))
    onesc = E(nc.sbuf_tensor("s6c_onesc", [128, 1], F32))
    bo_exp = E(nc.sbuf_tensor("s6c_boe", [128, D], F32))
    y_sb = E(nc.sbuf_tensor("s6c_y", [128, NT * D], F32))
    sq_scr = E(nc.sbuf_tensor("s6c_sq", [128, D], F32))
    acc_s = E(nc.sbuf_tensor("s6c_accs", [128, 16], F32))
    acc_q = E(nc.sbuf_tensor("s6c_accq", [128, 16], F32))
    srow = E(nc.sbuf_tensor("s6c_srow", [1, 16], F32))
    qrow = E(nc.sbuf_tensor("s6c_qrow", [1, 16], F32))
    mrow = E(nc.sbuf_tensor("s6c_mrow", [1, 16], F32))
    vrow = E(nc.sbuf_tensor("s6c_vrow", [1, 16], F32))
    rrow = E(nc.sbuf_tensor("s6c_rrow", [1, 16], F32))
    arow = E(nc.sbuf_tensor("s6c_arow", [1, D], F32))
    brow = E(nc.sbuf_tensor("s6c_brow", [1, D], F32))
    epsr = E(nc.sbuf_tensor("s6c_epsr", [1, 16], F32))
    a_exp = E(nc.sbuf_tensor("s6c_aexp", [128, D], F32))
    b_exp = E(nc.sbuf_tensor("s6c_bexp", [128, D], F32))
    s1t = [E(nc.sbuf_tensor(f"s6c_s1t{j}", [128, D], F32)) for j in range(2)]
    ps_s = [E(nc.psum_tensor(f"s6c_pss{j}", [128, 512], F32)) for j in range(2)]
    ps_b = E(nc.psum_tensor("s6c_psb", [128, D], F32))
    ps_r = E(nc.psum_tensor("s6c_psr", [1, 16], F32))

    sxl = ctx.make_sems("s6cxl", 1)[0]
    so1 = ctx.make_sems("s6co", 2)

    @blk.gpsimd
    def _(g):
        src3 = wo_d.rearrange("(c p) o -> p c o", p=128)
        ctx.inc("sgd", g.dma_start(
            out=wo_sb[:].rearrange("p (c o) -> p c o", o=D), in_=src3), 16)
    w_ld = ctx.val("sgd")

    @blk.sync
    def _(q):
        ctx.inc("sd", q.dma_start(out=bo_row[:], in_=bo_d), 16)
        ctx.inc("sd", q.dma_start(out=gam_row[:], in_=gam_d), 16)
        ctx.inc("sd", q.dma_start(out=bet_row[:], in_=bet_d), 16)
    b_ld = ctx.val("sd")

    @blk.vector
    def _(v):
        v.memset(onesf[:], 1.0)
        v.memset(onesc[:], 1.0)
        c = v.memset(epsr[:], 1e-5)
        io["_s6c_ones"] = ctx.inc("sv", c)

    @blk.tensor
    def _(t):
        ctx.wait(t, "sd", b_ld)
        ctx.wait(t, "sv", io["_s6c_ones"])
        for half in range(2):
            c = t.matmul(ps_b[:, half * 512:(half + 1) * 512], onesf[:],
                         bo_row[:, half * 512:(half + 1) * 512],
                         start=True, stop=True)
        io["_s6c_bomm"] = ctx.inc("st", c)

    @blk.vector
    def _(v):
        ctx.wait(v, "st", io["_s6c_bomm"])
        c = v.tensor_copy(bo_exp[:], ps_b[:])
        io["_s6c_boe"] = ctx.inc("sv", c)

    for s in range(NSEQ):
        @blk.sync
        def _(q):
            if s > 0:
                ctx.wait(q, "sv", io["_s6c_applydone"])
            src3 = oTl_d[s].rearrange("(c p) t -> p c t", p=128)
            ctx.inc(sxl, q.dma_start(
                out=oT_sb[:].rearrange("p (c t) -> p c t", t=S), in_=src3), 16)
        oT_ld = ctx.val(sxl)

        @blk.vector
        def _(v):
            if s > 0:
                ctx.wait(v, "sv", io["_s6c_applydone"])
            v.memset(acc_s[:], 0.0)
            c = v.memset(acc_q[:], 0.0)
            io["_s6c_acc0"] = ctx.inc("sv", c)

        for tt in range(NT):
            j = tt % 2
            for half in range(2):
                @blk.tensor
                def _(t):
                    if s == 0 and tt == 0 and half == 0:
                        ctx.wait(t, "sgd", w_ld)
                    ctx.wait(t, sxl, oT_ld)
                    if ("s6c_sev", j) in io:
                        ctx.wait(t, "sv", io[("s6c_sev", j)])
                    for kc in range(NCH):
                        c = t.matmul(
                            ps_s[j][:],
                            oT_sb[:, kc * S + tt * 128:kc * S + (tt + 1) * 128],
                            wo_sb[:, kc * D + half * 512:kc * D + (half + 1) * 512],
                            start=(kc == 0), stop=(kc == NCH - 1))
                    io[("s6c_smm", j, half)] = ctx.inc("st", c)

                @blk.vector
                def _(v):
                    ctx.wait(v, "st", io[("s6c_smm", j, half)])
                    if s == 0 and tt == 0 and half == 0:
                        ctx.wait(v, "sv", io["_s6c_boe"])
                    c = v.tensor_tensor(
                        out=y_sb[:, tt * D + half * 512:tt * D + (half + 1) * 512],
                        in0=ps_s[j][:],
                        in1=bo_exp[:, half * 512:(half + 1) * 512], op=ALU.add)
                    io[("s6c_sev", j)] = ctx.inc("sv", c)

            @blk.vector
            def _(v):
                ctx.wait(v, "sv", io["_s6c_acc0"])
                yv = y_sb[:, tt * D:(tt + 1) * D].rearrange("p (g e) -> p g e", e=64)
                c = v.tensor_reduce(out=sq_scr[:, 0:16], in_=yv, axis=AX.X,
                                    op=ALU.add)
                ctx.wait(v, "sv", ctx.inc("sv", c))
                c = v.tensor_tensor(out=acc_s[:], in0=acc_s[:],
                                    in1=sq_scr[:, 0:16], op=ALU.add)
                io["_s6c_sacc"] = ctx.inc("sv", c)

            @blk.scalar
            def _(sE):
                ctx.wait(sE, "sv", io["_s6c_sacc"])
                c = sE.activation(sq_scr[:], y_sb[:, tt * D:(tt + 1) * D],
                                  AF.Square, bias=0.0, scale=1.0)
                io["_s6c_sq"] = ctx.inc("ss", c)

            @blk.vector
            def _(v):
                ctx.wait(v, "ss", io["_s6c_sq"])
                sv3 = sq_scr[:].rearrange("p (g e) -> p g e", e=64)
                c = v.tensor_reduce(out=sq_scr[:, 0:16], in_=sv3, axis=AX.X,
                                    op=ALU.add)
                ctx.wait(v, "sv", ctx.inc("sv", c))
                c = v.tensor_tensor(out=acc_q[:], in0=acc_q[:],
                                    in1=sq_scr[:, 0:16], op=ALU.add)
                io["_s6c_qacc"] = ctx.inc("sv", c)

        @blk.tensor
        def _(t):
            ctx.wait(t, "sv", io["_s6c_qacc"])
            c = t.matmul(ps_r[:], onesc[:], acc_s[:], start=True, stop=True)
            io["_s6c_psum1"] = ctx.inc("st", c)

        @blk.vector
        def _(v):
            ctx.wait(v, "st", io["_s6c_psum1"])
            c = v.tensor_copy(srow[:], ps_r[:])
            io["_s6c_srow"] = ctx.inc("sv", c)

        @blk.tensor
        def _(t):
            ctx.wait(t, "sv", io["_s6c_srow"])
            c = t.matmul(ps_r[:], onesc[:], acc_q[:], start=True, stop=True)
            io["_s6c_psum2"] = ctx.inc("st", c)

        @blk.vector
        def _(v):
            ctx.wait(v, "st", io["_s6c_psum2"])
            c = v.tensor_copy(qrow[:], ps_r[:])
            ctx.wait(v, "sv", ctx.inc("sv", c))
            NTOT = 64.0 * S
            c = ts(v, mrow[:], srow[:], 1.0 / NTOT, ALU.mult)
            ctx.wait(v, "sv", ctx.inc("sv", c))
            c = v.tensor_tensor(out=vrow[:], in0=mrow[:], in1=mrow[:], op=ALU.mult)
            ctx.wait(v, "sv", ctx.inc("sv", c))
            c = v.scalar_tensor_tensor(out=vrow[:], in0=qrow[:], scalar=1.0 / NTOT,
                                       in1=vrow[:], op0=ALU.mult, op1=ALU.subtract)
            ctx.wait(v, "sv", ctx.inc("sv", c))
            c = v.tensor_tensor(out=vrow[:], in0=vrow[:], in1=epsr[:], op=ALU.add)
            io["_s6c_var"] = ctx.inc("sv", c)

        @blk.scalar
        def _(sE):
            ctx.wait(sE, "sv", io["_s6c_var"])
            c = sE.activation(rrow[:], vrow[:], AF.Sqrt, bias=0.0, scale=1.0)
            io["_s6c_sd"] = ctx.inc("ss", c)

        @blk.vector
        def _(v):
            ctx.wait(v, "ss", io["_s6c_sd"])
            c = v.reciprocal(rrow[:], rrow[:])
            ctx.wait(v, "sv", ctx.inc("sv", c))
            gv = gam_row[:].rearrange("p (g e) -> p g e", e=64)
            rv = AP(rrow.ap().tensor, rrow.ap().offset,
                    [list(rrow.ap().ap[0]), list(rrow.ap().ap[1]), [0, 64]])
            av = arow[:].rearrange("p (g e) -> p g e", e=64)
            c = v.tensor_tensor(out=av, in0=gv, in1=rv, op=ALU.mult)
            ctx.wait(v, "sv", ctx.inc("sv", c))
            mv = AP(mrow.ap().tensor, mrow.ap().offset,
                    [list(mrow.ap().ap[0]), list(mrow.ap().ap[1]), [0, 64]])
            bv3 = brow[:].rearrange("p (g e) -> p g e", e=64)
            c = v.tensor_tensor(out=bv3, in0=arow[:].rearrange("p (g e) -> p g e", e=64),
                                in1=mv, op=ALU.mult)
            ctx.wait(v, "sv", ctx.inc("sv", c))
            c = v.tensor_tensor(out=brow[:], in0=bet_row[:], in1=brow[:],
                                op=ALU.subtract)
            io["_s6c_ab"] = ctx.inc("sv", c)

        @blk.tensor
        def _(t):
            ctx.wait(t, "sv", io["_s6c_ab"])
            for half in range(2):
                c = t.matmul(ps_b[:, half * 512:(half + 1) * 512], onesf[:],
                             arow[:, half * 512:(half + 1) * 512],
                             start=True, stop=True)
            io["_s6c_amm"] = ctx.inc("st", c)

        @blk.vector
        def _(v):
            ctx.wait(v, "st", io["_s6c_amm"])
            c = v.tensor_copy(a_exp[:], ps_b[:])
            io["_s6c_aexp"] = ctx.inc("sv", c)

        @blk.tensor
        def _(t):
            ctx.wait(t, "sv", io["_s6c_aexp"])
            for half in range(2):
                c = t.matmul(ps_b[:, half * 512:(half + 1) * 512], onesf[:],
                             brow[:, half * 512:(half + 1) * 512],
                             start=True, stop=True)
            io["_s6c_bmm"] = ctx.inc("st", c)

        @blk.vector
        def _(v):
            ctx.wait(v, "st", io["_s6c_bmm"])
            c = v.tensor_copy(b_exp[:], ps_b[:])
            io["_s6c_bexp"] = ctx.inc("sv", c)

        for tt in range(NT):
            j = tt % 2

            @blk.vector
            def _(v):
                if tt == 0:
                    ctx.wait(v, "sv", io["_s6c_bexp"])
                ctx.wait(v, so1[j], ctx.val(so1[j]))
                c = v.tensor_tensor(out=s1t[j][:], in0=y_sb[:, tt * D:(tt + 1) * D],
                                    in1=a_exp[:], op=ALU.mult)
                ctx.wait(v, "sv", ctx.inc("sv", c))
                c = v.tensor_tensor(out=s1t[j][:], in0=s1t[j][:], in1=b_exp[:],
                                    op=ALU.add)
                io[("s6c_app", j)] = ctx.inc("sv", c)
                io["_s6c_applydone"] = io[("s6c_app", j)]

            @blk.gpsimd
            def _(g):
                ctx.wait(g, "sv", io[("s6c_app", j)])
                ctx.inc(so1[j], g.dma_start(
                    out=s1_d[s, tt * 128:(tt + 1) * 128, :], in_=s1t[j][:]), 16)

    io["s6c_done_sgd"] = ctx.val("sgd")



def emit_stage7(ctx, blk, io):
    nc, E = ctx.nc, ctx.E
    xhat_tm_d, dest_d = io["xhat_tm"], io["dest"]
    destf_d = io["destf"]
    xsort_d, zsort_d = io["xsort"], io["zsort"]
    w1_d, w2_d = io["w1"], io["w2"]
    ga_d, be_d = io["ga_pc"], io["be_pc"]
    b1_d = io["b1_pc"]
    b2_d = io["b2_rows"]
    slotfull_d = io["slotfull"]
    ident_bf = io["ident_bf"]
    FF = 4096
    C = 512
    NG = C // 128

    import contextlib
    with contextlib.ExitStack() as st0:
        E0 = st0.enter_context
        xh_sb = E0(nc.sbuf_tensor("s7p_xh", [128, NT * D], BF16))
        dest_sb = E0(nc.sbuf_tensor("s7p_dest", [128, NT], F32))
        slotf = E0(nc.sbuf_tensor("s7p_slot", [128, 4096], F32))
        pt = [E0(nc.sbuf_tensor(f"s7p_pt{j}", [128, 128], BF16)) for j in range(2)]
        xs_st = [E0(nc.sbuf_tensor(f"s7p_xs{j}", [128, D], BF16)) for j in range(2)]
        ps_x = [E0(nc.psum_tensor(f"s7p_psx{j}", [128, 512], F32)) for j in range(4)]
        sxl = ctx.make_sems("s7pxl", 1)[0]
        sso2 = ctx.make_sems("s7pso", 2)

        @blk.sync
        def _(q):
            ctx.inc("sd", q.dma_start(out=slotf[:], in_=slotfull_d), 16)
        slot_ld = ctx.val("sd")

        for s in range(NSEQ):
            @blk.sync
            def _(q):
                if s > 0:
                    ctx.wait(q, "st", io["_s7p_lastmm"])
                src3 = xhat_tm_d[s].rearrange("(tt p) d -> p tt d", p=128)
                ctx.inc(sxl, q.dma_start(
                    out=xh_sb[:].rearrange("p (tt d) -> p tt d", d=D), in_=src3), 16)
                ctx.inc(sxl, q.dma_start(out=dest_sb[:], in_=destf_d[s]), 16)
            xh_ld = ctx.val(sxl)

            for st_ in range(32):
                jj = st_ % 2
                for tc in range(NT):
                    @blk.vector
                    def _(v):
                        if st_ == 0 and tc == 0:
                            ctx.wait(v, sxl, xh_ld)
                            ctx.wait(v, "sd", slot_ld)
                        key = ("s7p_mmh", tc % 2)
                        if key in io:
                            ctx.wait(v, "st", io[key])
                        c = ts(v, pt[tc % 2][:],
                               slotf[:, st_ * 128:(st_ + 1) * 128],
                               dest_sb[:, tc:tc + 1], ALU.is_equal)
                        io[("s7p_pt", tc)] = ctx.inc("sv", c)

                    @blk.tensor
                    def _(t):
                        ctx.wait(t, "sv", io[("s7p_pt", tc)])
                        for half in range(2):
                            bank = (st_ % 2) * 2 + half
                            if tc == 0 and ("s7p_ev", bank) in io:
                                ctx.wait(t, "sv", io[("s7p_ev", bank)])
                            c = t.matmul(
                                ps_x[bank][:],
                                pt[tc % 2][:],
                                xh_sb[:, tc * D + half * 512:tc * D + (half + 1) * 512],
                                start=(tc == 0), stop=(tc == NT - 1))
                        io[("s7p_mmh", tc % 2)] = ctx.inc("st", c)
                        if tc == NT - 1:
                            for half in range(2):
                                io[("s7p_mmb", (st_ % 2) * 2 + half)] = \
                                    io[("s7p_mmh", tc % 2)]
                        io["_s7p_lastmm"] = io[("s7p_mmh", tc % 2)]

                @blk.vector
                def _(v):
                    ctx.wait(v, sso2[jj], ctx.val(sso2[jj]))
                    for half in range(2):
                        bank = (st_ % 2) * 2 + half
                        ctx.wait(v, "st", io[("s7p_mmb", bank)])
                        c = v.tensor_copy(xs_st[jj][:, half * 512:(half + 1) * 512],
                                          ps_x[bank][:])
                        io[("s7p_ev", bank)] = ctx.inc("sv", c)
                    io["_s7p_evl"] = ctx.val("sv")

                @blk.gpsimd
                def _(g):
                    ctx.wait(g, "sv", io["_s7p_evl"])
                    ctx.inc(sso2[jj], g.dma_start(
                        out=xsort_d[s, st_ * 128:(st_ + 1) * 128, :],
                        in_=xs_st[jj][:]), 16)
        @blk.vector
        def _(v):
            ctx.wait(v, sso2[0], ctx.val(sso2[0]))
            ctx.wait(v, sso2[1], ctx.val(sso2[1]))
    ctx.barrier(blk)

    w1_sb = E(nc.sbuf_tensor("s7_w1", [128, NCH * FF], BF16))
    w2_sb = E(nc.sbuf_tensor("s7_w2", [128, (FF // 128) * D], BF16))
    hT = E(nc.sbuf_tensor("s7_hT", [128, NCH * C], BF16))
    aT = E(nc.sbuf_tensor("s7_aT", [128, 32 * C], BF16))
    gb = [E(nc.sbuf_tensor(f"s7_gb{j}", [128, D], BF16)) for j in range(2)]
    zf = [E(nc.sbuf_tensor(f"s7_zf{j}", [128, D], F32)) for j in range(2)]
    ga_sb = E(nc.sbuf_tensor("s7_ga", [128, 64], F32))
    be_sb = E(nc.sbuf_tensor("s7_be", [128, 64], F32))
    b1_sb = E(nc.sbuf_tensor("s7_b1", [128, 8 * 32], F32))
    b2_row = E(nc.sbuf_tensor("s7_b2r", [1, D], F32))
    b2_exp = E(nc.sbuf_tensor("s7_b2e", [128, D], F32))
    onesf = E(nc.sbuf_tensor("s7_ones", [1, 128], F32))
    ps_a = [E(nc.psum_tensor(f"s7_psa{j}", [128, C], F32)) for j in range(2)]
    ps_z = [E(nc.psum_tensor(f"s7_psz{j}", [128, 512], F32)) for j in range(2)]
    ps_tr = E(nc.psum_tensor("s7_ptr", [128, D], BF16))
    ps_b = E(nc.psum_tensor("s7_psb", [128, D], F32))

    sw = ctx.make_sems("s7w", 2)
    sgi = ctx.make_sems("s7gi", 2)
    szo = ctx.make_sems("s7zo", 2)

    @blk.sync
    def _(q):
        ctx.inc("sd", q.dma_start(out=ga_sb[:], in_=ga_d), 16)
        ctx.inc("sd", q.dma_start(out=be_sb[:], in_=be_d), 16)
        ctx.inc("sd", q.dma_start(out=b1_sb[:], in_=b1_d), 16)
    misc_ld = ctx.val("sd")

    @blk.vector
    def _(v):
        c = v.memset(onesf[:], 1.0)
        io["_s7_zt"] = ctx.inc("sv", c)

    for e in range(8):
        @blk.gpsimd
        def _(g):
            if e > 0:
                ctx.wait(g, "st", io["_s7_lastA"])
            src1 = w1_d[e].rearrange("(c p) f -> p c f", p=128)
            ctx.inc(sw[0], g.dma_start(
                out=w1_sb[:].rearrange("p (c f) -> p c f", f=FF), in_=src1), 16)
            if e > 0:
                ctx.wait(g, "st", io["_s7_lastB"])
            src2 = w2_d[e].rearrange("(c p) f -> p c f", p=128)
            ctx.inc(sw[1], g.dma_start(
                out=w2_sb[:].rearrange("p (c f) -> p c f", f=D), in_=src2), 16)
        w_ld = [ctx.val(sw[0]), ctx.val(sw[1])]

        @blk.sync
        def _(q):
            if e > 0:
                ctx.wait(q, "sv", io["_s7_lastzev"])
            ctx.inc("sd", q.dma_start(out=b2_row[:], in_=b2_d[e:e + 1, :]), 16)
        b2_ld = ctx.val("sd")

        @blk.tensor
        def _(t):
            ctx.wait(t, "sd", b2_ld)
            ctx.wait(t, "sv", io["_s7_zt"])
            if ("s7_b2e",) in io:
                ctx.wait(t, "sv", io[("s7_b2e",)])
            for half in range(2):
                c = t.matmul(ps_b[:, half * 512:(half + 1) * 512], onesf[:],
                             b2_row[:, half * 512:(half + 1) * 512],
                             start=True, stop=True)
            io["_s7_b2mm"] = ctx.inc("st", c)

        @blk.vector
        def _(v):
            ctx.wait(v, "st", io["_s7_b2mm"])
            c = v.tensor_copy(b2_exp[:], ps_b[:])
            io[("s7_b2e",)] = ctx.inc("sv", c)

        for s in range(NSEQ):
            for k in range(NG):
                jg = k % 2

                @blk.sync
                def _(q):
                    if ("s7_trd", jg) in io:
                        ctx.wait(q, "st", io[("s7_trd", jg)])
                    ctx.inc(sgi[jg], q.dma_start(
                        out=gb[jg][:],
                        in_=xsort_d[s, e * C + k * 128:e * C + (k + 1) * 128, :]), 16)

                @blk.tensor
                def _(t):
                    ctx.wait(t, sgi[jg], ctx.val(sgi[jg]))
                    if "_s7_lasthev" in io:
                        ctx.wait(t, "sv", io["_s7_lasthev"])
                    for c_ in range(NCH):
                        c = t.transpose(ps_tr[:, c_ * 128:(c_ + 1) * 128],
                                        gb[jg][:, c_ * 128:(c_ + 1) * 128],
                                        ident_bf[:])
                    io[("s7_trd", jg)] = ctx.inc("st", c)

                @blk.vector
                def _(v):
                    ctx.wait(v, "st", io[("s7_trd", jg)])
                    ctx.wait(v, "sd", misc_ld)
                    if "_s7_lastamm" in io:
                        ctx.wait(v, "st", io["_s7_lastamm"])
                    for c_ in range(NCH):
                        c = ts(v, hT[:, c_ * C + k * 128:c_ * C + (k + 1) * 128],
                               ps_tr[:, c_ * 128:(c_ + 1) * 128],
                               ga_sb[:, e * 8 + c_:e * 8 + c_ + 1], ALU.mult,
                               be_sb[:, e * 8 + c_:e * 8 + c_ + 1], ALU.add)
                    io[("s7_hev", k)] = ctx.inc("sv", c)
                    io["_s7_lasthev"] = io[("s7_hev", k)]
            h_ready = [io[("s7_hev", k)] for k in range(NG)]

            for f128 in range(32):
                jA = f128 % 2

                @blk.tensor
                def _(t):
                    if f128 == 0:
                        ctx.wait(t, sw[0], w_ld[0])
                        for k in range(NG):
                            ctx.wait(t, "sv", h_ready[k])
                    if ("s7_rel", jA) in io:
                        ctx.wait(t, "sv", io[("s7_rel", jA)])
                    for dc in range(NCH):
                        c = t.matmul(
                            ps_a[jA][:],
                            w1_sb[:, dc * FF + f128 * 128:dc * FF + (f128 + 1) * 128],
                            hT[:, dc * C:(dc + 1) * C],
                            start=(dc == 0), stop=(dc == NCH - 1))
                    io[("s7_amm", jA)] = ctx.inc("st", c)
                    io["_s7_lastamm"] = io[("s7_amm", jA)]
                    io["_s7_lastA"] = io[("s7_amm", jA)]

                @blk.vector
                def _(v):
                    ctx.wait(v, "st", io[("s7_amm", jA)])
                    if ("s7_zmm", jA) in io:
                        ctx.wait(v, "st", io[("s7_zmm", jA)])
                    c = ts(v, aT[:, f128 * C:(f128 + 1) * C], ps_a[jA][:],
                           b1_sb[:, e * 32 + f128:e * 32 + f128 + 1],
                           ALU.add, 0.0, ALU.max)
                    io[("s7_rel", jA)] = ctx.inc("sv", c)

            for tk in range(NG):
                jz = tk % 2
                for dhalf in range(2):
                    @blk.tensor
                    def _(t):
                        if tk == 0 and dhalf == 0:
                            ctx.wait(t, sw[1], w_ld[1])
                            ctx.wait(t, "sv", io[("s7_rel", 0)])
                            ctx.wait(t, "sv", io[("s7_rel", 1)])
                        if ("s7_zev", jz) in io:
                            ctx.wait(t, "sv", io[("s7_zev", jz)])
                        for f128 in range(32):
                            c = t.matmul(
                                ps_z[jz][:],
                                aT[:, f128 * C + tk * 128:f128 * C + (tk + 1) * 128],
                                w2_sb[:, f128 * D + dhalf * 512:
                                      f128 * D + (dhalf + 1) * 512],
                                start=(f128 == 0), stop=(f128 == 31))
                        io[("s7_zmm", jz)] = ctx.inc("st", c)
                        io["_s7_lastB"] = io[("s7_zmm", jz)]

                    @blk.vector
                    def _(v):
                        ctx.wait(v, "st", io[("s7_zmm", jz)])
                        ctx.wait(v, "sv", io[("s7_b2e",)])
                        if dhalf == 0:
                            ctx.wait(v, szo[jz], ctx.val(szo[jz]))
                        c = v.tensor_tensor(
                            out=zf[jz][:, dhalf * 512:(dhalf + 1) * 512],
                            in0=ps_z[jz][:],
                            in1=b2_exp[:, dhalf * 512:(dhalf + 1) * 512],
                            op=ALU.add)
                        io[("s7_zev", jz)] = ctx.inc("sv", c)
                        io["_s7_lastzev"] = io[("s7_zev", jz)]

                @blk.gpsimd
                def _(g):
                    ctx.wait(g, "sv", io[("s7_zev", jz)])
                    ctx.inc(szo[jz], g.dma_start(
                        out=zsort_d[s, e * C + tk * 128:e * C + (tk + 1) * 128, :],
                        in_=zf[jz][:]), 16)

    io["s7_done_sgd"] = ctx.val("sgd")


def emit_stage8(ctx, blk, io):
    nc, E = ctx.nc, ctx.E
    feats_d, asel_d = io["feats"], io["asel"]
    s1_d, x1_d, out_d = io["s1"], io["x1"], io["out"]
    zsort_d, dest_d = io["zsort"], io["dest"]

    ft = [E(nc.sbuf_tensor(f"s8_ft{j}", [128, D], F32)) for j in range(2)]
    at = [E(nc.sbuf_tensor(f"s8_at{j}", [128, D], F32)) for j in range(2)]
    st_ = [E(nc.sbuf_tensor(f"s8_st{j}", [128, D], F32)) for j in range(2)]
    x1t = [E(nc.sbuf_tensor(f"s8_x1t{j}", [128, D], F32)) for j in range(2)]
    al = [E(nc.sbuf_tensor(f"s8_al{j}", [128, 1], F32)) for j in range(2)]
    dcol = [E(nc.sbuf_tensor(f"s8_dc{j}", [128, 1], mybir.dt.uint32))
            for j in range(2)]
    ot = [E(nc.sbuf_tensor(f"s8_ot{j}", [128, D], F32)) for j in range(2)]
    sld = ctx.make_sems("s8ld", 2)
    sgo = ctx.make_sems("s8go", 2)

    for i in range(NSEQ * NT):
        s, t = divmod(i, NT)
        j = i % 2

        @blk.sync
        def _(q):
            if i >= 2:
                ctx.wait(q, "so", io[("s8_out", j)])
            sl = slice(t * 128, (t + 1) * 128)
            ctx.inc(sld[j], q.dma_start(out=ft[j][:], in_=feats_d[s, sl, :]), 16)
            ctx.inc(sld[j], q.dma_start(out=st_[j][:], in_=s1_d[s, sl, :]), 16)
            ctx.inc(sld[j], q.dma_start(out=x1t[j][:], in_=x1_d[s, sl, :]), 16)
            ctx.inc(sld[j], q.dma_start(out=al[j][:], in_=asel_d[s, sl, :]), 16)
            ctx.inc(sld[j], q.dma_start(out=dcol[j][:], in_=dest_d[s, :, t:t + 1],
                                        allow_slow_non_contiguous=True), 16)
        ld = ctx.val(sld[j])

        @blk.vector
        def _(v):
            if i >= 2:
                ctx.wait(v, sgo[j], ctx.val(sgo[j]))
            c = v.memset(at[j][:], 0.0)
            io[("s8_z0", j)] = ctx.inc("sv", c)

        @blk.gpsimd
        def _(g):
            ctx.wait(g, "sv", io[("s8_z0", j)])
            ctx.wait(g, sld[j], ld)
            ctx.inc(sgo[j], g.indirect_dma_start(
                out=at[j][:], out_offset=None,
                in_=zsort_d.rearrange("n c d -> (n c) d"),
                in_offset=IndirectOffsetOnAxis(ap=dcol[j][:], axis=0),
                element_offset=s * 4096 * D,
                bounds_check=4095, oob_is_err=False), 16)

        @blk.vector
        def _(v):
            ctx.wait(v, sld[j], ld)
            ctx.wait(v, sgo[j], ctx.val(sgo[j]))
            c = ts(v, at[j][:], at[j][:], al[j][:], ALU.mult)
            ctx.wait(v, "sv", ctx.inc("sv", c))
            c = v.tensor_tensor(out=at[j][:], in0=at[j][:], in1=ft[j][:],
                                op=ALU.add)
            io["_s8_pre"] = ctx.inc("sv", c)

        @blk.scalar
        def _(sE):
            ctx.wait(sE, "sv", io["_s8_pre"])
            c = sE.activation(at[j][:], at[j][:], AF.Silu, bias=0.0, scale=1.0)
            io["_s8_silu"] = ctx.inc("ss", c)

        @blk.vector
        def _(v):
            ctx.wait(v, "ss", io["_s8_silu"])
            c = v.tensor_tensor(out=ot[j][:], in0=at[j][:], in1=st_[j][:],
                                op=ALU.mult)
            ctx.wait(v, "sv", ctx.inc("sv", c))
            c = v.tensor_tensor(out=ot[j][:], in0=ot[j][:], in1=x1t[j][:],
                                op=ALU.add)
            io[("s8_o", j)] = ctx.inc("sv", c)

        @blk.gpsimd
        def _(g):
            ctx.wait(g, "sv", io[("s8_o", j)])
            io[("s8_out", j)] = ctx.inc("so", g.dma_start(
                out=out_d[s, t * 128:(t + 1) * 128, :], in_=ot[j][:]), 16)

    @blk.sync
    def _(q):
        ctx.wait(q, "so", ctx.val("so"))



def _host_prep(inputs):
    f = {}
    f32 = np.float32
    perm = np.arange(1024).reshape(-1, 2)[:, ::-1].reshape(-1)
    for nm in ("Wq", "Wk"):
        W = np.ascontiguousarray(inputs[f"L0_attn_{nm}"].astype(f32))
        f[f"{nm}T"] = np.ascontiguousarray(W.T)
        f[f"{nm}Ts"] = np.ascontiguousarray(W.T[:, perm])
    for bn in ("bq", "bk"):
        b = inputs[f"L0_attn_{bn}"].astype(f32)
        f[f"{bn}_pc"] = np.ascontiguousarray(b.reshape(8, 128).T)
        f[f"{bn}s_pc"] = np.ascontiguousarray(b[perm].reshape(8, 128).T)
    f["WvT"] = np.ascontiguousarray(inputs["L0_attn_Wv"].astype(f32).T)
    f["bv_row"] = inputs["L0_attn_bv"].astype(f32).reshape(1, D)
    f["WoT"] = np.ascontiguousarray(inputs["L0_attn_Wo"].astype(f32).T)
    f["bo_row"] = inputs["L0_attn_bo"].astype(f32).reshape(1, D)
    f["ffn_w"] = np.ascontiguousarray(inputs["L0_ffn_weight"].astype(f32))
    f["ffn_b_row"] = inputs["L0_ffn_bias"].astype(f32).reshape(1, D)
    for nm in ("Wq", "Wk", "Wv", "Wo"):
        f[f"L1{nm}"] = np.ascontiguousarray(inputs[f"L1_attn_{nm}"].astype(f32))
    for bn in ("bq", "bk"):
        b = inputs[f"L1_attn_{bn}"].astype(f32)
        f[f"l1{bn}_pc"] = np.ascontiguousarray(b.reshape(8, 128).T)
    f["l1bv_row"] = inputs["L1_attn_bv"].astype(f32).reshape(1, D)
    f["l1bk_row"] = inputs["L1_attn_bk"].astype(f32).reshape(1, D)
    f["l1bo_row"] = inputs["L1_attn_bo"].astype(f32).reshape(1, D)
    f["l1gam_row"] = inputs["L1_attn_gamma"].astype(f32).reshape(1, D)
    f["l1bet_row"] = inputs["L1_attn_beta"].astype(f32).reshape(1, D)
    f["centT"] = np.ascontiguousarray(inputs["L1_ffn_centroids"].astype(f32).T)
    gam = inputs["L1_ffn_gamma"].astype(f32)
    bet = inputs["L1_ffn_beta"].astype(f32)
    ga_pc = np.empty((128, 64), f32)
    be_pc = np.empty((128, 64), f32)
    for e in range(8):
        ga_pc[:, e * 8:(e + 1) * 8] = gam[e].reshape(8, 128).T
        be_pc[:, e * 8:(e + 1) * 8] = bet[e].reshape(8, 128).T
    f["ga_pc"], f["be_pc"] = ga_pc, be_pc
    b1 = inputs["L1_ffn_b1"].astype(f32)
    b1_pc = np.empty((128, 8 * 32), f32)
    for e in range(8):
        b1_pc[:, e * 32:(e + 1) * 32] = b1[e].reshape(32, 128).T
    f["b1_pc"] = b1_pc
    f["b2_rows"] = np.ascontiguousarray(inputs["L1_ffn_b2"].astype(f32))
    f["w1"] = np.ascontiguousarray(inputs["L1_ffn_w1"].astype(f32))
    f["w2"] = np.ascontiguousarray(inputs["L1_ffn_w2"].astype(f32))
    f["ident_bf"] = np.eye(128, dtype=ml_dtypes.bfloat16)
    f["ident_f32"] = np.eye(128, dtype=np.float32)
    d = 64
    inv_freq = (1.0 / 10000.0 ** (np.arange(0, d, 2, dtype=f32) / d))
    pos = np.arange(S, dtype=f32)
    th = pos[None, :] * inv_freq[:, None]
    cos = np.cos(th).astype(f32)
    sin = np.sin(th).astype(f32)
    cos64 = np.repeat(cos, 2, axis=0)
    sin64 = np.empty((64, S), f32)
    sin64[0::2] = -sin
    sin64[1::2] = sin
    f["cos_ext"] = np.ascontiguousarray(np.concatenate([cos64, cos64], axis=0))
    f["sin_ext"] = np.ascontiguousarray(np.concatenate([sin64, sin64], axis=0))
    e16 = np.zeros((128, 128), f32)
    for c in range(8):
        for r_ in range(128):
            hh = 2 * c + (1 if r_ >= 64 else 0)
            e16[r_, c * 16 + hh] = 1.0
    f["e16"] = e16
    lam = 0.95
    q_ = np.arange(128)
    f["decay_mt"] = np.where(q_[None, :] - q_[:, None] >= 0,
                             lam ** (q_[None, :] - q_[:, None]), 0.0).astype(f32)
    f["wdec"] = (lam ** (127 - q_)).astype(f32).reshape(128, 1)
    f["lamq"] = (lam ** (q_ + 1)).astype(f32).reshape(128, 1)
    d2 = np.zeros((64, 128), f32)
    for i in range(128):
        d2[i % 64, i] = 1.0
    f["dup2"] = d2
    jj = np.arange(128)
    f["ltri"] = (jj[:, None] < jj[None, :]).astype(f32)
    f["er512"] = np.tile((512.0 * np.arange(8, dtype=f32))[None, :], (128, 1))
    f["slotfull"] = np.tile(np.arange(4096, dtype=f32)[None, :], (128, 1))
    return f


_IN_SPECS = [
    ("x", [NSEQ, S, D]), ("WqT", [D, D]), ("WqTs", [D, D]), ("WkT", [D, D]),
    ("WkTs", [D, D]), ("WvT", [D, D]), ("bq_pc", [128, 8]), ("bqs_pc", [128, 8]),
    ("bk_pc", [128, 8]), ("bks_pc", [128, 8]), ("bv_row", [1, D]),
    ("WoT", [D, D]), ("bo_row", [1, D]), ("ffn_w", [D, D]), ("ffn_b_row", [1, D]),
    ("L1Wq", [D, D]), ("L1Wk", [D, D]), ("L1Wv", [D, D]), ("L1Wo", [D, D]),
    ("l1bq_pc", [128, 8]), ("l1bk_pc", [128, 8]), ("l1bv_row", [1, D]),
    ("l1bk_row", [1, D]), ("l1bo_row", [1, D]), ("l1gam_row", [1, D]),
    ("l1bet_row", [1, D]), ("centT", [D, 8]), ("ga_pc", [128, 64]),
    ("be_pc", [128, 64]), ("b1_pc", [128, 256]), ("b2_rows", [8, D]),
    ("ident_f32", [128, 128]), ("cos_ext", [128, S]), ("sin_ext", [128, S]),
    ("e16", [128, 128]), ("decay_mt", [128, 128]), ("wdec", [128, 1]),
    ("lamq", [128, 1]), ("dup2", [64, 128]), ("ltri", [128, 128]),
    ("er512", [128, 8]), ("slotfull", [128, 4096]),
]
_IN_SPECS_U32 = []
_IN_SPECS_W1W2 = [("w1", [8, D, 4096]), ("w2", [8, 4096, D])]

_L0_INTER = [("wdT", [D, D], "BF16"), ("xnT", [NSEQ, D, S], "BF16"),
             ("xdT", [NSEQ, D, S], "BF16"), ("qT", [NSEQ, D, S], "BF16"),
             ("kT", [NSEQ, D, S], "BF16"), ("v", [NSEQ, S, D], "BF16"),
             ("oT", [NSEQ, D, S], "BF16"), ("x1", [NSEQ, S, D], "F32")]
_L1_INTER = [("xn1T", [NSEQ, D, S], "BF16"), ("xhatT", [NSEQ, D, S], "BF16"),
             ("feats", [NSEQ, S, D], "F32"), ("mask", [NSEQ, S, 8], "F32"),
             ("asel", [NSEQ, S, 1], "F32"), ("qTl", [NSEQ, D, S], "BF16"),
             ("kTl", [NSEQ, D, S], "BF16"), ("ktm", [NSEQ, S, D], "BF16"),
             ("vl", [NSEQ, S, D], "BF16"), ("oTl", [NSEQ, D, S], "BF16"),
             ("s1", [NSEQ, S, D], "F32"),
             ("xhat_tm", [NSEQ, S, D], "BF16"),
             ("xsort", [NSEQ, 4096, D], "BF16"),
             ("zsort", [NSEQ, 4096, D], "F32"),
             ("dest", [NSEQ, 128, NT], "U32"),
             ("destf", [NSEQ, 128, NT], "F32")]


_DT = {"F32": F32, "BF16": BF16, "U32": mybir.dt.uint32}


def _declare(nc, io, specs_in, inter, x1_kind=None):
    for nm, shape in specs_in:
        io[nm] = nc.dram_tensor(nm, shape, F32, kind="ExternalInput").ap()
    for nm, shape in _IN_SPECS_U32:
        io[nm] = nc.dram_tensor(nm, shape, mybir.dt.uint32,
                                kind="ExternalInput").ap()
    for nm, shape, dt in inter:
        k = "Internal"
        if x1_kind and nm == "x1":
            k = x1_kind
        io[nm] = nc.dram_tensor(nm, shape, _DT[dt], kind=k).ap()


def _preamble(ctx, blk, nc, io):
    ident_d = nc.dram_tensor("ident_bf", [128, 128], BF16,
                             kind="ExternalInput").ap()
    identb = ctx.outer_E(nc.sbuf_tensor("identb", [128, 128], BF16))

    @blk.sync
    def _(q):
        ctx.inc("sd", q.dma_start(out=identb[:], in_=ident_d), 16)
    ld = ctx.val("sd")

    @blk.vector
    def _(v):
        ctx.wait(v, "sd", ld)

    @blk.tensor
    def _(t):
        ctx.wait(t, "sd", ld)

    @blk.scalar
    def _(sE):
        ctx.wait(sE, "sd", ld)

    @blk.gpsimd
    def _(g):
        ctx.wait(g, "sd", ld)
    io["ident_bf"] = identb


def build_full():
    nc = bass.Bass("TRN2", target_bir_lowering=False)
    io = {}
    _declare(nc, io, _IN_SPECS + _IN_SPECS_W1W2, _L0_INTER + _L1_INTER)
    io["out"] = nc.dram_tensor("out", [NSEQ, S, D], F32,
                               kind="ExternalOutput").ap()
    with ExitStack() as stack:
        ctx = Ctx(nc, stack)
        blk = ctx.E(nc.Block())
        _preamble(ctx, blk, nc, io)
        for em in (emit_stage0w, emit_stage1, emit_stage2, emit_stage3,
                   emit_stage4, emit_stage5, emit_stage6a, emit_stage6b,
                   emit_stage6c, emit_stage7, emit_stage8):
            with ctx.stage():
                em(ctx, blk, io)
            ctx.barrier(blk)
    return nc


def build_l0():
    nc = bass.Bass("TRN2", target_bir_lowering=False)
    io = {}
    _declare(nc, io, _IN_SPECS, _L0_INTER, x1_kind="ExternalOutput")
    with ExitStack() as stack:
        ctx = Ctx(nc, stack)
        blk = ctx.E(nc.Block())
        _preamble(ctx, blk, nc, io)
        for em in (emit_stage0w, emit_stage1, emit_stage2, emit_stage3,
                   emit_stage4):
            with ctx.stage():
                em(ctx, blk, io)
            ctx.barrier(blk)
        @blk.sync
        def _(q):
            for nm in ("s4x10", "s4x11"):
                ctx.wait(q, nm, ctx.val(nm))
    return nc


def build_l1():
    nc = bass.Bass("TRN2", target_bir_lowering=False)
    io = {}
    specs = [s for s in _IN_SPECS if s[0] not in
             ("WqT", "WqTs", "WkT", "WkTs", "WvT", "bq_pc", "bqs_pc", "bk_pc",
              "bks_pc", "bv_row", "WoT", "bo_row", "ffn_w", "ffn_b_row",
              "cos_ext", "sin_ext", "e16", "x")]
    specs = [("x1", [NSEQ, S, D])] + specs
    _declare(nc, io, specs + _IN_SPECS_W1W2, _L1_INTER)
    io["out"] = nc.dram_tensor("out", [NSEQ, S, D], F32,
                               kind="ExternalOutput").ap()
    with ExitStack() as stack:
        ctx = Ctx(nc, stack)
        blk = ctx.E(nc.Block())
        _preamble(ctx, blk, nc, io)
        for em in (emit_stage5, emit_stage6a, emit_stage6b, emit_stage6c,
                   emit_stage7, emit_stage8):
            with ctx.stage():
                em(ctx, blk, io)
            ctx.barrier(blk)
    return nc


FUSED = True


def kernel(**inputs):
    f = _host_prep(inputs)
    x = np.ascontiguousarray(inputs["L0_attn_x"].astype(np.float32))
    ncores = 8
    if FUSED:
        nc = build_full()
        in_maps = []
        for ci in range(ncores):
            m = dict(f)
            m["x"] = np.ascontiguousarray(x[2 * ci:2 * ci + 2])
            in_maps.append(m)
        res = run_bass_kernel_spmd(nc, in_maps, core_ids=list(range(ncores)))
        out = np.concatenate([np.asarray(r["out"]) for r in res.results], axis=0)
        return out
    nc0 = build_l0()
    keys0 = {s[0] for s in _IN_SPECS} | {"ident_bf"}
    in_maps = []
    for ci in range(ncores):
        m = {k: v for k, v in f.items() if k in keys0}
        m["x"] = np.ascontiguousarray(x[2 * ci:2 * ci + 2])
        in_maps.append(m)
    res0 = run_bass_kernel_spmd(nc0, in_maps, core_ids=list(range(ncores)))
    nc1 = build_l1()
    keys1 = {"L1Wq", "L1Wk", "L1Wv", "L1Wo", "l1bq_pc", "l1bk_pc", "l1bv_row",
             "l1bk_row", "l1bo_row", "l1gam_row", "l1bet_row", "centT",
             "ga_pc", "be_pc", "b1_pc", "b2_rows", "w1", "w2", "ident_f32",
             "decay_mt", "wdec", "lamq", "dup2", "ident_bf", "ltri", "er512",
             "slotfull"}
    in_maps = []
    for ci in range(ncores):
        m = {k: v for k, v in f.items() if k in keys1}
        m["x1"] = np.asarray(res0.results[ci]["x1"])
        in_maps.append(m)
    res1 = run_bass_kernel_spmd(nc1, in_maps, core_ids=list(range(ncores)))
    return np.concatenate([np.asarray(r["out"]) for r in res1.results], axis=0)


# version 10 (final)
# speedup vs baseline: 1.6517x; 1.1865x over previous
import math
from contextlib import ExitStack, contextmanager
import numpy as np
import ml_dtypes
import concourse.bass as bass
import concourse.mybir as mybir
from concourse.bass import AP, IndirectOffsetOnAxis
from concourse.bass_utils import run_bass_kernel_spmd

F32 = mybir.dt.float32
BF16 = mybir.dt.bfloat16
I32 = mybir.dt.int32
AF = mybir.ActivationFunctionType
ALU = mybir.AluOpType
AX = mybir.AxisListType

NSEQ = 2
S = 2048
D = 1024
NT = S // 128
NCH = D // 128


class Ctx:
    def __init__(self, nc, stack):
        self.nc = nc
        self.outer_E = stack.enter_context
        self.E = stack.enter_context
        self.counts = {}
        self.sems = {}
        for name in ("sd", "sgd", "sv", "st", "ss", "sg", "so"):
            self.sems[name] = self.E(nc.semaphore(name))
            self.counts[name] = 0

    def inc(self, name, instr, amt=1):
        instr.then_inc(self.sems[name], amt)
        self.counts[name] += amt
        return self.counts[name]

    def val(self, name):
        return self.counts[name]

    def wait(self, eng, name, val):
        if val > 0:
            eng.wait_ge(self.sems[name], val)

    def barrier(self, blk):
        snap = {k: v for k, v in self.counts.items() if v > 0}
        for eng_name in ("sync", "scalar", "vector", "tensor", "gpsimd"):
            sec = getattr(blk, eng_name)

            def _mk(engine):
                for k, v in snap.items():
                    engine.wait_ge(self.sems[k], v)
            sec(_mk)

    def make_sems(self, prefix, n):
        for k in range(n):
            name = f"{prefix}{k}"
            self.sems[name] = self.outer_E(self.nc.semaphore(name))
            self.counts[name] = 0
        return [f"{prefix}{k}" for k in range(n)]

    @contextmanager
    def stage(self):
        st = ExitStack()
        old = self.E
        self.E = st.enter_context
        try:
            yield
        finally:
            st.close()
            self.E = old


def ts(eng, out, in0, s1, op0, s2=None, op1=ALU.bypass, accum_out=None):
    return eng.tensor_scalar(out=out, in0=in0, scalar1=s1, scalar2=s2,
                             op0=op0, op1=op1, accum_out=accum_out)


def bcast32(tile_ap):
    return AP(tile_ap.tensor, tile_ap.offset,
              [list(tile_ap.ap[0]), list(tile_ap.ap[1]), [0, 32]])


def view32(ap2d):
    return ap2d.rearrange("p (b e) -> p b e", e=32)



def emit_stage1(ctx, blk, io):
    nc, E = ctx.nc, ctx.E
    x_in, xnT, xdT, ident_bf = io["x"], io["xnT"], io["xdT"], io["ident_bf"]

    NBUF = 4
    xa = [E(nc.sbuf_tensor(f"s1_xa{i}", [128, 1024], F32)) for i in range(NBUF)]
    t0 = E(nc.sbuf_tensor("s1_t0", [128, 1024], F32))
    t1 = E(nc.sbuf_tensor("s1_t1", [128, 1024], F32))
    t2 = E(nc.sbuf_tensor("s1_t2", [128, 1024], F32))
    t3 = E(nc.sbuf_tensor("s1_t3", [128, 1024], F32))
    t5 = E(nc.sbuf_tensor("s1_t5", [128, 1024], F32))
    xn = E(nc.sbuf_tensor("s1_xn", [128, 1024], F32))
    sqdump = E(nc.sbuf_tensor("s1_sqdump", [128, 1024], F32))
    xnb = [E(nc.sbuf_tensor(f"s1_xnb{i}", [128, 1024], BF16)) for i in range(2)]
    xdb = [E(nc.sbuf_tensor(f"s1_xdb{i}", [128, 1024], BF16)) for i in range(2)]
    sm = E(nc.sbuf_tensor("s1_sm", [128, 32], F32))
    sc = E(nc.sbuf_tensor("s1_sc", [128, 32], F32))
    rsc = E(nc.sbuf_tensor("s1_rsc", [128, 32], F32))
    stat = E(nc.sbuf_tensor("s1_stat", [128, 4], F32))
    rstd = E(nc.sbuf_tensor("s1_rstd", [128, 1], F32))
    epsb = E(nc.sbuf_tensor("s1_epsb", [128, 1], F32))
    asn = E(nc.sbuf_tensor("s1_asn", [128, NCH * S], BF16))
    asd = E(nc.sbuf_tensor("s1_asd", [128, NCH * S], BF16))
    ps_n = E(nc.psum_tensor("s1_psn", [128, 1024], BF16))
    ps_d = E(nc.psum_tensor("s1_psd", [128, 1024], BF16))

    tile_done = {}
    evac_done = {}
    trn_done = {}
    trd_done = {}
    sx = ctx.make_sems("s1x", NBUF)

    @blk.vector
    def _(v):
        v.memset(epsb[:], 1e-5)

    def emit_load(i):
        s, t = divmod(i, NT)

        @blk.sync
        def _(q):
            if i >= NBUF:
                ctx.wait(q, "sv", tile_done[i - NBUF])
            ctx.inc(sx[i % NBUF], q.dma_start(
                out=xa[i % NBUF][:],
                in_=x_in[s, t * 128:(t + 1) * 128, :]), 16)

    for i in range(NBUF):
        emit_load(i)

    for i in range(NSEQ * NT):
        s, t = divmod(i, NT)
        xt = xa[i % NBUF]

        @blk.vector
        def _(v):
            ctx.wait(v, sx[i % NBUF], ctx.val(sx[i % NBUF]))
            c = v.tensor_reduce(out=stat[:, 0:1], in_=xt[:], axis=AX.X, op=ALU.add)
            ctx.wait(v, "sv", ctx.inc("sv", c))
            c = ts(v, stat[:, 1:2], stat[:, 0:1], 1.0 / D, ALU.mult)
            ctx.wait(v, "sv", ctx.inc("sv", c))
            c = ts(v, t0[:], xt[:], stat[:, 1:2], ALU.subtract)
            io["_m_xc"] = ctx.inc("sv", c)

        @blk.scalar
        def _(sE):
            ctx.wait(sE, "sv", io["_m_xc"])
            c = sE.activation(sqdump[:], t0[:], AF.Square, bias=0.0, scale=1.0,
                              accum_out=stat[:, 2:3])
            io["_m_ssq"] = ctx.inc("ss", c)

        @blk.vector
        def _(v):
            ctx.wait(v, "ss", io["_m_ssq"])
            c = ts(v, stat[:, 3:4], stat[:, 2:3], 1.0 / D, ALU.mult)
            io["_m_var"] = ctx.inc("sv", c)

        @blk.scalar
        def _(sE):
            ctx.wait(sE, "sv", io["_m_var"])
            c = sE.activation(rstd[:], stat[:, 3:4], AF.Sqrt, bias=epsb[:], scale=1.0)
            io["_m_std"] = ctx.inc("ss", c)

        @blk.vector
        def _(v):
            ctx.wait(v, "ss", io["_m_std"])
            c = v.reciprocal(rstd[:], rstd[:])
            ctx.wait(v, "sv", ctx.inc("sv", c))
            ts(v, xn[:], t0[:], rstd[:], ALU.mult)
            if i >= 2:
                ctx.wait(v, "st", trn_done[i - 2])
            c = ts(v, xnb[i % 2][:], t0[:], rstd[:], ALU.mult)
            io["_m_xnb"] = ctx.inc("sv", c)

        @blk.tensor
        def _(tE):
            ctx.wait(tE, "sv", io["_m_xnb"])
            if i >= 1:
                ctx.wait(tE, "sv", evac_done.get(i - 1, 0))
            for p in range(NCH):
                ins = tE.transpose(ps_n[:, p * 128:(p + 1) * 128],
                                   xnb[i % 2][:, p * 128:(p + 1) * 128],
                                   ident_bf[:])
            trn_done[i] = ctx.inc("st", ins)

        @blk.vector
        def _(v):
            c = v.tensor_reduce(out=sm[:], in_=view32(xn[:]), axis=AX.X,
                                op=ALU.max, apply_absolute_value=True)
            ctx.wait(v, "sv", ctx.inc("sv", c))
            c = ts(v, sc[:], sm[:], 1.0 / 7.5, ALU.mult, 1e-30, ALU.max)
            ctx.wait(v, "sv", ctx.inc("sv", c))
            c = v.reciprocal(rsc[:], sc[:])
            ctx.wait(v, "sv", ctx.inc("sv", c))
            v.tensor_tensor(out=view32(t1[:]), in0=view32(xn[:]),
                            in1=bcast32(rsc[:]), op=ALU.mult)
            v.scalar_tensor_tensor(out=t2[:], in0=t1[:], scalar=-1.0,
                                   in1=t1[:], op0=ALU.mult, op1=ALU.max)
            ts(v, t3[:], t2[:], 2.0, ALU.is_ge, -4.0, ALU.mult)
            ts(v, t1[:], t2[:], 4.0, ALU.is_ge, -2.0, ALU.mult)
            v.scalar_tensor_tensor(out=t3[:], in0=t3[:], scalar=8.0,
                                   in1=t1[:], op0=ALU.add, op1=ALU.add)
            v.tensor_tensor(out=t1[:], in0=t2[:], in1=t3[:], op=ALU.mult)
            ts(v, t2[:], t1[:], 15.5, ALU.is_ge, 16.0, ALU.mult)
            v.tensor_tensor(out=t5[:], in0=t1[:], in1=t2[:], op=ALU.subtract)
            ts(v, t2[:], t5[:], 7.5, ALU.is_ge, 8.0, ALU.mult)
            v.tensor_tensor(out=t5[:], in0=t5[:], in1=t2[:], op=ALU.subtract)
            ts(v, t2[:], t5[:], 3.5, ALU.is_ge, 4.0, ALU.mult)
            v.tensor_tensor(out=t5[:], in0=t5[:], in1=t2[:], op=ALU.subtract)
            ts(v, t2[:], t5[:], 1.5, ALU.is_ge, 2.0, ALU.mult)
            v.tensor_tensor(out=t5[:], in0=t5[:], in1=t2[:], op=ALU.subtract)
            ts(v, t2[:], t5[:], 0.5, ALU.is_ge, 1.0, ALU.mult)
            v.tensor_tensor(out=t5[:], in0=t5[:], in1=t2[:], op=ALU.subtract)
            v.tensor_tensor(out=t1[:], in0=t1[:], in1=t5[:], op=ALU.subtract)
            v.reciprocal(t2[:], t3[:])
            v.tensor_tensor(out=t1[:], in0=t1[:], in1=t2[:], op=ALU.mult)
            ts(v, t2[:], xn[:], 0.0, ALU.is_lt, -2.0, ALU.mult)
            ts(v, t2[:], t2[:], 1.0, ALU.add)
            v.tensor_tensor(out=t1[:], in0=t1[:], in1=t2[:], op=ALU.mult)
            if i >= 2:
                ctx.wait(v, "st", trd_done[i - 2])
            c = v.tensor_tensor(out=view32(xdb[i % 2][:]), in0=view32(t1[:]),
                                in1=bcast32(sc[:]), op=ALU.mult)
            io["_m_xdb"] = ctx.inc("sv", c)

        @blk.tensor
        def _(tE):
            ctx.wait(tE, "sv", io["_m_xdb"])
            for p in range(NCH):
                ins = tE.transpose(ps_d[:, p * 128:(p + 1) * 128],
                                   xdb[i % 2][:, p * 128:(p + 1) * 128],
                                   ident_bf[:])
            trd_done[i] = ctx.inc("st", ins)

        @blk.vector
        def _(v):
            dstn = asn[:].rearrange("p (c q) -> p c q", q=S)[:, :, t * 128:(t + 1) * 128]
            dstd = asd[:].rearrange("p (c q) -> p c q", q=S)[:, :, t * 128:(t + 1) * 128]
            srcn = ps_n[:].rearrange("p (c q) -> p c q", q=128)
            srcd = ps_d[:].rearrange("p (c q) -> p c q", q=128)
            ctx.wait(v, "st", trn_done[i])
            v.tensor_copy(dstn, srcn)
            ctx.wait(v, "st", trd_done[i])
            c = v.tensor_copy(dstd, srcd)
            evac_done[i] = ctx.inc("sv", c)
            tile_done[i] = evac_done[i]

        if i + NBUF < NSEQ * NT:
            emit_load(i + NBUF)

        if t == NT - 1:
            @blk.gpsimd
            def _(g):
                ctx.wait(g, "sv", tile_done[i])
                for p in range(NCH):
                    ctx.inc("sgd", g.dma_start(
                        out=xnT[s, p * 128:(p + 1) * 128, :],
                        in_=asn[:, p * S:(p + 1) * S]), 16)
                    ctx.inc("sgd", g.dma_start(
                        out=xdT[s, p * 128:(p + 1) * 128, :],
                        in_=asd[:, p * S:(p + 1) * S]), 16)
            io["_m_s1_store"] = ctx.val("sgd")
            if s == 0:
                @blk.vector
                def _(v):
                    ctx.wait(v, "sgd", io["_m_s1_store"])

    io["s1_done_sgd"] = ctx.val("sgd")



def host_constants():
    c = {}
    c["ident_bf"] = np.eye(128, dtype=ml_dtypes.bfloat16)
    c["ident_f32"] = np.eye(128, dtype=np.float32)
    c["ones_row"] = np.ones((1, 128), dtype=np.float32)
    d = 64
    inv_freq = (1.0 / 10000.0 ** (np.arange(0, d, 2, dtype=np.float32) / d))
    pos = np.arange(S, dtype=np.float32)
    th = pos[None, :] * inv_freq[:, None]
    cos = np.cos(th).astype(np.float32)
    sin = np.sin(th).astype(np.float32)
    cos64 = np.repeat(cos, 2, axis=0)
    sin64 = np.empty((64, S), dtype=np.float32)
    sin64[0::2] = -sin
    sin64[1::2] = sin
    c["cos_ext"] = np.concatenate([cos64, cos64], axis=0)
    c["sin_ext"] = np.concatenate([sin64, sin64], axis=0)
    return c



def emit_stage2(ctx, blk, io):
    nc, E = ctx.nc, ctx.E
    xnT_d = io["xnT"]
    qT_d, kT_d, v_d = io["qT"], io["kT"], io["v"]
    wq_d, wqs_d = io["WqT"], io["WqTs"]
    wk_d, wks_d = io["WkT"], io["WkTs"]
    wv_d = io["WvT"]
    bq_d, bqs_d = io["bq_pc"], io["bqs_pc"]
    bk_d, bks_d = io["bk_pc"], io["bks_pc"]
    bv_d = io["bv_row"]
    cos_d, sin_d = io["cos_ext"], io["sin_ext"]

    xnT_sb = E(nc.sbuf_tensor("s2_xnT", [128, NCH * S], BF16))
    wsb = {}
    for nm in ("wq", "wqs", "wk", "wks", "wv"):
        wsb[nm] = E(nc.sbuf_tensor(f"s2_{nm}", [128, NCH * D], BF16))
    cos_sb = E(nc.sbuf_tensor("s2_cos", [128, S], F32))
    sin_sb = E(nc.sbuf_tensor("s2_sin", [128, S], F32))
    bqs_sb = E(nc.sbuf_tensor("s2_bq", [128, NCH], F32))
    bqss_sb = E(nc.sbuf_tensor("s2_bqs", [128, NCH], F32))
    bks_sb = E(nc.sbuf_tensor("s2_bk", [128, NCH], F32))
    bkss_sb = E(nc.sbuf_tensor("s2_bks", [128, NCH], F32))
    bv_sb = E(nc.sbuf_tensor("s2_bv", [1, D], BF16))
    ones_bf = E(nc.sbuf_tensor("s2_ones1", [1, 128], BF16))
    tq = E(nc.sbuf_tensor("s2_tq", [128, 512], F32))
    tq2 = E(nc.sbuf_tensor("s2_tq2", [128, 512], F32))
    cbuf = {"q": [E(nc.sbuf_tensor(f"s2_cq{j}", [128, S], BF16)) for j in range(2)],
            "k": [E(nc.sbuf_tensor(f"s2_ck{j}", [128, S], BF16)) for j in range(2)]}
    v_sb = E(nc.sbuf_tensor("s2_v", [128, NT * D], BF16))
    ps_a = E(nc.psum_tensor("s2_psa", [128, 512], F32))
    ps_b = E(nc.psum_tensor("s2_psb", [128, 512], F32))
    ps_v = [E(nc.psum_tensor(f"s2_psv{j}", [128, 512], F32)) for j in range(2)]

    sxl = ctx.make_sems("s2xl", 1)[0]
    scb = ctx.make_sems("s2cb", 4)

    @blk.gpsimd
    def _(g):
        for nm, d_ in (("wq", wq_d), ("wqs", wqs_d), ("wk", wk_d),
                       ("wks", wks_d), ("wv", wv_d)):
            src3 = d_.rearrange("(c p) o -> p c o", p=128)
            dst3 = wsb[nm][:].rearrange("p (c o) -> p c o", o=D)
            ctx.inc("sgd", g.dma_start(out=dst3, in_=src3), 16)
        ctx.inc("sgd", g.dma_start(out=bv_sb[:], in_=bv_d), 16)
    w_loaded = ctx.val("sgd")

    @blk.sync
    def _(q):
        ctx.inc("sd", q.dma_start(out=cos_sb[:], in_=cos_d), 16)
        ctx.inc("sd", q.dma_start(out=sin_sb[:], in_=sin_d), 16)
        for bd, bs_ in ((bq_d, bqs_sb), (bqs_d, bqss_sb), (bk_d, bks_sb),
                        (bks_d, bkss_sb)):
            ctx.inc("sd", q.dma_start(out=bs_[:], in_=bd), 16)
    misc_loaded = ctx.val("sd")

    @blk.vector
    def _(v):
        c0 = v.memset(ones_bf[:], 1.0)
        io["_s2_ones"] = ctx.inc("sv", c0)

    for s in range(NSEQ):
        @blk.sync
        def _(q):
            if s > 0:
                ctx.wait(q, "st", io["_s2_vmm"])
            src3 = xnT_d[s].rearrange("(c p) t -> p c t", p=128)
            dst3 = xnT_sb[:].rearrange("p (c t) -> p c t", t=S)
            ctx.inc(sxl, q.dma_start(out=dst3, in_=src3), 16)
        xn_ready = ctx.val(sxl)

        for kind, wa, wb_, ba, bb_, dram in (
                ("q", "wq", "wqs", bqs_sb, bqss_sb, qT_d),
                ("k", "wk", "wks", bks_sb, bkss_sb, kT_d)):
            for c in range(NCH):
                slot = c % 2
                for g_ in range(4):
                    @blk.tensor
                    def _(t):
                        if s == 0 and kind == "q" and c == 0 and g_ == 0:
                            ctx.wait(t, "sgd", w_loaded)
                            ctx.wait(t, "sd", misc_loaded)
                        ctx.wait(t, sxl, xn_ready)
                        if not (c == 0 and g_ == 0):
                            ctx.wait(t, "sv", io["_s2_evac"])
                        elif s > 0 or kind == "k":
                            ctx.wait(t, "sv", io.get("_s2_evac", 0))
                        for kc in range(NCH):
                            t.matmul(ps_a[:],
                                     wsb[wa][:, kc * D + c * 128:kc * D + (c + 1) * 128],
                                     xnT_sb[:, kc * S + g_ * 512:kc * S + g_ * 512 + 512],
                                     start=(kc == 0), stop=(kc == NCH - 1))
                        for kc in range(NCH):
                            ins = t.matmul(ps_b[:],
                                     wsb[wb_][:, kc * D + c * 128:kc * D + (c + 1) * 128],
                                     xnT_sb[:, kc * S + g_ * 512:kc * S + g_ * 512 + 512],
                                     start=(kc == 0), stop=(kc == NCH - 1))
                        io["_s2_mm"] = ctx.inc("st", ins)
                        io["_s2_lastmm"] = io["_s2_mm"]

                    @blk.vector
                    def _(v):
                        ctx.wait(v, "st", io["_s2_mm"])
                        if g_ == 0:
                            ctx.wait(v, scb[(0 if kind == "q" else 2) + slot],
                                     ctx.val(scb[(0 if kind == "q" else 2) + slot]))
                        cs = cos_sb[:, g_ * 512:(g_ + 1) * 512]
                        sn = sin_sb[:, g_ * 512:(g_ + 1) * 512]
                        v.scalar_tensor_tensor(out=tq[:], in0=ps_a[:],
                                               scalar=ba[:, c:c + 1], in1=cs,
                                               op0=ALU.add, op1=ALU.mult)
                        v.scalar_tensor_tensor(out=tq2[:], in0=ps_b[:],
                                               scalar=bb_[:, c:c + 1], in1=sn,
                                               op0=ALU.add, op1=ALU.mult)
                        c2 = v.tensor_tensor(
                            out=cbuf[kind][slot][:, g_ * 512:(g_ + 1) * 512],
                            in0=tq[:], in1=tq2[:], op=ALU.add)
                        io["_s2_evac"] = ctx.inc("sv", c2)

                @blk.gpsimd
                def _(g):
                    ctx.wait(g, "sv", io["_s2_evac"])
                    ctx.inc(scb[(0 if kind == "q" else 2) + slot],
                            g.dma_start(out=dram[s, c * 128:(c + 1) * 128, :],
                                        in_=cbuf[kind][slot][:]), 16)

        for tt in range(NT):
            for hf in range(2):
                j = (tt * 2 + hf) % 2
                @blk.tensor
                def _(t):
                    if s == 0 and tt == 0 and hf == 0:
                        ctx.wait(t, "sv", io["_s2_ones"])
                    if tt * 2 + hf >= 2:
                        ctx.wait(t, "sv", io["_s2_vevac"][j])
                    for kc in range(NCH):
                        t.matmul(ps_v[j][:],
                                 xnT_sb[:, kc * S + tt * 128:kc * S + (tt + 1) * 128],
                                 wsb["wv"][:, kc * D + hf * 512:kc * D + (hf + 1) * 512],
                                 start=(kc == 0), stop=False)
                    ins = t.matmul(ps_v[j][:], ones_bf[:],
                                   bv_sb[:, hf * 512:(hf + 1) * 512],
                                   start=False, stop=True)
                    io["_s2_vmm"] = ctx.inc("st", ins)

                @blk.vector
                def _(vE):
                    ctx.wait(vE, "st", io["_s2_vmm"])
                    c2 = vE.tensor_copy(
                        v_sb[:, tt * D + hf * 512:tt * D + (hf + 1) * 512],
                        ps_v[j][:])
                    if "_s2_vevac" not in io or tt * 2 + hf < 2:
                        io.setdefault("_s2_vevac", {})
                    io["_s2_vevac"][j] = ctx.inc("sv", c2)

        @blk.gpsimd
        def _(g):
            ctx.wait(g, "sv", io["_s2_vevac"][0])
            ctx.wait(g, "sv", io["_s2_vevac"][1])
            dst3 = v_d[s].rearrange("(tt p) f -> p tt f", p=128)
            src3 = v_sb[:].rearrange("p (tt f) -> p tt f", f=D)
            ctx.inc("sgd", g.dma_start(out=dst3, in_=src3), 16)
        io["_s2_vstore"] = ctx.val("sgd")
        if s == 0:
            @blk.vector
            def _(vE):
                ctx.wait(vE, "sgd", io["_s2_vstore"])

    io["s2_done_sgd"] = ctx.val("sgd")



def emit_stage3(ctx, blk, io):
    nc, E = ctx.nc, ctx.E
    qT_d, kT_d, v_d, oT_d = io["qT"], io["kT"], io["v"], io["oT"]
    e16_d = io["e16"]
    identf_d = io["ident_f32"]

    qT_sb = E(nc.sbuf_tensor("s3_qT", [128, NCH * S], BF16))
    kT_sb = E(nc.sbuf_tensor("s3_kT", [128, NCH * S], BF16))
    v_sb = E(nc.sbuf_tensor("s3_v", [128, NT * D], BF16))
    v65 = [E(nc.sbuf_tensor(f"s3_v65_{j}", [128, NT * 65], BF16)) for j in range(2)]
    pT = [E(nc.sbuf_tensor(f"s3_pT{j}", [128, NT * 512], BF16)) for j in range(2)]
    sq_scr = E(nc.sbuf_tensor("s3_sq", [128, NCH * 512], F32))
    e16_sb = E(nc.sbuf_tensor("s3_e16", [128, NCH * 16], F32))
    identf = E(nc.sbuf_tensor("s3_idf", [128, 128], F32))
    onesf = E(nc.sbuf_tensor("s3_onesf", [1, 128], F32))
    bnd = E(nc.sbuf_tensor("s3_bnd", [16, 8], F32))
    nrow = E(nc.sbuf_tensor("s3_nrow", [1, 16], F32))
    bias16 = E(nc.sbuf_tensor("s3_bias16", [128, 16], F32))
    l_sb = E(nc.sbuf_tensor("s3_l", [1, 512], F32))
    linv = E(nc.sbuf_tensor("s3_linv", [1, 512], F32))
    lexp = E(nc.sbuf_tensor("s3_lexp", [64, 512], F32))
    ot_stage = [E(nc.sbuf_tensor(f"s3_ot{j}", [64, 512], BF16)) for j in range(2)]

    ps_s = [E(nc.psum_tensor(f"s3_pss{j}", [128, 512], F32)) for j in range(4)]
    ps_o = [E(nc.psum_tensor(f"s3_pso{j}", [128, 512], F32)) for j in range(2)]
    ps_l = E(nc.psum_tensor("s3_psl", [64, 512], F32))
    ps_n1 = ps_o[1][0:16, 0:512]
    ps_r = ps_o[0][0:128, 0:16]

    sq_ld = ctx.make_sems("s3ld", 3)
    s_ot = ctx.make_sems("s3ot", 2)

    @blk.sync
    def _(q):
        ctx.inc("sd", q.dma_start(out=e16_sb[:], in_=e16_d), 16)
        ctx.inc("sd", q.dma_start(out=identf[:], in_=identf_d), 16)
    misc_ld = ctx.val("sd")

    @blk.vector
    def _(v):
        c = v.memset(onesf[:], 1.0)
        io["_s3_ones"] = ctx.inc("sv", c)

    exp_done = {}
    av_done = {}
    tt_done = {}

    for s in range(NSEQ):
        @blk.sync
        def _(q):
            if s > 0:
                ctx.wait(q, "sv", io["_s3_lastTT"])
            src_q = qT_d[s].rearrange("(c p) t -> p c t", p=128)
            ctx.inc(sq_ld[0], q.dma_start(
                out=qT_sb[:].rearrange("p (c t) -> p c t", t=S), in_=src_q), 16)
            src_k = kT_d[s].rearrange("(c p) t -> p c t", p=128)
            ctx.inc(sq_ld[1], q.dma_start(
                out=kT_sb[:].rearrange("p (c t) -> p c t", t=S), in_=src_k), 16)
            src_v = v_d[s].rearrange("(tt p) f -> p tt f", p=128)
            ctx.inc(sq_ld[2], q.dma_start(
                out=v_sb[:].rearrange("p (tt f) -> p tt f", f=D), in_=src_v), 16)
        qld, kld, vld = ctx.val(sq_ld[0]), ctx.val(sq_ld[1]), ctx.val(sq_ld[2])

        for pi, (src_sb, ldsem, ldval, col) in enumerate(
                ((qT_sb, sq_ld[0], qld, 2), (kT_sb, sq_ld[1], kld, 3))):
            @blk.vector
            def _(v):
                c2 = v.memset(bnd[:, 0:1], 0.0)
                io["_s3_bacc"] = ctx.inc("sv", c2)

            for g_ in range(4):
                for c in range(NCH):
                    @blk.scalar
                    def _(sE):
                        ctx.wait(sE, ldsem, ldval)
                        if s > 0 and pi == 0 and g_ == 0 and c == 0:
                            ctx.wait(sE, "sv", io["_s3_lastTT"])
                        c2 = sE.activation(
                            sq_scr[:, c * 512:(c + 1) * 512],
                            src_sb[:, c * S + g_ * 512:c * S + (g_ + 1) * 512],
                            AF.Square, bias=0.0, scale=1.0)
                        io["_s3_sq"] = ctx.inc("ss", c2)

                    @blk.tensor
                    def _(t):
                        ctx.wait(t, "ss", io["_s3_sq"])
                        ctx.wait(t, "sd", misc_ld)
                        if c == 0 and "_s3_red" in io:
                            ctx.wait(t, "sv", io["_s3_red"])
                        c2 = t.matmul(ps_n1, e16_sb[:, c * 16:(c + 1) * 16],
                                      sq_scr[:, c * 512:(c + 1) * 512],
                                      start=(c == 0), stop=(c == NCH - 1))
                        if c == NCH - 1:
                            io["_s3_nmm"] = ctx.inc("st", c2)

                @blk.vector
                def _(v):
                    ctx.wait(v, "st", io["_s3_nmm"])
                    if g_ == 0:
                        ctx.wait(v, "sv", io["_s3_bacc"])
                    c2 = v.tensor_reduce(out=bnd[:, 1:2], in_=ps_n1, axis=AX.X,
                                         op=ALU.max)
                    ctx.wait(v, "sv", ctx.inc("sv", c2))
                    c2 = v.tensor_tensor(out=bnd[:, 0:1], in0=bnd[:, 0:1],
                                         in1=bnd[:, 1:2], op=ALU.max)
                    ctx.wait(v, "sv", ctx.inc("sv", c2))
                    io["_s3_red"] = ctx.val("sv")

            @blk.scalar
            def _(sE):
                ctx.wait(sE, "sv", io["_s3_red"])
                c2 = sE.activation(bnd[:, col:col + 1], bnd[:, 0:1], AF.Sqrt,
                                   bias=0.0, scale=1.0)
                io["_s3_sqrt"] = ctx.inc("ss", c2)

        @blk.vector
        def _(v):
            ctx.wait(v, "ss", io["_s3_sqrt"])
            c2 = v.tensor_tensor(out=bnd[:, 4:5], in0=bnd[:, 2:3],
                                 in1=bnd[:, 3:4], op=ALU.mult)
            ctx.wait(v, "sv", ctx.inc("sv", c2))
            c2 = ts(v, bnd[:, 4:5], bnd[:, 4:5], -0.125, ALU.mult)
            io["_s3_negc"] = ctx.inc("sv", c2)

        @blk.tensor
        def _(t):
            ctx.wait(t, "sv", io["_s3_negc"])
            c2 = t.matmul(ps_r[0:1, 0:16], bnd[0:16, 4:5], identf[0:16, 0:16],
                          start=True, stop=True)
            io["_s3_rowmm"] = ctx.inc("st", c2)

        @blk.vector
        def _(v):
            ctx.wait(v, "st", io["_s3_rowmm"])
            c2 = v.tensor_copy(nrow[:], ps_r[0:1, 0:16])
            io["_s3_nrowc"] = ctx.inc("sv", c2)

        @blk.tensor
        def _(t):
            ctx.wait(t, "sv", io["_s3_nrowc"])
            c2 = t.matmul(ps_r, onesf[:], nrow[:], start=True, stop=True)
            io["_s3_bmm"] = ctx.inc("st", c2)

        @blk.vector
        def _(v):
            ctx.wait(v, "st", io["_s3_bmm"])
            ctx.wait(v, "sv", io["_s3_ones"])
            c2 = v.tensor_copy(bias16[:], ps_r)
            io["_s3_bias"] = ctx.inc("sv", c2)

        for h in range(16):
            hb = h % 2
            ch = h // 2
            po = (h % 2) * 64
            @blk.vector
            def _(v):
                ctx.wait(v, sq_ld[2], vld)
                if (s, h - 2, 3) in av_done:
                    ctx.wait(v, "st", av_done[(s, h - 2, 3)])
                src = v_sb[:].rearrange("p (tt f) -> p tt f", f=D)[
                    :, :, h * 64:(h + 1) * 64]
                dst = v65[hb][:].rearrange("p (tt f) -> p tt f", f=65)[:, :, 0:64]
                v.tensor_copy(dst, src)
                onescol = v65[hb][:].rearrange("p (tt f) -> p tt f", f=65)[
                    :, :, 64:65]
                c2 = v.memset(onescol, 1.0)
                io["_s3_v65"] = ctx.inc("sv", c2)
            v65_ready = io["_s3_v65"]

            for g_ in range(4):
                u = (s, h, g_)
                up = (h * 4 + g_) % 2
                for g4 in range(4):
                    @blk.tensor
                    def _(t):
                        if g4 == 0 and g_ == 0 and h == 0:
                            ctx.wait(t, sq_ld[0], qld)
                            ctx.wait(t, sq_ld[1], kld)
                            ctx.wait(t, "sv", io["_s3_bias"])
                        for kc in range(4 * g4, 4 * g4 + 4):
                            bank = kc % 4
                            key = ("expb", bank)
                            if key in io:
                                ctx.wait(t, "ss", io[key])
                            c2 = t.matmul(
                                ps_s[bank][:],
                                kT_sb[po:po + 64, ch * S + kc * 128:ch * S + (kc + 1) * 128],
                                qT_sb[po:po + 64, ch * S + g_ * 512:ch * S + (g_ + 1) * 512],
                                start=True, stop=True)
                            io[("mm", kc)] = ctx.inc("st", c2)

                    @blk.scalar
                    def _(sE):
                        if g4 == 0 and g_ == 0 and h == 0:
                            ctx.wait(sE, "sv", io["_s3_bias"])
                        if g4 == 0:
                            key = ("avd", up)
                            if key in io:
                                ctx.wait(sE, "st", io[key])
                        for kc in range(4 * g4, 4 * g4 + 4):
                            ctx.wait(sE, "st", io[("mm", kc)])
                            c2 = sE.activation(pT[up][:, kc * 512:(kc + 1) * 512],
                                               ps_s[kc % 4][:], AF.Exp,
                                               bias=bias16[:, h:h + 1], scale=0.125)
                            io[("expb", kc % 4)] = ctx.inc("ss", c2)
                            io[("exp", kc)] = io[("expb", kc % 4)]

                ob = (h * 4 + g_) % 2
                @blk.tensor
                def _(t):
                    ctx.wait(t, "sv", v65_ready)
                    key = ("ttd", ob)
                    if key in io:
                        ctx.wait(t, "sv", io[key])
                    for kc in range(NT):
                        ctx.wait(t, "ss", io[("exp", kc)])
                        c2 = t.matmul(ps_o[ob][0:65, :],
                                      v65[hb][:, kc * 65:(kc + 1) * 65],
                                      pT[up][:, kc * 512:(kc + 1) * 512],
                                      start=(kc == 0), stop=(kc == NT - 1))
                    io[("avd", up)] = ctx.inc("st", c2)
                    av_done[(s, h, g_)] = io[("avd", up)]

                @blk.vector
                def _(v):
                    ctx.wait(v, "st", io[("avd", up)])
                    c2 = v.tensor_copy(l_sb[:], ps_o[ob][64:65, :])
                    ctx.wait(v, "sv", ctx.inc("sv", c2))
                    c2 = v.reciprocal(linv[:], l_sb[:])
                    io["_s3_linv"] = ctx.inc("sv", c2)

                @blk.tensor
                def _(t):
                    ctx.wait(t, "sv", io["_s3_linv"])
                    c2 = t.matmul(ps_l[:], onesf[:, 0:64], linv[:],
                                  start=True, stop=True)
                    io["_s3_lexp"] = ctx.inc("st", c2)

                @blk.vector
                def _(v):
                    ctx.wait(v, "st", io["_s3_lexp"])
                    v.tensor_copy(lexp[:], ps_l[:])
                    jj = (h * 4 + g_) % 2
                    ctx.wait(v, s_ot[jj], ctx.val(s_ot[jj]))
                    c2 = v.tensor_tensor(out=ot_stage[jj][:], in0=ps_o[ob][0:64, :],
                                         in1=lexp[:], op=ALU.mult)
                    io[("ttd", ob)] = ctx.inc("sv", c2)
                    io["_s3_lastTT"] = io[("ttd", ob)]
                    tt_done[u] = io[("ttd", ob)]

                @blk.gpsimd
                def _(g):
                    jj = (h * 4 + g_) % 2
                    ctx.wait(g, "sv", tt_done[u])
                    ctx.inc(s_ot[jj], g.dma_start(
                        out=oT_d[s, h * 64:(h + 1) * 64, g_ * 512:(g_ + 1) * 512],
                        in_=ot_stage[jj][:]), 16)

    io["s3_done_sgd"] = ctx.val("sgd")


def emit_quant_chain(ctx, v, src_ap, dst_ap, q):
    t1, t2, t3, t5 = q["t1"], q["t2"], q["t3"], q["t5"]
    sm, sc, rsc = q["sm"], q["sc"], q["rsc"]
    c = v.tensor_reduce(out=sm[:], in_=src_ap.rearrange("p (b e) -> p b e", e=32),
                        axis=AX.X, op=ALU.max, apply_absolute_value=True)
    ctx.wait(v, "sv", ctx.inc("sv", c))
    c = ts(v, sc[:], sm[:], 1.0 / 7.5, ALU.mult, 1e-30, ALU.max)
    ctx.wait(v, "sv", ctx.inc("sv", c))
    c = v.reciprocal(rsc[:], sc[:])
    ctx.wait(v, "sv", ctx.inc("sv", c))
    v.tensor_tensor(out=view32(t1[:]), in0=src_ap.rearrange("p (b e) -> p b e", e=32),
                    in1=bcast32(rsc[:]), op=ALU.mult)
    v.scalar_tensor_tensor(out=t2[:], in0=t1[:], scalar=-1.0,
                           in1=t1[:], op0=ALU.mult, op1=ALU.max)
    ts(v, t3[:], t2[:], 2.0, ALU.is_ge, -4.0, ALU.mult)
    ts(v, t1[:], t2[:], 4.0, ALU.is_ge, -2.0, ALU.mult)
    v.scalar_tensor_tensor(out=t3[:], in0=t3[:], scalar=8.0,
                           in1=t1[:], op0=ALU.add, op1=ALU.add)
    v.tensor_tensor(out=t1[:], in0=t2[:], in1=t3[:], op=ALU.mult)
    ts(v, t2[:], t1[:], 15.5, ALU.is_ge, 16.0, ALU.mult)
    v.tensor_tensor(out=t5[:], in0=t1[:], in1=t2[:], op=ALU.subtract)
    ts(v, t2[:], t5[:], 7.5, ALU.is_ge, 8.0, ALU.mult)
    v.tensor_tensor(out=t5[:], in0=t5[:], in1=t2[:], op=ALU.subtract)
    ts(v, t2[:], t5[:], 3.5, ALU.is_ge, 4.0, ALU.mult)
    v.tensor_tensor(out=t5[:], in0=t5[:], in1=t2[:], op=ALU.subtract)
    ts(v, t2[:], t5[:], 1.5, ALU.is_ge, 2.0, ALU.mult)
    v.tensor_tensor(out=t5[:], in0=t5[:], in1=t2[:], op=ALU.subtract)
    ts(v, t2[:], t5[:], 0.5, ALU.is_ge, 1.0, ALU.mult)
    v.tensor_tensor(out=t5[:], in0=t5[:], in1=t2[:], op=ALU.subtract)
    v.tensor_tensor(out=t1[:], in0=t1[:], in1=t5[:], op=ALU.subtract)
    v.reciprocal(t2[:], t3[:])
    v.tensor_tensor(out=t1[:], in0=t1[:], in1=t2[:], op=ALU.mult)
    ts(v, t2[:], src_ap, 0.0, ALU.is_lt, -2.0, ALU.mult)
    ts(v, t2[:], t2[:], 1.0, ALU.add)
    v.tensor_tensor(out=t1[:], in0=t1[:], in1=t2[:], op=ALU.mult)
    c = v.tensor_tensor(out=dst_ap.rearrange("p (b e) -> p b e", e=32),
                        in0=view32(t1[:]), in1=bcast32(sc[:]), op=ALU.mult)
    return c



def emit_stage0w(ctx, blk, io):
    nc, E = ctx.nc, ctx.E
    w_d, wdT_d, ident_bf = io["ffn_w"], io["wdT"], io["ident_bf"]
    import contextlib
    with contextlib.ExitStack() as st0:
        E0 = st0.enter_context
        wa = [E0(nc.sbuf_tensor(f"s0_wa{j}", [128, D], F32)) for j in range(2)]
        wq = E0(nc.sbuf_tensor("s0_wq", [128, D], BF16))
        q = {k: E0(nc.sbuf_tensor(f"s0_{k}", [128, D], F32))
             for k in ("t1", "t2", "t3", "t5")}
        q["sm"] = E0(nc.sbuf_tensor("s0_sm", [128, 32], F32))
        q["sc"] = E0(nc.sbuf_tensor("s0_sc", [128, 32], F32))
        q["rsc"] = E0(nc.sbuf_tensor("s0_rsc", [128, 32], F32))
        asm = E0(nc.sbuf_tensor("s0_asm", [128, NCH * D], BF16))
        ps = E0(nc.psum_tensor("s0_ps", [128, D], BF16))
        sw = ctx.make_sems("s0w", 2)

        for r in range(NCH):
            @blk.sync
            def _(qq):
                if r >= 2:
                    ctx.wait(qq, "sv", io[("s0_qd", r - 2)])
                ctx.inc(sw[r % 2], qq.dma_start(
                    out=wa[r % 2][:], in_=w_d[r * 128:(r + 1) * 128, :]), 16)

            @blk.vector
            def _(v):
                ctx.wait(v, sw[r % 2], ctx.val(sw[r % 2]))
                if r >= 1:
                    ctx.wait(v, "st", io[("s0_tr", r - 1)])
                c = emit_quant_chain(ctx, v, wa[r % 2][:], wq[:], q)
                io[("s0_qd", r)] = ctx.inc("sv", c)

            @blk.tensor
            def _(t):
                ctx.wait(t, "sv", io[("s0_qd", r)])
                if r >= 1:
                    ctx.wait(t, "sv", io[("s0_ev", r - 1)])
                for c_ in range(NCH):
                    ins = t.transpose(ps[:, c_ * 128:(c_ + 1) * 128],
                                      wq[:, c_ * 128:(c_ + 1) * 128], ident_bf[:])
                io[("s0_tr", r)] = ctx.inc("st", ins)

            @blk.vector
            def _(v):
                ctx.wait(v, "st", io[("s0_tr", r)])
                dst = asm[:].rearrange("p (c o) -> p c o", o=D)[:, :, r * 128:(r + 1) * 128]
                src = ps[:].rearrange("p (c o) -> p c o", o=128)
                c = v.tensor_copy(dst, src)
                io[("s0_ev", r)] = ctx.inc("sv", c)

        @blk.gpsimd
        def _(g):
            ctx.wait(g, "sv", io[("s0_ev", NCH - 1)])
            src3 = asm[:].rearrange("p (c o) -> p c o", o=D)
            dst3 = wdT_d.rearrange("(c p) o -> p c o", p=128)
            ctx.inc("sgd", g.dma_start(out=dst3, in_=src3), 16)
        io["s0_done_sgd"] = ctx.val("sgd")



def emit_stage4(ctx, blk, io):
    nc, E = ctx.nc, ctx.E
    xdT_d, oT_d, x_d, x1_d = io["xdT"], io["oT"], io["x"], io["x1"]
    wdT_d, woT_d = io["wdT"], io["WoT"]
    fb_row_d, bo_row_d = io["ffn_b_row"], io["bo_row"]

    xdT_sb = E(nc.sbuf_tensor("s4_xdT", [128, NCH * S], BF16))
    oT_sb = E(nc.sbuf_tensor("s4_oT", [128, NCH * S], BF16))
    wd_sb = E(nc.sbuf_tensor("s4_wd", [128, NCH * D], BF16))
    wo_sb = E(nc.sbuf_tensor("s4_wo", [128, NCH * D], BF16))
    fb_row = E(nc.sbuf_tensor("s4_fbr", [1, D], F32))
    bo_row = E(nc.sbuf_tensor("s4_bor", [1, D], F32))
    onesf = E(nc.sbuf_tensor("s4_ones", [1, 128], F32))
    fb_exp = E(nc.sbuf_tensor("s4_fbe", [128, D], F32))
    bo_exp = E(nc.sbuf_tensor("s4_boe", [128, D], F32))
    xa = [E(nc.sbuf_tensor(f"s4_xa{j}", [128, D], F32)) for j in range(2)]
    gt = E(nc.sbuf_tensor("s4_gt", [128, D], F32))
    st_t = E(nc.sbuf_tensor("s4_st", [128, D], F32))
    x1t = [E(nc.sbuf_tensor(f"s4_x1t{j}", [128, D], F32)) for j in range(2)]
    ps_f = [E(nc.psum_tensor(f"s4_psf{j}", [128, 512], F32)) for j in range(2)]
    ps_s = [E(nc.psum_tensor(f"s4_pss{j}", [128, 512], F32)) for j in range(2)]
    ps_b = E(nc.psum_tensor("s4_psb", [128, D], F32))

    sxl = ctx.make_sems("s4xl", 2)
    sxa = ctx.make_sems("s4xa", 2)
    sx1 = ctx.make_sems("s4x1", 2)

    @blk.gpsimd
    def _(g):
        for d_, sb in ((wdT_d, wd_sb), (woT_d, wo_sb)):
            src3 = d_.rearrange("(c p) o -> p c o", p=128)
            dst3 = sb[:].rearrange("p (c o) -> p c o", o=D)
            ctx.inc("sgd", g.dma_start(out=dst3, in_=src3), 16)
    w_ld = ctx.val("sgd")

    @blk.sync
    def _(q):
        ctx.inc("sd", q.dma_start(out=fb_row[:], in_=fb_row_d), 16)
        ctx.inc("sd", q.dma_start(out=bo_row[:], in_=bo_row_d), 16)
    b_ld = ctx.val("sd")

    @blk.vector
    def _(v):
        c = v.memset(onesf[:], 1.0)
        io["_s4_ones"] = ctx.inc("sv", c)

    @blk.tensor
    def _(t):
        ctx.wait(t, "sd", b_ld)
        ctx.wait(t, "sv", io["_s4_ones"])
        for half in range(2):
            c = t.matmul(ps_b[:, half * 512:(half + 1) * 512], onesf[:],
                         fb_row[:, half * 512:(half + 1) * 512],
                         start=True, stop=True)
        io["_s4_fbmm"] = ctx.inc("st", c)

    @blk.vector
    def _(v):
        ctx.wait(v, "st", io["_s4_fbmm"])
        c = v.tensor_copy(fb_exp[:], ps_b[:])
        io["_s4_fbe"] = ctx.inc("sv", c)

    @blk.tensor
    def _(t):
        ctx.wait(t, "sv", io["_s4_fbe"])
        for half in range(2):
            c = t.matmul(ps_b[:, half * 512:(half + 1) * 512], onesf[:],
                         bo_row[:, half * 512:(half + 1) * 512],
                         start=True, stop=True)
        io["_s4_bomm"] = ctx.inc("st", c)

    @blk.vector
    def _(v):
        ctx.wait(v, "st", io["_s4_bomm"])
        c = v.tensor_copy(bo_exp[:], ps_b[:])
        io["_s4_boe"] = ctx.inc("sv", c)

    for s in range(NSEQ):
        @blk.sync
        def _(q):
            if s > 0:
                ctx.wait(q, "st", io["_s4_lastmm"])
            ctx.inc(sxl[0], q.dma_start(
                out=xdT_sb[:].rearrange("p (c t) -> p c t", t=S),
                in_=xdT_d[s].rearrange("(c p) t -> p c t", p=128)), 16)
            ctx.inc(sxl[1], q.dma_start(
                out=oT_sb[:].rearrange("p (c t) -> p c t", t=S),
                in_=oT_d[s].rearrange("(c p) t -> p c t", p=128)), 16)
        xd_ld, oT_ld = ctx.val(sxl[0]), ctx.val(sxl[1])

        for tt in range(NT):
            j = tt % 2

            @blk.sync
            def _(q):
                if s * NT + tt >= 2:
                    ctx.wait(q, "sv", io[("s4_xc", j)])
                ctx.inc(sxa[j], q.dma_start(
                    out=xa[j][:], in_=x_d[s, tt * 128:(tt + 1) * 128, :]), 16)
            xa_ld = ctx.val(sxa[j])

            for half in range(2):
                @blk.tensor
                def _(t):
                    if s == 0 and tt == 0 and half == 0:
                        ctx.wait(t, "sgd", w_ld)
                    ctx.wait(t, sxl[0], xd_ld)
                    if ("s4_fev", j) in io:
                        ctx.wait(t, "sv", io[("s4_fev", j)])
                    for kc in range(NCH):
                        c = t.matmul(
                            ps_f[j][:],
                            xdT_sb[:, kc * S + tt * 128:kc * S + (tt + 1) * 128],
                            wd_sb[:, kc * D + half * 512:kc * D + (half + 1) * 512],
                            start=(kc == 0), stop=(kc == NCH - 1))
                    io[("s4_fmm", j, half)] = ctx.inc("st", c)

                @blk.vector
                def _(v):
                    ctx.wait(v, "st", io[("s4_fmm", j, half)])
                    if s == 0 and tt == 0 and half == 0:
                        ctx.wait(v, "sv", io["_s4_fbe"])
                    if half == 0 and "s4_prev_xc" in io:
                        ctx.wait(v, "sv", io["s4_prev_xc"])
                    c = v.tensor_tensor(out=gt[:, half * 512:(half + 1) * 512],
                                        in0=ps_f[j][:],
                                        in1=fb_exp[:, half * 512:(half + 1) * 512],
                                        op=ALU.add)
                    io[("s4_fev", j)] = ctx.inc("sv", c)

            @blk.vector
            def _(v):
                ctx.wait(v, "sv", io[("s4_fev", j)])
                io["_s4_gadd"] = ctx.val("sv")

            @blk.scalar
            def _(sE):
                ctx.wait(sE, "sv", io["_s4_gadd"])
                c = sE.activation(gt[:], gt[:], AF.Silu, bias=0.0, scale=1.0)
                io["_s4_silu"] = ctx.inc("ss", c)

            for half in range(2):
                @blk.tensor
                def _(t):
                    ctx.wait(t, sxl[1], oT_ld)
                    if ("s4_sev", j) in io:
                        ctx.wait(t, "sv", io[("s4_sev", j)])
                    for kc in range(NCH):
                        c = t.matmul(
                            ps_s[j][:],
                            oT_sb[:, kc * S + tt * 128:kc * S + (tt + 1) * 128],
                            wo_sb[:, kc * D + half * 512:kc * D + (half + 1) * 512],
                            start=(kc == 0), stop=(kc == NCH - 1))
                    io[("s4_smm", j, half)] = ctx.inc("st", c)
                    io["_s4_lastmm"] = ctx.val("st")

                @blk.vector
                def _(v):
                    ctx.wait(v, "st", io[("s4_smm", j, half)])
                    if s == 0 and tt == 0 and half == 0:
                        ctx.wait(v, "sv", io["_s4_boe"])
                    c = v.tensor_tensor(out=st_t[:, half * 512:(half + 1) * 512],
                                        in0=ps_s[j][:],
                                        in1=bo_exp[:, half * 512:(half + 1) * 512],
                                        op=ALU.add)
                    io[("s4_sev", j)] = ctx.inc("sv", c)

            @blk.vector
            def _(v):
                ctx.wait(v, "ss", io["_s4_silu"])
                v.tensor_tensor(out=st_t[:], in0=st_t[:], in1=gt[:], op=ALU.mult)
                ctx.wait(v, sxa[j], xa_ld)
                ctx.wait(v, sx1[j], ctx.val(sx1[j]))
                c = v.tensor_tensor(out=x1t[j][:], in0=st_t[:], in1=xa[j][:],
                                    op=ALU.add)
                io[("s4_xc", j)] = ctx.inc("sv", c)
                io["s4_prev_xc"] = io[("s4_xc", j)]

            @blk.gpsimd
            def _(g):
                ctx.wait(g, "sv", io[("s4_xc", j)])
                ctx.inc(sx1[j], g.dma_start(
                    out=x1_d[s, tt * 128:(tt + 1) * 128, :], in_=x1t[j][:]), 16)

    io["s4_done_sgd"] = ctx.val("sgd")



def emit_stage5(ctx, blk, io):
    nc, E = ctx.nc, ctx.E
    x1_d = io["x1"]
    xn1T_d, xhatT_d = io["xn1T"], io["xhatT"]
    feats_d, mask_d, asel_d = io["feats"], io["mask"], io["asel"]
    centT_d = io["centT"]
    ident_bf, identf_d = io["ident_bf"], io["ident_f32"]
    xhat_tm_d = io["xhat_tm"]
    dest_d = io["dest"]
    destf_d = io["destf"]
    ltri_d = io["ltri"]
    er512_d = io["er512"]

    NBUF = 4
    xa = [E(nc.sbuf_tensor(f"s5_xa{i}", [128, D], F32)) for i in range(NBUF)]
    xn1 = [E(nc.sbuf_tensor(f"s5_xn1_{j}", [128, D], F32)) for j in range(2)]
    t0 = E(nc.sbuf_tensor("s5_t0", [128, D], F32))
    sqdump = E(nc.sbuf_tensor("s5_sqd", [128, D], F32))
    xn1b = [E(nc.sbuf_tensor(f"s5_xn1b{j}", [128, D], BF16)) for j in range(2)]
    xhb = [E(nc.sbuf_tensor(f"s5_xhb{j}", [128, D], BF16)) for j in range(2)]
    xf32T = E(nc.sbuf_tensor("s5_xf32T", [128, D], F32))
    stat = E(nc.sbuf_tensor("s5_stat", [128, 4], F32))
    rstd = E(nc.sbuf_tensor("s5_rstd", [128, 1], F32))
    epsb = E(nc.sbuf_tensor("s5_epsb", [128, 1], F32))
    centT = E(nc.sbuf_tensor("s5_centT", [128, NCH * 8], F32))
    rmax = E(nc.sbuf_tensor("s5_rmax", [128, 1], F32))
    maskt = [E(nc.sbuf_tensor(f"s5_mask{j}", [128, 8], F32)) for j in range(2)]
    alphat = E(nc.sbuf_tensor("s5_alpha", [128, 8], F32))
    aselt = [E(nc.sbuf_tensor(f"s5_asel{j}", [128, 1], F32)) for j in range(2)]
    identf = E(nc.sbuf_tensor("s5_idf", [128, 128], F32))
    ltri = E(nc.sbuf_tensor("s5_ltri", [128, 128], F32))
    er512 = E(nc.sbuf_tensor("s5_er512", [128, 8], F32))
    onesfr = E(nc.sbuf_tensor("s5_onesfr", [1, 128], F32))
    onescl = E(nc.sbuf_tensor("s5_onescl", [128, 1], F32))
    runrow = E(nc.sbuf_tensor("s5_runrow", [1, 8], F32))
    dst0 = E(nc.sbuf_tensor("s5_dst0", [128, 1], F32))
    osel = E(nc.sbuf_tensor("s5_osel", [128, 1], F32))
    rq = E(nc.sbuf_tensor("s5_rq", [128, 8], F32))
    desti = [E(nc.sbuf_tensor(f"s5_desti{j}", [128, 1], mybir.dt.uint32))
             for j in range(2)]
    asn1 = E(nc.sbuf_tensor("s5_asn1", [128, NCH * S], BF16))
    asx = E(nc.sbuf_tensor("s5_asx", [128, NCH * S], BF16))
    ps_n = E(nc.psum_tensor("s5_psn", [128, D], BF16))
    ps_h = E(nc.psum_tensor("s5_psh", [128, D], BF16))
    ps_f = E(nc.psum_tensor("s5_psf", [128, D], F32))
    ps_aff = E(nc.psum_tensor("s5_aff", [128, 8], F32))
    ps_off = E(nc.psum_tensor("s5_off", [128, 8], F32))
    ps_cs = E(nc.psum_tensor("s5_cs", [1, 8], F32))

    sx = ctx.make_sems("s5x", NBUF)
    sfo = ctx.make_sems("s5fo", 2)
    smo = ctx.make_sems("s5mo", 2)
    sho = ctx.make_sems("s5ho", 2)
    sso = ctx.make_sems("s5so", 2)

    @blk.vector
    def _(v):
        v.memset(epsb[:], 1e-5)

    @blk.sync
    def _(q):
        ctx.inc("sd", q.dma_start(out=identf[:], in_=identf_d), 16)
        src3 = centT_d.rearrange("(c p) e -> p c e", p=128)
        dst3 = centT[:].rearrange("p (c e) -> p c e", e=8)
        ctx.inc("sd", q.dma_start(out=dst3, in_=src3), 16)
        ctx.inc("sd", q.dma_start(out=ltri[:], in_=ltri_d), 16)
        ctx.inc("sd", q.dma_start(out=er512[:], in_=er512_d), 16)
    misc_ld = ctx.val("sd")

    @blk.vector
    def _(v):
        v.memset(onesfr[:], 1.0)
        v.memset(onescl[:], 1.0)
        c = v.memset(runrow[:], 0.0)
        io["_s5_rr"] = ctx.inc("sv", c)

    tile_done = {}
    trn_done = {}
    trh_done = {}
    trf_done = {}
    evac_done = {}

    def emit_load(i):
        s, t = divmod(i, NT)

        @blk.sync
        def _(q):
            if i >= NBUF:
                ctx.wait(q, "sv", tile_done[i - NBUF])
            ctx.inc(sx[i % NBUF], q.dma_start(
                out=xa[i % NBUF][:], in_=x1_d[s, t * 128:(t + 1) * 128, :]), 16)

    for i in range(NBUF):
        emit_load(i)

    for i in range(NSEQ * NT):
        s, t = divmod(i, NT)
        xt = xa[i % NBUF]
        j = i % 2

        @blk.vector
        def _(v):
            ctx.wait(v, sx[i % NBUF], ctx.val(sx[i % NBUF]))
            c = v.tensor_reduce(out=stat[:, 0:1], in_=xt[:], axis=AX.X, op=ALU.add)
            ctx.wait(v, "sv", ctx.inc("sv", c))
            c = ts(v, stat[:, 1:2], stat[:, 0:1], 1.0 / D, ALU.mult)
            ctx.wait(v, "sv", ctx.inc("sv", c))
            c = ts(v, t0[:], xt[:], stat[:, 1:2], ALU.subtract)
            io["_m5_xc"] = ctx.inc("sv", c)

        @blk.scalar
        def _(sE):
            ctx.wait(sE, "sv", io["_m5_xc"])
            c = sE.activation(sqdump[:], t0[:], AF.Square, bias=0.0, scale=1.0,
                              accum_out=stat[:, 2:3])
            io["_m5_ssq"] = ctx.inc("ss", c)

        @blk.vector
        def _(v):
            ctx.wait(v, "ss", io["_m5_ssq"])
            c = ts(v, stat[:, 3:4], stat[:, 2:3], 1.0 / D, ALU.mult)
            io["_m5_var"] = ctx.inc("sv", c)

        @blk.scalar
        def _(sE):
            ctx.wait(sE, "sv", io["_m5_var"])
            c = sE.activation(rstd[:], stat[:, 3:4], AF.Sqrt, bias=epsb[:], scale=1.0)
            io["_m5_std"] = ctx.inc("ss", c)

        @blk.vector
        def _(v):
            ctx.wait(v, "ss", io["_m5_std"])
            c = v.reciprocal(rstd[:], rstd[:])
            ctx.wait(v, "sv", ctx.inc("sv", c))
            if i >= 2:
                ctx.wait(v, sfo[j], ctx.val(sfo[j]))
                ctx.wait(v, "st", trf_done[i - 2])
            ts(v, xn1[j][:], t0[:], rstd[:], ALU.mult)
            if i >= 2:
                ctx.wait(v, "st", trn_done[i - 2])
            c = ts(v, xn1b[j][:], t0[:], rstd[:], ALU.mult)
            io["_m5_xn1b"] = ctx.inc("sv", c)

        @blk.vector
        def _(v):
            c = v.tensor_reduce(out=stat[:, 0:1], in_=xn1[j][:], axis=AX.X,
                                op=ALU.add)
            ctx.wait(v, "sv", ctx.inc("sv", c))
            c = ts(v, stat[:, 1:2], stat[:, 0:1], 1.0 / D, ALU.mult)
            ctx.wait(v, "sv", ctx.inc("sv", c))
            c = ts(v, t0[:], xn1[j][:], stat[:, 1:2], ALU.subtract)
            io["_m5_xc2"] = ctx.inc("sv", c)

        @blk.scalar
        def _(sE):
            ctx.wait(sE, "sv", io["_m5_xc2"])
            c = sE.activation(sqdump[:], t0[:], AF.Square, bias=0.0, scale=1.0,
                              accum_out=stat[:, 2:3])
            io["_m5_ssq2"] = ctx.inc("ss", c)

        @blk.vector
        def _(v):
            ctx.wait(v, "ss", io["_m5_ssq2"])
            c = ts(v, stat[:, 3:4], stat[:, 2:3], 1.0 / D, ALU.mult)
            io["_m5_var2"] = ctx.inc("sv", c)

        @blk.scalar
        def _(sE):
            ctx.wait(sE, "sv", io["_m5_var2"])
            c = sE.activation(rstd[:], stat[:, 3:4], AF.Sqrt, bias=epsb[:], scale=1.0)
            io["_m5_std2"] = ctx.inc("ss", c)

        @blk.vector
        def _(v):
            ctx.wait(v, "ss", io["_m5_std2"])
            c = v.reciprocal(rstd[:], rstd[:])
            ctx.wait(v, "sv", ctx.inc("sv", c))
            if i >= 2:
                ctx.wait(v, "st", trh_done[i - 2])
                ctx.wait(v, sho[j], ctx.val(sho[j]))
            c = ts(v, xhb[j][:], t0[:], rstd[:], ALU.mult)
            io["_m5_xhb"] = ctx.inc("sv", c)

        @blk.tensor
        def _(tE):
            ctx.wait(tE, "sv", io["_m5_xn1b"])
            if i >= 1:
                ctx.wait(tE, "sv", evac_done.get(i - 1, 0))
            for p in range(NCH):
                ins = tE.transpose(ps_n[:, p * 128:(p + 1) * 128],
                                   xn1b[j][:, p * 128:(p + 1) * 128], ident_bf[:])
            trn_done[i] = ctx.inc("st", ins)
            ctx.wait(tE, "sv", io["_m5_xhb"])
            for p in range(NCH):
                ins = tE.transpose(ps_h[:, p * 128:(p + 1) * 128],
                                   xhb[j][:, p * 128:(p + 1) * 128], ident_bf[:])
            trh_done[i] = ctx.inc("st", ins)
            ctx.wait(tE, "sd", misc_ld)
            for p in range(NCH):
                ins = tE.transpose(ps_f[:, p * 128:(p + 1) * 128],
                                   xn1[j][:, p * 128:(p + 1) * 128], identf[:])
            trf_done[i] = ctx.inc("st", ins)

        @blk.vector
        def _(v):
            dstn = asn1[:].rearrange("p (c q) -> p c q", q=S)[:, :, t * 128:(t + 1) * 128]
            dsth = asx[:].rearrange("p (c q) -> p c q", q=S)[:, :, t * 128:(t + 1) * 128]
            ctx.wait(v, "st", trn_done[i])
            v.tensor_copy(dstn, ps_n[:].rearrange("p (c q) -> p c q", q=128))
            ctx.wait(v, "st", trh_done[i])
            v.tensor_copy(dsth, ps_h[:].rearrange("p (c q) -> p c q", q=128))
            ctx.wait(v, "st", trf_done[i])
            c = v.tensor_copy(xf32T[:], ps_f[:])
            evac_done[i] = ctx.inc("sv", c)

        @blk.tensor
        def _(tE):
            ctx.wait(tE, "sv", evac_done[i])
            if ("s5_affr",) in io:
                ctx.wait(tE, "ss", io[("s5_affr",)])
            for kc in range(NCH):
                c = tE.matmul(ps_aff[:], xf32T[:, kc * 128:(kc + 1) * 128],
                              centT[:, kc * 8:(kc + 1) * 8],
                              start=(kc == 0), stop=(kc == NCH - 1))
            io["_m5_aff"] = ctx.inc("st", c)

        @blk.vector
        def _(v):
            ctx.wait(v, "st", io["_m5_aff"])
            c = v.tensor_reduce(out=rmax[:], in_=ps_aff[:], axis=AX.X, op=ALU.max)
            ctx.wait(v, "sv", ctx.inc("sv", c))
            ctx.wait(v, smo[j], ctx.val(smo[j]))
            if "_s5_csmm" in io:
                ctx.wait(v, "st", io["_s5_csmm"])
            c = ts(v, maskt[j][:], ps_aff[:], rmax[:], ALU.is_equal)
            io["_m5_mask"] = ctx.inc("sv", c)

        @blk.scalar
        def _(sE):
            ctx.wait(sE, "sv", io["_m5_mask"])
            c = sE.activation(alphat[:], ps_aff[:], AF.Sigmoid, bias=0.0, scale=1.0)
            io[("s5_affr",)] = ctx.inc("ss", c)

        @blk.vector
        def _(v):
            ctx.wait(v, "ss", io[("s5_affr",)])
            c = v.tensor_tensor(out=alphat[:], in0=alphat[:], in1=maskt[j][:],
                                op=ALU.mult)
            ctx.wait(v, "sv", ctx.inc("sv", c))
            c = v.tensor_reduce(out=aselt[j][:], in_=alphat[:], axis=AX.X,
                                op=ALU.add)
            tile_done[i] = ctx.inc("sv", c)
            io["_m5_tile"] = tile_done[i]

        if t == 0:
            @blk.vector
            def _(v):
                if "_s5_offmm" in io:
                    ctx.wait(v, "st", io["_s5_offmm"])
                c = v.memset(runrow[:], 0.0)
                io["_s5_rrup"] = ctx.inc("sv", c)

        @blk.tensor
        def _(tE):
            ctx.wait(tE, "sv", io["_m5_mask"])
            ctx.wait(tE, "sv", io["_s5_rrup"])
            if "_s5_dest" in io:
                ctx.wait(tE, "sv", io["_s5_dest"])
            tE.matmul(ps_off[:], ltri[:], maskt[j][:], start=True, stop=False)
            c = tE.matmul(ps_off[:], onesfr[:], runrow[:], start=False, stop=True)
            io["_s5_offmm"] = ctx.inc("st", c)
            c = tE.matmul(ps_cs[:], onescl[:], maskt[j][:], start=True, stop=True)
            io["_s5_csmm"] = ctx.inc("st", c)

        @blk.vector
        def _(v):
            ctx.wait(v, "st", io["_s5_offmm"])
            c = v.tensor_tensor(out=rq[:], in0=ps_off[:], in1=er512[:], op=ALU.add)
            ctx.wait(v, "sv", ctx.inc("sv", c))
            c = v.tensor_tensor(out=rq[:], in0=rq[:], in1=maskt[j][:], op=ALU.mult)
            ctx.wait(v, "sv", ctx.inc("sv", c))
            c = v.tensor_reduce(out=dst0[:], in_=rq[:], axis=AX.X, op=ALU.add)
            ctx.wait(v, "sv", ctx.inc("sv", c))
            c = v.tensor_tensor(out=rq[:], in0=ps_off[:], in1=maskt[j][:], op=ALU.mult)
            ctx.wait(v, "sv", ctx.inc("sv", c))
            c = v.tensor_reduce(out=osel[:], in_=rq[:], axis=AX.X, op=ALU.add)
            ctx.wait(v, "sv", ctx.inc("sv", c))
            c = ts(v, osel[:], osel[:], 512.0, ALU.is_ge, 33554432.0, ALU.mult)
            ctx.wait(v, "sv", ctx.inc("sv", c))
            ctx.wait(v, sso[j], ctx.val(sso[j]))
            c = v.tensor_tensor(out=dst0[:], in0=dst0[:], in1=osel[:], op=ALU.add)
            io["_s5_dest"] = ctx.inc("sv", c)
            ctx.wait(v, "st", io["_s5_csmm"])
            c = v.tensor_tensor(out=runrow[:], in0=runrow[:], in1=ps_cs[:],
                                op=ALU.add)
            io["_s5_rrup"] = ctx.inc("sv", c)

        @blk.scalar
        def _(sE):
            ctx.wait(sE, "sv", io["_s5_dest"])
            if i >= 2:
                ctx.wait(sE, sso[j], ctx.val(sso[j]))
            c = sE.activation(desti[j][:], dst0[:], AF.Copy, bias=0.0, scale=1.0)
            io["_s5_desti"] = ctx.inc("ss", c)

        @blk.gpsimd
        def _(g):
            ctx.wait(g, "ss", io["_s5_desti"])
            ctx.inc(sso[j], g.dma_start(
                out=dest_d[s, :, t:t + 1], in_=desti[j][:],
                allow_slow_non_contiguous=True), 16)
            ctx.inc(sso[j], g.dma_start(
                out=destf_d[s, :, t:t + 1], in_=dst0[:],
                allow_slow_non_contiguous=True), 16)

        @blk.gpsimd
        def _(g):
            ctx.wait(g, "sv", tile_done[i])
            ctx.inc(sfo[j], g.dma_start(
                out=feats_d[s, t * 128:(t + 1) * 128, :], in_=xn1[j][:]), 16)
            ctx.inc(smo[j], g.dma_start(
                out=mask_d[s, t * 128:(t + 1) * 128, :], in_=maskt[j][:]), 16)
            ctx.inc(smo[j], g.dma_start(
                out=asel_d[s, t * 128:(t + 1) * 128, :], in_=aselt[j][:]), 16)
            ctx.wait(g, "sv", io["_m5_xhb"])
            ctx.inc(sho[j], g.dma_start(
                out=xhat_tm_d[s, t * 128:(t + 1) * 128, :], in_=xhb[j][:]), 16)

        if i + NBUF < NSEQ * NT:
            emit_load(i + NBUF)

        if t == NT - 1:
            @blk.gpsimd
            def _(g):
                ctx.wait(g, "sv", tile_done[i])
                for p in range(NCH):
                    ctx.inc("sgd", g.dma_start(
                        out=xn1T_d[s, p * 128:(p + 1) * 128, :],
                        in_=asn1[:, p * S:(p + 1) * S]), 16)
                    ctx.inc("sgd", g.dma_start(
                        out=xhatT_d[s, p * 128:(p + 1) * 128, :],
                        in_=asx[:, p * S:(p + 1) * S]), 16)
            io["_m5_store"] = ctx.val("sgd")
            if s == 0:
                @blk.vector
                def _(v):
                    ctx.wait(v, "sgd", io["_m5_store"])

    io["s5_done_sgd"] = ctx.val("sgd")



def emit_stage6a(ctx, blk, io):
    nc, E = ctx.nc, ctx.E
    xn1T_d = io["xn1T"]
    qTl_d, kTl_d, vl_d = io["qTl"], io["kTl"], io["vl"]
    ktm_d = io["ktm"]
    wq_d, wk_d, wv_d = io["L1Wq"], io["L1Wk"], io["L1Wv"]
    bq_d, bk_d = io["l1bq_pc"], io["l1bk_pc"]
    bv_d = io["l1bv_row"]

    xnT_sb = E(nc.sbuf_tensor("s6a_xnT", [128, NCH * S], BF16))
    wsb = {}
    for nm in ("wq", "wk", "wv"):
        wsb[nm] = E(nc.sbuf_tensor(f"s6a_{nm}", [128, NCH * D], BF16))
    bq_sb = E(nc.sbuf_tensor("s6a_bq", [128, NCH], F32))
    bk_sb = E(nc.sbuf_tensor("s6a_bk", [128, NCH], F32))
    bv_sb = E(nc.sbuf_tensor("s6a_bv", [1, D], BF16))
    bkr_sb = E(nc.sbuf_tensor("s6a_bkr", [1, D], BF16))
    ones_bf = E(nc.sbuf_tensor("s6a_ones1", [1, 128], BF16))
    cbuf = {"q": [E(nc.sbuf_tensor(f"s6a_cq{j}", [128, S], BF16)) for j in range(2)],
            "k": [E(nc.sbuf_tensor(f"s6a_ck{j}", [128, S], BF16)) for j in range(2)]}
    v_sb = E(nc.sbuf_tensor("s6a_v", [128, NT * D], BF16))
    ps_a = [E(nc.psum_tensor(f"s6a_psa{j}", [128, 512], F32)) for j in range(2)]
    ps_v = [E(nc.psum_tensor(f"s6a_psv{j}", [128, 512], F32)) for j in range(2)]

    sxl = ctx.make_sems("s6axl", 1)[0]
    scb = ctx.make_sems("s6acb", 4)

    @blk.gpsimd
    def _(g):
        for nm, d_ in (("wq", wq_d), ("wk", wk_d), ("wv", wv_d)):
            src3 = d_.rearrange("(c p) o -> p c o", p=128)
            dst3 = wsb[nm][:].rearrange("p (c o) -> p c o", o=D)
            ctx.inc("sgd", g.dma_start(out=dst3, in_=src3), 16)
        ctx.inc("sgd", g.dma_start(out=bv_sb[:], in_=bv_d), 16)
        ctx.inc("sgd", g.dma_start(out=bkr_sb[:], in_=io["l1bk_row"]), 16)
    w_loaded = ctx.val("sgd")

    @blk.sync
    def _(q):
        ctx.inc("sd", q.dma_start(out=bq_sb[:], in_=bq_d), 16)
        ctx.inc("sd", q.dma_start(out=bk_sb[:], in_=bk_d), 16)
    misc_loaded = ctx.val("sd")

    @blk.vector
    def _(v):
        c0 = v.memset(ones_bf[:], 1.0)
        io["_s6a_ones"] = ctx.inc("sv", c0)

    for s in range(NSEQ):
        @blk.sync
        def _(q):
            if s > 0:
                ctx.wait(q, "st", io["_s6a_vmm"])
            src3 = xn1T_d[s].rearrange("(c p) t -> p c t", p=128)
            dst3 = xnT_sb[:].rearrange("p (c t) -> p c t", t=S)
            ctx.inc(sxl, q.dma_start(out=dst3, in_=src3), 16)
        xn_ready = ctx.val(sxl)

        for kind, wa, ba, dram in (("q", "wq", bq_sb, qTl_d),
                                   ("k", "wk", bk_sb, kTl_d)):
            for c in range(NCH):
                slot = c % 2
                for g_ in range(4):
                    jj = g_ % 2

                    @blk.tensor
                    def _(t):
                        if s == 0 and kind == "q" and c == 0 and g_ == 0:
                            ctx.wait(t, "sgd", w_loaded)
                            ctx.wait(t, "sd", misc_loaded)
                        ctx.wait(t, sxl, xn_ready)
                        if ("s6a_ev", jj) in io:
                            ctx.wait(t, "sv", io[("s6a_ev", jj)])
                        for kc in range(NCH):
                            ins = t.matmul(
                                ps_a[jj][:],
                                wsb[wa][:, kc * D + c * 128:kc * D + (c + 1) * 128],
                                xnT_sb[:, kc * S + g_ * 512:kc * S + g_ * 512 + 512],
                                start=(kc == 0), stop=(kc == NCH - 1))
                        io["_s6a_mm"] = ctx.inc("st", ins)

                    @blk.vector
                    def _(v):
                        ctx.wait(v, "st", io["_s6a_mm"])
                        if g_ == 0:
                            sem = scb[(0 if kind == "q" else 2) + slot]
                            ctx.wait(v, sem, ctx.val(sem))
                        c2 = ts(v, cbuf[kind][slot][:, g_ * 512:(g_ + 1) * 512],
                                ps_a[jj][:], ba[:, c:c + 1], ALU.add)
                        io[("s6a_ev", jj)] = ctx.inc("sv", c2)
                        io["_s6a_evl"] = io[("s6a_ev", jj)]

                @blk.gpsimd
                def _(g):
                    ctx.wait(g, "sv", io["_s6a_evl"])
                    sem = scb[(0 if kind == "q" else 2) + slot]
                    ctx.inc(sem, g.dma_start(
                        out=dram[s, c * 128:(c + 1) * 128, :],
                        in_=cbuf[kind][slot][:]), 16)

        for kind2, wname, brow, dram2 in (("v", "wv", bv_sb, vl_d),
                                          ("ktm", "wk", bkr_sb, ktm_d)):
            for tt in range(NT):
                for hf in range(2):
                    j = (tt * 2 + hf) % 2

                    @blk.tensor
                    def _(t):
                        if s == 0 and kind2 == "v" and tt == 0 and hf == 0:
                            ctx.wait(t, "sv", io["_s6a_ones"])
                        if ("s6a_vev", j) in io:
                            ctx.wait(t, "sv", io[("s6a_vev", j)])
                        for kc in range(NCH):
                            t.matmul(ps_v[j][:],
                                     xnT_sb[:, kc * S + tt * 128:kc * S + (tt + 1) * 128],
                                     wsb[wname][:, kc * D + hf * 512:kc * D + (hf + 1) * 512],
                                     start=(kc == 0), stop=False)
                        ins = t.matmul(ps_v[j][:], ones_bf[:],
                                       brow[:, hf * 512:(hf + 1) * 512],
                                       start=False, stop=True)
                        io["_s6a_vmm"] = ctx.inc("st", ins)

                    @blk.vector
                    def _(vE):
                        ctx.wait(vE, "st", io["_s6a_vmm"])
                        c2 = vE.tensor_copy(
                            v_sb[:, tt * D + hf * 512:tt * D + (hf + 1) * 512],
                            ps_v[j][:])
                        io[("s6a_vev", j)] = ctx.inc("sv", c2)
                        io["_s6a_vevl"] = io[("s6a_vev", j)]

            @blk.gpsimd
            def _(g):
                ctx.wait(g, "sv", io[("s6a_vev", 0)])
                ctx.wait(g, "sv", io[("s6a_vev", 1)])
                dst3 = dram2[s].rearrange("(tt p) f -> p tt f", p=128)
                src3 = v_sb[:].rearrange("p (tt f) -> p tt f", f=D)
                ctx.inc("sgd", g.dma_start(out=dst3, in_=src3), 16)
            io["_s6a_vstore"] = ctx.val("sgd")

            @blk.vector
            def _(vE):
                ctx.wait(vE, "sgd", io["_s6a_vstore"])

    io["s6a_done_sgd"] = ctx.val("sgd")



def emit_stage6b(ctx, blk, io):
    nc, E = ctx.nc, ctx.E
    qTl_d, kTl_d, vl_d, oTl_d = io["qTl"], io["kTl"], io["vl"], io["oTl"]
    ktm_d = io["ktm"]
    mt_d = io["decay_mt"]
    dup2_d = io["dup2"]
    wdec_d = io["wdec"]
    lamq_d = io["lamq"]
    LAM128 = 0.95 ** 128
    ident_bf = io["ident_bf"]

    qT_sb = E(nc.sbuf_tensor("s6b_qT", [128, NCH * S], BF16))
    kT_sb = E(nc.sbuf_tensor("s6b_kT", [128, NCH * S], BF16))
    v_sb = E(nc.sbuf_tensor("s6b_v", [128, NT * D], BF16))
    ktm_sb = E(nc.sbuf_tensor("s6b_ktm", [128, NT * D], BF16))
    mt_sb = E(nc.sbuf_tensor("s6b_mt", [128, 128], F32))
    wdec = E(nc.sbuf_tensor("s6b_wdec", [128, 1], F32))
    lamq = E(nc.sbuf_tensor("s6b_lamq", [128, 1], F32))
    Am = [E(nc.sbuf_tensor(f"s6b_am{j}", [128, 128], BF16)) for j in range(2)]
    Kd = E(nc.sbuf_tensor("s6b_kd", [128, 64], BF16))
    G = E(nc.sbuf_tensor("s6b_G", [64, 64], F32))
    Gb = E(nc.sbuf_tensor("s6b_Gb", [128, 64], BF16))
    dup2 = E(nc.sbuf_tensor("s6b_dup2", [64, 128], F32))
    tb = E(nc.sbuf_tensor("s6b_tb", [128, 64], F32))
    Ob = E(nc.sbuf_tensor("s6b_Ob", [128, 64], BF16))
    oTl_asm = E(nc.sbuf_tensor("s6b_asm", [128, NCH * S], BF16))
    ps_sA = E(nc.psum_tensor("s6b_psA", [128, 128], F32))
    ps_oA = E(nc.psum_tensor("s6b_poA", [128, 64], F32))
    ps_g = E(nc.psum_tensor("s6b_pg", [64, 64], F32))
    ps_oB = E(nc.psum_tensor("s6b_poB", [128, 64], F32))
    ps_gb = E(nc.psum_tensor("s6b_pgb", [128, 64], F32))
    ps_ot = E(nc.psum_tensor("s6b_pot", [64, 128], BF16))

    sld = ctx.make_sems("s6bld", 4)

    @blk.sync
    def _(q):
        ctx.inc("sd", q.dma_start(out=mt_sb[:], in_=mt_d), 16)
        ctx.inc("sd", q.dma_start(out=wdec[:], in_=wdec_d), 16)
        ctx.inc("sd", q.dma_start(out=lamq[:], in_=lamq_d), 16)
        ctx.inc("sd", q.dma_start(out=dup2[:], in_=dup2_d), 16)
    misc_ld = ctx.val("sd")

    for s in range(NSEQ):
        @blk.sync
        def _(q):
            if s > 0:
                ctx.wait(q, "sgd", io["_s6b_store"])
            for dsem, d_, sb, il in ((sld[0], qTl_d, qT_sb, "(c p) t -> p c t"),
                                     (sld[1], kTl_d, kT_sb, "(c p) t -> p c t"),
                                     (sld[2], vl_d, v_sb, "(tt p) f -> p tt f"),
                                     (sld[3], ktm_d, ktm_sb, "(tt p) f -> p tt f")):
                src3 = d_[s].rearrange(il, p=128)
                if sb is v_sb or sb is ktm_sb:
                    dst3 = sb[:].rearrange("p (tt f) -> p tt f", f=D)
                else:
                    dst3 = sb[:].rearrange("p (c t) -> p c t", t=S)
                ctx.inc(dsem, q.dma_start(out=dst3, in_=src3), 16)
        lds = [ctx.val(x) for x in sld]

        for h in range(16):
            ch, po = h // 2, (h % 2) * 64

            @blk.vector
            def _(v):
                if s == 0 and h == 0:
                    ctx.wait(v, "sd", misc_ld)
                c = v.memset(G[:], 0.0)
                io["_s6b_G0"] = ctx.inc("sv", c)

            for ct in range(NT):
                ja = ct % 2

                @blk.tensor
                def _(t):
                    ctx.wait(t, sld[0], lds[0])
                    ctx.wait(t, sld[1], lds[1])
                    ctx.wait(t, sld[2], lds[2])
                    ctx.wait(t, sld[3], lds[3])
                    if ("s6b_amL",) in io:
                        ctx.wait(t, "sv", io[("s6b_amL",)])
                    c = t.matmul(ps_sA[:],
                                 kT_sb[po:po + 64, ch * S + ct * 128:ch * S + (ct + 1) * 128],
                                 qT_sb[po:po + 64, ch * S + ct * 128:ch * S + (ct + 1) * 128],
                                 start=True, stop=True)
                    io["_s6b_sA"] = ctx.inc("st", c)

                @blk.vector
                def _(v):
                    ctx.wait(v, "st", io["_s6b_sA"])
                    c = v.tensor_tensor(out=Am[ja][:], in0=ps_sA[:], in1=mt_sb[:],
                                        op=ALU.mult)
                    io[("s6b_am", ja)] = ctx.inc("sv", c)
                    io[("s6b_amL",)] = io[("s6b_am", ja)]
                    if ("s6b_gmm",) in io:
                        ctx.wait(v, "st", io[("s6b_gmm",)])
                    c = ts(v, Kd[:], ktm_sb[:, ct * D + h * 64:ct * D + (h + 1) * 64],
                           wdec[:], ALU.mult)
                    io[("s6b_kd",)] = ctx.inc("sv", c)

                @blk.tensor
                def _(t):
                    ctx.wait(t, "sv", io[("s6b_am", ja)])
                    if ("s6b_obc",) in io:
                        ctx.wait(t, "sv", io[("s6b_obc",)])
                    c = t.matmul(ps_oA[:], Am[ja][:],
                                 v_sb[:, ct * D + h * 64:ct * D + (h + 1) * 64],
                                 start=True, stop=True)
                    io["_s6b_oA"] = ctx.inc("st", c)
                    ctx.wait(t, "sv", io[("s6b_kd",)])
                    if ("s6b_gup",) in io:
                        ctx.wait(t, "sv", io[("s6b_gup",)])
                    c = t.matmul(ps_g[:], Kd[:],
                                 v_sb[:, ct * D + h * 64:ct * D + (h + 1) * 64],
                                 start=True, stop=True)
                    io["_s6b_g"] = ctx.inc("st", c)
                    io[("s6b_gmm",)] = io["_s6b_g"]
                    if ct > 0:
                        c = t.matmul(ps_oB[:],
                                     qT_sb[po:po + 64, ch * S + ct * 128:ch * S + (ct + 1) * 128],
                                     Gb[po:po + 64, :], start=True, stop=True)
                        io[("s6b_oB",)] = ctx.inc("st", c)

                @blk.vector
                def _(v):
                    ctx.wait(v, "st", io["_s6b_oA"])
                    if ct > 0:
                        ctx.wait(v, "st", io[("s6b_oB",)])
                        c = ts(v, tb[:], ps_oB[:], lamq[:], ALU.mult)
                        ctx.wait(v, "sv", ctx.inc("sv", c))
                        c = v.tensor_tensor(out=Ob[:], in0=ps_oA[:], in1=tb[:],
                                            op=ALU.add)
                    else:
                        c = v.tensor_copy(Ob[:], ps_oA[:])
                    io["_s6b_ob"] = ctx.inc("sv", c)
                    ctx.wait(v, "st", io["_s6b_g"])
                    c = v.scalar_tensor_tensor(out=G[:], in0=G[:], scalar=LAM128,
                                               in1=ps_g[:], op0=ALU.mult,
                                               op1=ALU.add)
                    io[("s6b_gup",)] = ctx.inc("sv", c)

                @blk.tensor
                def _(t):
                    ctx.wait(t, "sv", io["_s6b_ob"])
                    if ("s6b_otc",) in io:
                        ctx.wait(t, "sv", io[("s6b_otc",)])
                    c = t.transpose(ps_ot[:], Ob[:], ident_bf[:])
                    io["_s6b_ot"] = ctx.inc("st", c)
                    ctx.wait(t, "sv", io[("s6b_gup",)])
                    c = t.matmul(ps_gb[:], dup2[:], G[:], start=True, stop=True)
                    io["_s6b_gbm"] = ctx.inc("st", c)

                @blk.vector
                def _(v):
                    ctx.wait(v, "st", io["_s6b_gbm"])
                    if ct > 0:
                        ctx.wait(v, "st", io[("s6b_oB",)])
                    c = v.tensor_copy(Gb[:], ps_gb[:])
                    io["_s6b_gb"] = ctx.inc("sv", c)

                @blk.tensor
                def _(t):
                    if ct < NT - 1:
                        ctx.wait(t, "sv", io["_s6b_gb"])

                @blk.vector
                def _(v):
                    ctx.wait(v, "st", io["_s6b_ot"])
                    dst = oTl_asm[po:po + 64, ch * S + ct * 128:ch * S + (ct + 1) * 128]
                    c = v.tensor_copy(dst, ps_ot[:])
                    io[("s6b_otc",)] = ctx.inc("sv", c)

        @blk.gpsimd
        def _(g):
            ctx.wait(g, "sv", io[("s6b_otc",)])
            for p in range(NCH):
                ctx.inc("sgd", g.dma_start(
                    out=oTl_d[s, p * 128:(p + 1) * 128, :],
                    in_=oTl_asm[:, p * S:(p + 1) * S]), 16)
        io["_s6b_store"] = ctx.val("sgd")

    io["s6b_done_sgd"] = ctx.val("sgd")



def emit_stage6c(ctx, blk, io):
    nc, E = ctx.nc, ctx.E
    oTl_d, s1_d = io["oTl"], io["s1"]
    wo_d = io["L1Wo"]
    bo_d = io["l1bo_row"]
    gam_d, bet_d = io["l1gam_row"], io["l1bet_row"]

    oT_sb = E(nc.sbuf_tensor("s6c_oT", [128, NCH * S], BF16))
    wo_sb = E(nc.sbuf_tensor("s6c_wo", [128, NCH * D], BF16))
    bo_row = E(nc.sbuf_tensor("s6c_bor", [1, D], F32))
    gam_row = E(nc.sbuf_tensor("s6c_gam", [1, D], F32))
    bet_row = E(nc.sbuf_tensor("s6c_bet", [1, D], F32))
    onesf = E(nc.sbuf_tensor("s6c_ones", [1, 128], F32))
    onesc = E(nc.sbuf_tensor("s6c_onesc", [128, 1], F32))
    bo_exp = E(nc.sbuf_tensor("s6c_boe", [128, D], F32))
    y_sb = E(nc.sbuf_tensor("s6c_y", [128, NT * D], F32))
    sq_scr = E(nc.sbuf_tensor("s6c_sq", [128, D], F32))
    acc_s = E(nc.sbuf_tensor("s6c_accs", [128, 16], F32))
    acc_q = E(nc.sbuf_tensor("s6c_accq", [128, 16], F32))
    srow = E(nc.sbuf_tensor("s6c_srow", [1, 16], F32))
    qrow = E(nc.sbuf_tensor("s6c_qrow", [1, 16], F32))
    mrow = E(nc.sbuf_tensor("s6c_mrow", [1, 16], F32))
    vrow = E(nc.sbuf_tensor("s6c_vrow", [1, 16], F32))
    rrow = E(nc.sbuf_tensor("s6c_rrow", [1, 16], F32))
    arow = E(nc.sbuf_tensor("s6c_arow", [1, D], F32))
    brow = E(nc.sbuf_tensor("s6c_brow", [1, D], F32))
    epsr = E(nc.sbuf_tensor("s6c_epsr", [1, 16], F32))
    a_exp = E(nc.sbuf_tensor("s6c_aexp", [128, D], F32))
    b_exp = E(nc.sbuf_tensor("s6c_bexp", [128, D], F32))
    s1t = [E(nc.sbuf_tensor(f"s6c_s1t{j}", [128, D], F32)) for j in range(2)]
    ps_s = [E(nc.psum_tensor(f"s6c_pss{j}", [128, 512], F32)) for j in range(2)]
    ps_b = E(nc.psum_tensor("s6c_psb", [128, D], F32))
    ps_r = E(nc.psum_tensor("s6c_psr", [1, 16], F32))

    sxl = ctx.make_sems("s6cxl", 1)[0]
    so1 = ctx.make_sems("s6co", 2)

    @blk.gpsimd
    def _(g):
        src3 = wo_d.rearrange("(c p) o -> p c o", p=128)
        ctx.inc("sgd", g.dma_start(
            out=wo_sb[:].rearrange("p (c o) -> p c o", o=D), in_=src3), 16)
    w_ld = ctx.val("sgd")

    @blk.sync
    def _(q):
        ctx.inc("sd", q.dma_start(out=bo_row[:], in_=bo_d), 16)
        ctx.inc("sd", q.dma_start(out=gam_row[:], in_=gam_d), 16)
        ctx.inc("sd", q.dma_start(out=bet_row[:], in_=bet_d), 16)
    b_ld = ctx.val("sd")

    @blk.vector
    def _(v):
        v.memset(onesf[:], 1.0)
        v.memset(onesc[:], 1.0)
        c = v.memset(epsr[:], 1e-5)
        io["_s6c_ones"] = ctx.inc("sv", c)

    @blk.tensor
    def _(t):
        ctx.wait(t, "sd", b_ld)
        ctx.wait(t, "sv", io["_s6c_ones"])
        for half in range(2):
            c = t.matmul(ps_b[:, half * 512:(half + 1) * 512], onesf[:],
                         bo_row[:, half * 512:(half + 1) * 512],
                         start=True, stop=True)
        io["_s6c_bomm"] = ctx.inc("st", c)

    @blk.vector
    def _(v):
        ctx.wait(v, "st", io["_s6c_bomm"])
        c = v.tensor_copy(bo_exp[:], ps_b[:])
        io["_s6c_boe"] = ctx.inc("sv", c)

    for s in range(NSEQ):
        @blk.sync
        def _(q):
            if s > 0:
                ctx.wait(q, "sv", io["_s6c_applydone"])
            src3 = oTl_d[s].rearrange("(c p) t -> p c t", p=128)
            ctx.inc(sxl, q.dma_start(
                out=oT_sb[:].rearrange("p (c t) -> p c t", t=S), in_=src3), 16)
        oT_ld = ctx.val(sxl)

        @blk.vector
        def _(v):
            if s > 0:
                ctx.wait(v, "sv", io["_s6c_applydone"])
            v.memset(acc_s[:], 0.0)
            c = v.memset(acc_q[:], 0.0)
            io["_s6c_acc0"] = ctx.inc("sv", c)

        for tt in range(NT):
            j = tt % 2
            for half in range(2):
                @blk.tensor
                def _(t):
                    if s == 0 and tt == 0 and half == 0:
                        ctx.wait(t, "sgd", w_ld)
                    ctx.wait(t, sxl, oT_ld)
                    if ("s6c_sev", j) in io:
                        ctx.wait(t, "sv", io[("s6c_sev", j)])
                    for kc in range(NCH):
                        c = t.matmul(
                            ps_s[j][:],
                            oT_sb[:, kc * S + tt * 128:kc * S + (tt + 1) * 128],
                            wo_sb[:, kc * D + half * 512:kc * D + (half + 1) * 512],
                            start=(kc == 0), stop=(kc == NCH - 1))
                    io[("s6c_smm", j, half)] = ctx.inc("st", c)

                @blk.vector
                def _(v):
                    ctx.wait(v, "st", io[("s6c_smm", j, half)])
                    if s == 0 and tt == 0 and half == 0:
                        ctx.wait(v, "sv", io["_s6c_boe"])
                    c = v.tensor_tensor(
                        out=y_sb[:, tt * D + half * 512:tt * D + (half + 1) * 512],
                        in0=ps_s[j][:],
                        in1=bo_exp[:, half * 512:(half + 1) * 512], op=ALU.add)
                    io[("s6c_sev", j)] = ctx.inc("sv", c)

            @blk.vector
            def _(v):
                ctx.wait(v, "sv", io["_s6c_acc0"])
                yv = y_sb[:, tt * D:(tt + 1) * D].rearrange("p (g e) -> p g e", e=64)
                c = v.tensor_reduce(out=sq_scr[:, 0:16], in_=yv, axis=AX.X,
                                    op=ALU.add)
                ctx.wait(v, "sv", ctx.inc("sv", c))
                c = v.tensor_tensor(out=acc_s[:], in0=acc_s[:],
                                    in1=sq_scr[:, 0:16], op=ALU.add)
                io["_s6c_sacc"] = ctx.inc("sv", c)

            @blk.scalar
            def _(sE):
                ctx.wait(sE, "sv", io["_s6c_sacc"])
                c = sE.activation(sq_scr[:], y_sb[:, tt * D:(tt + 1) * D],
                                  AF.Square, bias=0.0, scale=1.0)
                io["_s6c_sq"] = ctx.inc("ss", c)

            @blk.vector
            def _(v):
                ctx.wait(v, "ss", io["_s6c_sq"])
                sv3 = sq_scr[:].rearrange("p (g e) -> p g e", e=64)
                c = v.tensor_reduce(out=sq_scr[:, 0:16], in_=sv3, axis=AX.X,
                                    op=ALU.add)
                ctx.wait(v, "sv", ctx.inc("sv", c))
                c = v.tensor_tensor(out=acc_q[:], in0=acc_q[:],
                                    in1=sq_scr[:, 0:16], op=ALU.add)
                io["_s6c_qacc"] = ctx.inc("sv", c)

        @blk.tensor
        def _(t):
            ctx.wait(t, "sv", io["_s6c_qacc"])
            c = t.matmul(ps_r[:], onesc[:], acc_s[:], start=True, stop=True)
            io["_s6c_psum1"] = ctx.inc("st", c)

        @blk.vector
        def _(v):
            ctx.wait(v, "st", io["_s6c_psum1"])
            c = v.tensor_copy(srow[:], ps_r[:])
            io["_s6c_srow"] = ctx.inc("sv", c)

        @blk.tensor
        def _(t):
            ctx.wait(t, "sv", io["_s6c_srow"])
            c = t.matmul(ps_r[:], onesc[:], acc_q[:], start=True, stop=True)
            io["_s6c_psum2"] = ctx.inc("st", c)

        @blk.vector
        def _(v):
            ctx.wait(v, "st", io["_s6c_psum2"])
            c = v.tensor_copy(qrow[:], ps_r[:])
            ctx.wait(v, "sv", ctx.inc("sv", c))
            NTOT = 64.0 * S
            c = ts(v, mrow[:], srow[:], 1.0 / NTOT, ALU.mult)
            ctx.wait(v, "sv", ctx.inc("sv", c))
            c = v.tensor_tensor(out=vrow[:], in0=mrow[:], in1=mrow[:], op=ALU.mult)
            ctx.wait(v, "sv", ctx.inc("sv", c))
            c = v.scalar_tensor_tensor(out=vrow[:], in0=qrow[:], scalar=1.0 / NTOT,
                                       in1=vrow[:], op0=ALU.mult, op1=ALU.subtract)
            ctx.wait(v, "sv", ctx.inc("sv", c))
            c = v.tensor_tensor(out=vrow[:], in0=vrow[:], in1=epsr[:], op=ALU.add)
            io["_s6c_var"] = ctx.inc("sv", c)

        @blk.scalar
        def _(sE):
            ctx.wait(sE, "sv", io["_s6c_var"])
            c = sE.activation(rrow[:], vrow[:], AF.Sqrt, bias=0.0, scale=1.0)
            io["_s6c_sd"] = ctx.inc("ss", c)

        @blk.vector
        def _(v):
            ctx.wait(v, "ss", io["_s6c_sd"])
            c = v.reciprocal(rrow[:], rrow[:])
            ctx.wait(v, "sv", ctx.inc("sv", c))
            gv = gam_row[:].rearrange("p (g e) -> p g e", e=64)
            rv = AP(rrow.ap().tensor, rrow.ap().offset,
                    [list(rrow.ap().ap[0]), list(rrow.ap().ap[1]), [0, 64]])
            av = arow[:].rearrange("p (g e) -> p g e", e=64)
            c = v.tensor_tensor(out=av, in0=gv, in1=rv, op=ALU.mult)
            ctx.wait(v, "sv", ctx.inc("sv", c))
            mv = AP(mrow.ap().tensor, mrow.ap().offset,
                    [list(mrow.ap().ap[0]), list(mrow.ap().ap[1]), [0, 64]])
            bv3 = brow[:].rearrange("p (g e) -> p g e", e=64)
            c = v.tensor_tensor(out=bv3, in0=arow[:].rearrange("p (g e) -> p g e", e=64),
                                in1=mv, op=ALU.mult)
            ctx.wait(v, "sv", ctx.inc("sv", c))
            c = v.tensor_tensor(out=brow[:], in0=bet_row[:], in1=brow[:],
                                op=ALU.subtract)
            io["_s6c_ab"] = ctx.inc("sv", c)

        @blk.tensor
        def _(t):
            ctx.wait(t, "sv", io["_s6c_ab"])
            for half in range(2):
                c = t.matmul(ps_b[:, half * 512:(half + 1) * 512], onesf[:],
                             arow[:, half * 512:(half + 1) * 512],
                             start=True, stop=True)
            io["_s6c_amm"] = ctx.inc("st", c)

        @blk.vector
        def _(v):
            ctx.wait(v, "st", io["_s6c_amm"])
            c = v.tensor_copy(a_exp[:], ps_b[:])
            io["_s6c_aexp"] = ctx.inc("sv", c)

        @blk.tensor
        def _(t):
            ctx.wait(t, "sv", io["_s6c_aexp"])
            for half in range(2):
                c = t.matmul(ps_b[:, half * 512:(half + 1) * 512], onesf[:],
                             brow[:, half * 512:(half + 1) * 512],
                             start=True, stop=True)
            io["_s6c_bmm"] = ctx.inc("st", c)

        @blk.vector
        def _(v):
            ctx.wait(v, "st", io["_s6c_bmm"])
            c = v.tensor_copy(b_exp[:], ps_b[:])
            io["_s6c_bexp"] = ctx.inc("sv", c)

        for tt in range(NT):
            j = tt % 2

            @blk.vector
            def _(v):
                if tt == 0:
                    ctx.wait(v, "sv", io["_s6c_bexp"])
                ctx.wait(v, so1[j], ctx.val(so1[j]))
                c = v.tensor_tensor(out=s1t[j][:], in0=y_sb[:, tt * D:(tt + 1) * D],
                                    in1=a_exp[:], op=ALU.mult)
                ctx.wait(v, "sv", ctx.inc("sv", c))
                c = v.tensor_tensor(out=s1t[j][:], in0=s1t[j][:], in1=b_exp[:],
                                    op=ALU.add)
                io[("s6c_app", j)] = ctx.inc("sv", c)
                io["_s6c_applydone"] = io[("s6c_app", j)]

            @blk.gpsimd
            def _(g):
                ctx.wait(g, "sv", io[("s6c_app", j)])
                ctx.inc(so1[j], g.dma_start(
                    out=s1_d[s, tt * 128:(tt + 1) * 128, :], in_=s1t[j][:]), 16)

    io["s6c_done_sgd"] = ctx.val("sgd")



def emit_stage7(ctx, blk, io):
    nc, E = ctx.nc, ctx.E
    xhat_tm_d, dest_d = io["xhat_tm"], io["dest"]
    destf_d = io["destf"]
    xsort_d, zsort_d = io["xsort"], io["zsort"]
    w1_d, w2_d = io["w1"], io["w2"]
    ga_d, be_d = io["ga_pc"], io["be_pc"]
    b1_d = io["b1_pc"]
    b2_d = io["b2_rows"]
    slotfull_d = io["slotfull"]
    ident_bf = io["ident_bf"]
    FF = 4096
    C = 512
    NG = C // 128

    import contextlib
    with contextlib.ExitStack() as st0:
        E0 = st0.enter_context
        xh_sb = E0(nc.sbuf_tensor("s7p_xh", [128, NT * D], BF16))
        dest_sb = E0(nc.sbuf_tensor("s7p_dest", [128, NT], F32))
        slotf = E0(nc.sbuf_tensor("s7p_slot", [128, 4096], F32))
        pt = [E0(nc.sbuf_tensor(f"s7p_pt{j}", [128, 128], BF16)) for j in range(2)]
        xs_st = [E0(nc.sbuf_tensor(f"s7p_xs{j}", [128, D], BF16)) for j in range(2)]
        ps_x = [E0(nc.psum_tensor(f"s7p_psx{j}", [128, 512], F32)) for j in range(4)]
        sxl = ctx.make_sems("s7pxl", 1)[0]
        sso2 = ctx.make_sems("s7pso", 2)

        @blk.sync
        def _(q):
            ctx.inc("sd", q.dma_start(out=slotf[:], in_=slotfull_d), 16)
        slot_ld = ctx.val("sd")

        for s in range(NSEQ):
            @blk.sync
            def _(q):
                if s > 0:
                    ctx.wait(q, "st", io["_s7p_lastmm"])
                src3 = xhat_tm_d[s].rearrange("(tt p) d -> p tt d", p=128)
                ctx.inc(sxl, q.dma_start(
                    out=xh_sb[:].rearrange("p (tt d) -> p tt d", d=D), in_=src3), 16)
                ctx.inc(sxl, q.dma_start(out=dest_sb[:], in_=destf_d[s]), 16)
            xh_ld = ctx.val(sxl)

            for st_ in range(32):
                jj = st_ % 2
                for tc in range(NT):
                    @blk.vector
                    def _(v):
                        if st_ == 0 and tc == 0:
                            ctx.wait(v, sxl, xh_ld)
                            ctx.wait(v, "sd", slot_ld)
                        key = ("s7p_mmh", tc % 2)
                        if key in io:
                            ctx.wait(v, "st", io[key])
                        c = ts(v, pt[tc % 2][:],
                               slotf[:, st_ * 128:(st_ + 1) * 128],
                               dest_sb[:, tc:tc + 1], ALU.is_equal)
                        io[("s7p_pt", tc)] = ctx.inc("sv", c)

                    @blk.tensor
                    def _(t):
                        ctx.wait(t, "sv", io[("s7p_pt", tc)])
                        for half in range(2):
                            bank = (st_ % 2) * 2 + half
                            if tc == 0 and ("s7p_ev", bank) in io:
                                ctx.wait(t, "sv", io[("s7p_ev", bank)])
                            c = t.matmul(
                                ps_x[bank][:],
                                pt[tc % 2][:],
                                xh_sb[:, tc * D + half * 512:tc * D + (half + 1) * 512],
                                start=(tc == 0), stop=(tc == NT - 1))
                        io[("s7p_mmh", tc % 2)] = ctx.inc("st", c)
                        if tc == NT - 1:
                            for half in range(2):
                                io[("s7p_mmb", (st_ % 2) * 2 + half)] = \
                                    io[("s7p_mmh", tc % 2)]
                        io["_s7p_lastmm"] = io[("s7p_mmh", tc % 2)]

                @blk.vector
                def _(v):
                    ctx.wait(v, sso2[jj], ctx.val(sso2[jj]))
                    for half in range(2):
                        bank = (st_ % 2) * 2 + half
                        ctx.wait(v, "st", io[("s7p_mmb", bank)])
                        c = v.tensor_copy(xs_st[jj][:, half * 512:(half + 1) * 512],
                                          ps_x[bank][:])
                        io[("s7p_ev", bank)] = ctx.inc("sv", c)
                    io["_s7p_evl"] = ctx.val("sv")

                @blk.gpsimd
                def _(g):
                    ctx.wait(g, "sv", io["_s7p_evl"])
                    ctx.inc(sso2[jj], g.dma_start(
                        out=xsort_d[s, st_ * 128:(st_ + 1) * 128, :],
                        in_=xs_st[jj][:]), 16)
        @blk.vector
        def _(v):
            ctx.wait(v, sso2[0], ctx.val(sso2[0]))
            ctx.wait(v, sso2[1], ctx.val(sso2[1]))
    ctx.barrier(blk)

    w1_sb = E(nc.sbuf_tensor("s7_w1", [128, NCH * FF], BF16))
    w2_sb = E(nc.sbuf_tensor("s7_w2", [128, (FF // 128) * D], BF16))
    hT = E(nc.sbuf_tensor("s7_hT", [128, NCH * C], BF16))
    aT = E(nc.sbuf_tensor("s7_aT", [128, 32 * C], BF16))
    gb = [E(nc.sbuf_tensor(f"s7_gb{j}", [128, D], BF16)) for j in range(2)]
    zf = [E(nc.sbuf_tensor(f"s7_zf{j}", [128, D], F32)) for j in range(2)]
    ga_sb = E(nc.sbuf_tensor("s7_ga", [128, 64], F32))
    be_sb = E(nc.sbuf_tensor("s7_be", [128, 64], F32))
    b1_sb = E(nc.sbuf_tensor("s7_b1", [128, 8 * 32], F32))
    b2_row = E(nc.sbuf_tensor("s7_b2r", [1, D], F32))
    b2_exp = E(nc.sbuf_tensor("s7_b2e", [128, D], F32))
    onesf = E(nc.sbuf_tensor("s7_ones", [1, 128], F32))
    ps_a = [E(nc.psum_tensor(f"s7_psa{j}", [128, C], F32)) for j in range(2)]
    ps_z = [E(nc.psum_tensor(f"s7_psz{j}", [128, 512], F32)) for j in range(2)]
    ps_tr = E(nc.psum_tensor("s7_ptr", [128, D], BF16))
    ps_b = E(nc.psum_tensor("s7_psb", [128, D], F32))

    sw = ctx.make_sems("s7w", 2)
    sgi = ctx.make_sems("s7gi", 2)
    szo = ctx.make_sems("s7zo", 2)

    @blk.sync
    def _(q):
        ctx.inc("sd", q.dma_start(out=ga_sb[:], in_=ga_d), 16)
        ctx.inc("sd", q.dma_start(out=be_sb[:], in_=be_d), 16)
        ctx.inc("sd", q.dma_start(out=b1_sb[:], in_=b1_d), 16)
    misc_ld = ctx.val("sd")

    @blk.vector
    def _(v):
        c = v.memset(onesf[:], 1.0)
        io["_s7_zt"] = ctx.inc("sv", c)

    for e in range(8):
        @blk.gpsimd
        def _(g):
            if e > 0:
                ctx.wait(g, "st", io["_s7_lastA"])
            src1 = w1_d[e].rearrange("(c p) f -> p c f", p=128)
            ctx.inc(sw[0], g.dma_start(
                out=w1_sb[:].rearrange("p (c f) -> p c f", f=FF), in_=src1), 16)
            if e > 0:
                ctx.wait(g, "st", io["_s7_lastB"])
            src2 = w2_d[e].rearrange("(c p) f -> p c f", p=128)
            ctx.inc(sw[1], g.dma_start(
                out=w2_sb[:].rearrange("p (c f) -> p c f", f=D), in_=src2), 16)
        w_ld = [ctx.val(sw[0]), ctx.val(sw[1])]

        @blk.sync
        def _(q):
            if e > 0:
                ctx.wait(q, "sv", io["_s7_lastzev"])
            ctx.inc("sd", q.dma_start(out=b2_row[:], in_=b2_d[e:e + 1, :]), 16)
        b2_ld = ctx.val("sd")

        @blk.tensor
        def _(t):
            ctx.wait(t, "sd", b2_ld)
            ctx.wait(t, "sv", io["_s7_zt"])
            if ("s7_b2e",) in io:
                ctx.wait(t, "sv", io[("s7_b2e",)])
            for half in range(2):
                c = t.matmul(ps_b[:, half * 512:(half + 1) * 512], onesf[:],
                             b2_row[:, half * 512:(half + 1) * 512],
                             start=True, stop=True)
            io["_s7_b2mm"] = ctx.inc("st", c)

        @blk.vector
        def _(v):
            ctx.wait(v, "st", io["_s7_b2mm"])
            c = v.tensor_copy(b2_exp[:], ps_b[:])
            io[("s7_b2e",)] = ctx.inc("sv", c)

        for s in range(NSEQ):
            for k in range(NG):
                jg = k % 2

                @blk.sync
                def _(q):
                    if ("s7_trd", jg) in io:
                        ctx.wait(q, "st", io[("s7_trd", jg)])
                    ctx.inc(sgi[jg], q.dma_start(
                        out=gb[jg][:],
                        in_=xsort_d[s, e * C + k * 128:e * C + (k + 1) * 128, :]), 16)

                @blk.tensor
                def _(t):
                    ctx.wait(t, sgi[jg], ctx.val(sgi[jg]))
                    if "_s7_lasthev" in io:
                        ctx.wait(t, "sv", io["_s7_lasthev"])
                    for c_ in range(NCH):
                        c = t.transpose(ps_tr[:, c_ * 128:(c_ + 1) * 128],
                                        gb[jg][:, c_ * 128:(c_ + 1) * 128],
                                        ident_bf[:])
                    io[("s7_trd", jg)] = ctx.inc("st", c)

                @blk.vector
                def _(v):
                    ctx.wait(v, "st", io[("s7_trd", jg)])
                    ctx.wait(v, "sd", misc_ld)
                    if "_s7_lastamm" in io:
                        ctx.wait(v, "st", io["_s7_lastamm"])
                    for c_ in range(NCH):
                        c = ts(v, hT[:, c_ * C + k * 128:c_ * C + (k + 1) * 128],
                               ps_tr[:, c_ * 128:(c_ + 1) * 128],
                               ga_sb[:, e * 8 + c_:e * 8 + c_ + 1], ALU.mult,
                               be_sb[:, e * 8 + c_:e * 8 + c_ + 1], ALU.add)
                    io[("s7_hev", k)] = ctx.inc("sv", c)
                    io["_s7_lasthev"] = io[("s7_hev", k)]
            h_ready = [io[("s7_hev", k)] for k in range(NG)]

            for f128 in range(32):
                jA = f128 % 2

                @blk.tensor
                def _(t):
                    if f128 == 0:
                        ctx.wait(t, sw[0], w_ld[0])
                        for k in range(NG):
                            ctx.wait(t, "sv", h_ready[k])
                    if ("s7_rel", jA) in io:
                        ctx.wait(t, "sv", io[("s7_rel", jA)])
                    for dc in range(NCH):
                        c = t.matmul(
                            ps_a[jA][:],
                            w1_sb[:, dc * FF + f128 * 128:dc * FF + (f128 + 1) * 128],
                            hT[:, dc * C:(dc + 1) * C],
                            start=(dc == 0), stop=(dc == NCH - 1))
                    io[("s7_amm", jA)] = ctx.inc("st", c)
                    io["_s7_lastamm"] = io[("s7_amm", jA)]
                    io["_s7_lastA"] = io[("s7_amm", jA)]

                @blk.vector
                def _(v):
                    ctx.wait(v, "st", io[("s7_amm", jA)])
                    if ("s7_zmm", jA) in io:
                        ctx.wait(v, "st", io[("s7_zmm", jA)])
                    c = ts(v, aT[:, f128 * C:(f128 + 1) * C], ps_a[jA][:],
                           b1_sb[:, e * 32 + f128:e * 32 + f128 + 1],
                           ALU.add, 0.0, ALU.max)
                    io[("s7_rel", jA)] = ctx.inc("sv", c)

            for tk in range(NG):
                jz = tk % 2
                for dhalf in range(2):
                    @blk.tensor
                    def _(t):
                        if tk == 0 and dhalf == 0:
                            ctx.wait(t, sw[1], w_ld[1])
                            ctx.wait(t, "sv", io[("s7_rel", 0)])
                            ctx.wait(t, "sv", io[("s7_rel", 1)])
                        if ("s7_zev", jz) in io:
                            ctx.wait(t, "sv", io[("s7_zev", jz)])
                        for f128 in range(32):
                            c = t.matmul(
                                ps_z[jz][:],
                                aT[:, f128 * C + tk * 128:f128 * C + (tk + 1) * 128],
                                w2_sb[:, f128 * D + dhalf * 512:
                                      f128 * D + (dhalf + 1) * 512],
                                start=(f128 == 0), stop=(f128 == 31))
                        io[("s7_zmm", jz)] = ctx.inc("st", c)
                        io["_s7_lastB"] = io[("s7_zmm", jz)]

                    @blk.vector
                    def _(v):
                        ctx.wait(v, "st", io[("s7_zmm", jz)])
                        ctx.wait(v, "sv", io[("s7_b2e",)])
                        if dhalf == 0:
                            ctx.wait(v, szo[jz], ctx.val(szo[jz]))
                        c = v.tensor_tensor(
                            out=zf[jz][:, dhalf * 512:(dhalf + 1) * 512],
                            in0=ps_z[jz][:],
                            in1=b2_exp[:, dhalf * 512:(dhalf + 1) * 512],
                            op=ALU.add)
                        io[("s7_zev", jz)] = ctx.inc("sv", c)
                        io["_s7_lastzev"] = io[("s7_zev", jz)]

                @blk.gpsimd
                def _(g):
                    ctx.wait(g, "sv", io[("s7_zev", jz)])
                    ctx.inc(szo[jz], g.dma_start(
                        out=zsort_d[s, e * C + tk * 128:e * C + (tk + 1) * 128, :],
                        in_=zf[jz][:]), 16)

    io["s7_done_sgd"] = ctx.val("sgd")


def emit_stage8(ctx, blk, io):
    nc, E = ctx.nc, ctx.E
    feats_d, asel_d = io["feats"], io["asel"]
    s1_d, x1_d, out_d = io["s1"], io["x1"], io["out"]
    zsort_d, dest_d = io["zsort"], io["dest"]

    ft = [E(nc.sbuf_tensor(f"s8_ft{j}", [128, D], F32)) for j in range(2)]
    at = [E(nc.sbuf_tensor(f"s8_at{j}", [128, D], F32)) for j in range(2)]
    st_ = [E(nc.sbuf_tensor(f"s8_st{j}", [128, D], F32)) for j in range(2)]
    x1t = [E(nc.sbuf_tensor(f"s8_x1t{j}", [128, D], F32)) for j in range(2)]
    al = [E(nc.sbuf_tensor(f"s8_al{j}", [128, 1], F32)) for j in range(2)]
    dcol = [E(nc.sbuf_tensor(f"s8_dc{j}", [128, 1], mybir.dt.uint32))
            for j in range(2)]
    ot = [E(nc.sbuf_tensor(f"s8_ot{j}", [128, D], F32)) for j in range(2)]
    sld = ctx.make_sems("s8ld", 2)
    sgo = ctx.make_sems("s8go", 2)

    for i in range(NSEQ * NT):
        s, t = divmod(i, NT)
        j = i % 2

        @blk.sync
        def _(q):
            if i >= 2:
                ctx.wait(q, "so", io[("s8_out", j)])
            sl = slice(t * 128, (t + 1) * 128)
            ctx.inc(sld[j], q.dma_start(out=ft[j][:], in_=feats_d[s, sl, :]), 16)
            ctx.inc(sld[j], q.dma_start(out=st_[j][:], in_=s1_d[s, sl, :]), 16)
            ctx.inc(sld[j], q.dma_start(out=x1t[j][:], in_=x1_d[s, sl, :]), 16)
            ctx.inc(sld[j], q.dma_start(out=al[j][:], in_=asel_d[s, sl, :]), 16)
            ctx.inc(sld[j], q.dma_start(out=dcol[j][:], in_=dest_d[s, :, t:t + 1],
                                        allow_slow_non_contiguous=True), 16)
        ld = ctx.val(sld[j])

        @blk.vector
        def _(v):
            if i >= 2:
                ctx.wait(v, sgo[j], ctx.val(sgo[j]))
            c = v.memset(at[j][:], 0.0)
            io[("s8_z0", j)] = ctx.inc("sv", c)

        @blk.gpsimd
        def _(g):
            ctx.wait(g, "sv", io[("s8_z0", j)])
            ctx.wait(g, sld[j], ld)
            ctx.inc(sgo[j], g.indirect_dma_start(
                out=at[j][:], out_offset=None,
                in_=zsort_d.rearrange("n c d -> (n c) d"),
                in_offset=IndirectOffsetOnAxis(ap=dcol[j][:], axis=0),
                element_offset=s * 4096 * D,
                bounds_check=4095, oob_is_err=False), 16)

        @blk.vector
        def _(v):
            ctx.wait(v, sld[j], ld)
            ctx.wait(v, sgo[j], ctx.val(sgo[j]))
            c = ts(v, at[j][:], at[j][:], al[j][:], ALU.mult)
            ctx.wait(v, "sv", ctx.inc("sv", c))
            c = v.tensor_tensor(out=at[j][:], in0=at[j][:], in1=ft[j][:],
                                op=ALU.add)
            io["_s8_pre"] = ctx.inc("sv", c)

        @blk.scalar
        def _(sE):
            ctx.wait(sE, "sv", io["_s8_pre"])
            c = sE.activation(at[j][:], at[j][:], AF.Silu, bias=0.0, scale=1.0)
            io["_s8_silu"] = ctx.inc("ss", c)

        @blk.vector
        def _(v):
            ctx.wait(v, "ss", io["_s8_silu"])
            c = v.tensor_tensor(out=ot[j][:], in0=at[j][:], in1=st_[j][:],
                                op=ALU.mult)
            ctx.wait(v, "sv", ctx.inc("sv", c))
            c = v.tensor_tensor(out=ot[j][:], in0=ot[j][:], in1=x1t[j][:],
                                op=ALU.add)
            io[("s8_o", j)] = ctx.inc("sv", c)

        @blk.gpsimd
        def _(g):
            ctx.wait(g, "sv", io[("s8_o", j)])
            io[("s8_out", j)] = ctx.inc("so", g.dma_start(
                out=out_d[s, t * 128:(t + 1) * 128, :], in_=ot[j][:]), 16)

    @blk.sync
    def _(q):
        ctx.wait(q, "so", ctx.val("so"))



def _host_prep(inputs):
    f = {}
    f32 = np.float32
    perm = np.arange(1024).reshape(-1, 2)[:, ::-1].reshape(-1)
    for nm in ("Wq", "Wk"):
        W = np.ascontiguousarray(inputs[f"L0_attn_{nm}"].astype(f32))
        f[f"{nm}T"] = np.ascontiguousarray(W.T)
        f[f"{nm}Ts"] = np.ascontiguousarray(W.T[:, perm])
    for bn in ("bq", "bk"):
        b = inputs[f"L0_attn_{bn}"].astype(f32)
        f[f"{bn}_pc"] = np.ascontiguousarray(b.reshape(8, 128).T)
        f[f"{bn}s_pc"] = np.ascontiguousarray(b[perm].reshape(8, 128).T)
    f["WvT"] = np.ascontiguousarray(inputs["L0_attn_Wv"].astype(f32).T)
    f["bv_row"] = inputs["L0_attn_bv"].astype(f32).reshape(1, D)
    f["WoT"] = np.ascontiguousarray(inputs["L0_attn_Wo"].astype(f32).T)
    f["bo_row"] = inputs["L0_attn_bo"].astype(f32).reshape(1, D)
    f["ffn_w"] = np.ascontiguousarray(inputs["L0_ffn_weight"].astype(f32))
    f["ffn_b_row"] = inputs["L0_ffn_bias"].astype(f32).reshape(1, D)
    for nm in ("Wq", "Wk", "Wv", "Wo"):
        f[f"L1{nm}"] = np.ascontiguousarray(inputs[f"L1_attn_{nm}"].astype(f32))
    for bn in ("bq", "bk"):
        b = inputs[f"L1_attn_{bn}"].astype(f32)
        f[f"l1{bn}_pc"] = np.ascontiguousarray(b.reshape(8, 128).T)
    f["l1bv_row"] = inputs["L1_attn_bv"].astype(f32).reshape(1, D)
    f["l1bk_row"] = inputs["L1_attn_bk"].astype(f32).reshape(1, D)
    f["l1bo_row"] = inputs["L1_attn_bo"].astype(f32).reshape(1, D)
    f["l1gam_row"] = inputs["L1_attn_gamma"].astype(f32).reshape(1, D)
    f["l1bet_row"] = inputs["L1_attn_beta"].astype(f32).reshape(1, D)
    f["centT"] = np.ascontiguousarray(inputs["L1_ffn_centroids"].astype(f32).T)
    gam = inputs["L1_ffn_gamma"].astype(f32)
    bet = inputs["L1_ffn_beta"].astype(f32)
    ga_pc = np.empty((128, 64), f32)
    be_pc = np.empty((128, 64), f32)
    for e in range(8):
        ga_pc[:, e * 8:(e + 1) * 8] = gam[e].reshape(8, 128).T
        be_pc[:, e * 8:(e + 1) * 8] = bet[e].reshape(8, 128).T
    f["ga_pc"], f["be_pc"] = ga_pc, be_pc
    b1 = inputs["L1_ffn_b1"].astype(f32)
    b1_pc = np.empty((128, 8 * 32), f32)
    for e in range(8):
        b1_pc[:, e * 32:(e + 1) * 32] = b1[e].reshape(32, 128).T
    f["b1_pc"] = b1_pc
    f["b2_rows"] = np.ascontiguousarray(inputs["L1_ffn_b2"].astype(f32))
    f["w1"] = np.ascontiguousarray(inputs["L1_ffn_w1"].astype(f32))
    f["w2"] = np.ascontiguousarray(inputs["L1_ffn_w2"].astype(f32))
    f["ident_bf"] = np.eye(128, dtype=ml_dtypes.bfloat16)
    f["ident_f32"] = np.eye(128, dtype=np.float32)
    d = 64
    inv_freq = (1.0 / 10000.0 ** (np.arange(0, d, 2, dtype=f32) / d))
    pos = np.arange(S, dtype=f32)
    th = pos[None, :] * inv_freq[:, None]
    cos = np.cos(th).astype(f32)
    sin = np.sin(th).astype(f32)
    cos64 = np.repeat(cos, 2, axis=0)
    sin64 = np.empty((64, S), f32)
    sin64[0::2] = -sin
    sin64[1::2] = sin
    f["cos_ext"] = np.ascontiguousarray(np.concatenate([cos64, cos64], axis=0))
    f["sin_ext"] = np.ascontiguousarray(np.concatenate([sin64, sin64], axis=0))
    e16 = np.zeros((128, 128), f32)
    for c in range(8):
        for r_ in range(128):
            hh = 2 * c + (1 if r_ >= 64 else 0)
            e16[r_, c * 16 + hh] = 1.0
    f["e16"] = e16
    lam = 0.95
    q_ = np.arange(128)
    f["decay_mt"] = np.where(q_[None, :] - q_[:, None] >= 0,
                             lam ** (q_[None, :] - q_[:, None]), 0.0).astype(f32)
    f["wdec"] = (lam ** (127 - q_)).astype(f32).reshape(128, 1)
    f["lamq"] = (lam ** (q_ + 1)).astype(f32).reshape(128, 1)
    d2 = np.zeros((64, 128), f32)
    for i in range(128):
        d2[i % 64, i] = 1.0
    f["dup2"] = d2
    jj = np.arange(128)
    f["ltri"] = (jj[:, None] < jj[None, :]).astype(f32)
    f["er512"] = np.tile((512.0 * np.arange(8, dtype=f32))[None, :], (128, 1))
    f["slotfull"] = np.tile(np.arange(4096, dtype=f32)[None, :], (128, 1))
    return f


_IN_SPECS = [
    ("x", [NSEQ, S, D]), ("WqT", [D, D]), ("WqTs", [D, D]), ("WkT", [D, D]),
    ("WkTs", [D, D]), ("WvT", [D, D]), ("bq_pc", [128, 8]), ("bqs_pc", [128, 8]),
    ("bk_pc", [128, 8]), ("bks_pc", [128, 8]), ("bv_row", [1, D]),
    ("WoT", [D, D]), ("bo_row", [1, D]), ("ffn_w", [D, D]), ("ffn_b_row", [1, D]),
    ("L1Wq", [D, D]), ("L1Wk", [D, D]), ("L1Wv", [D, D]), ("L1Wo", [D, D]),
    ("l1bq_pc", [128, 8]), ("l1bk_pc", [128, 8]), ("l1bv_row", [1, D]),
    ("l1bk_row", [1, D]), ("l1bo_row", [1, D]), ("l1gam_row", [1, D]),
    ("l1bet_row", [1, D]), ("centT", [D, 8]), ("ga_pc", [128, 64]),
    ("be_pc", [128, 64]), ("b1_pc", [128, 256]), ("b2_rows", [8, D]),
    ("ident_f32", [128, 128]), ("cos_ext", [128, S]), ("sin_ext", [128, S]),
    ("e16", [128, 128]), ("decay_mt", [128, 128]), ("wdec", [128, 1]),
    ("lamq", [128, 1]), ("dup2", [64, 128]), ("ltri", [128, 128]),
    ("er512", [128, 8]), ("slotfull", [128, 4096]),
]
_IN_SPECS_U32 = []
_IN_SPECS_W1W2 = [("w1", [8, D, 4096]), ("w2", [8, 4096, D])]

_L0_INTER = [("wdT", [D, D], "BF16"), ("xnT", [NSEQ, D, S], "BF16"),
             ("xdT", [NSEQ, D, S], "BF16"), ("qT", [NSEQ, D, S], "BF16"),
             ("kT", [NSEQ, D, S], "BF16"), ("v", [NSEQ, S, D], "BF16"),
             ("oT", [NSEQ, D, S], "BF16"), ("x1", [NSEQ, S, D], "F32")]
_L1_INTER = [("xn1T", [NSEQ, D, S], "BF16"), ("xhatT", [NSEQ, D, S], "BF16"),
             ("feats", [NSEQ, S, D], "F32"), ("mask", [NSEQ, S, 8], "F32"),
             ("asel", [NSEQ, S, 1], "F32"), ("qTl", [NSEQ, D, S], "BF16"),
             ("kTl", [NSEQ, D, S], "BF16"), ("ktm", [NSEQ, S, D], "BF16"),
             ("vl", [NSEQ, S, D], "BF16"), ("oTl", [NSEQ, D, S], "BF16"),
             ("s1", [NSEQ, S, D], "F32"),
             ("xhat_tm", [NSEQ, S, D], "BF16"),
             ("xsort", [NSEQ, 4096, D], "BF16"),
             ("zsort", [NSEQ, 4096, D], "F32"),
             ("dest", [NSEQ, 128, NT], "U32"),
             ("destf", [NSEQ, 128, NT], "F32")]


_DT = {"F32": F32, "BF16": BF16, "U32": mybir.dt.uint32}


def _declare(nc, io, specs_in, inter, x1_kind=None):
    for nm, shape in specs_in:
        io[nm] = nc.dram_tensor(nm, shape, F32, kind="ExternalInput").ap()
    for nm, shape in _IN_SPECS_U32:
        io[nm] = nc.dram_tensor(nm, shape, mybir.dt.uint32,
                                kind="ExternalInput").ap()
    for nm, shape, dt in inter:
        k = "Internal"
        if x1_kind and nm == "x1":
            k = x1_kind
        io[nm] = nc.dram_tensor(nm, shape, _DT[dt], kind=k).ap()


def _preamble(ctx, blk, nc, io):
    ident_d = nc.dram_tensor("ident_bf", [128, 128], BF16,
                             kind="ExternalInput").ap()
    identb = ctx.outer_E(nc.sbuf_tensor("identb", [128, 128], BF16))

    @blk.sync
    def _(q):
        ctx.inc("sd", q.dma_start(out=identb[:], in_=ident_d), 16)
    ld = ctx.val("sd")

    @blk.vector
    def _(v):
        ctx.wait(v, "sd", ld)

    @blk.tensor
    def _(t):
        ctx.wait(t, "sd", ld)

    @blk.scalar
    def _(sE):
        ctx.wait(sE, "sd", ld)

    @blk.gpsimd
    def _(g):
        ctx.wait(g, "sd", ld)
    io["ident_bf"] = identb


def build_full():
    nc = bass.Bass("TRN2", target_bir_lowering=False,
                   detect_race_conditions=False)
    io = {}
    _declare(nc, io, _IN_SPECS + _IN_SPECS_W1W2, _L0_INTER + _L1_INTER)
    io["out"] = nc.dram_tensor("out", [NSEQ, S, D], F32,
                               kind="ExternalOutput").ap()
    with ExitStack() as stack:
        ctx = Ctx(nc, stack)
        blk = ctx.E(nc.Block())
        _preamble(ctx, blk, nc, io)
        for em in (emit_stage0w, emit_stage1, emit_stage2, emit_stage3,
                   emit_stage4, emit_stage5, emit_stage6a, emit_stage6b,
                   emit_stage6c, emit_stage7, emit_stage8):
            with ctx.stage():
                em(ctx, blk, io)
            ctx.barrier(blk)
    return nc


def build_l0():
    nc = bass.Bass("TRN2", target_bir_lowering=False,
                   detect_race_conditions=False)
    io = {}
    _declare(nc, io, _IN_SPECS, _L0_INTER, x1_kind="ExternalOutput")
    with ExitStack() as stack:
        ctx = Ctx(nc, stack)
        blk = ctx.E(nc.Block())
        _preamble(ctx, blk, nc, io)
        for em in (emit_stage0w, emit_stage1, emit_stage2, emit_stage3,
                   emit_stage4):
            with ctx.stage():
                em(ctx, blk, io)
            ctx.barrier(blk)
        @blk.sync
        def _(q):
            for nm in ("s4x10", "s4x11"):
                ctx.wait(q, nm, ctx.val(nm))
    return nc


def build_l1():
    nc = bass.Bass("TRN2", target_bir_lowering=False,
                   detect_race_conditions=False)
    io = {}
    specs = [s for s in _IN_SPECS if s[0] not in
             ("WqT", "WqTs", "WkT", "WkTs", "WvT", "bq_pc", "bqs_pc", "bk_pc",
              "bks_pc", "bv_row", "WoT", "bo_row", "ffn_w", "ffn_b_row",
              "cos_ext", "sin_ext", "e16", "x")]
    specs = [("x1", [NSEQ, S, D])] + specs
    _declare(nc, io, specs + _IN_SPECS_W1W2, _L1_INTER)
    io["out"] = nc.dram_tensor("out", [NSEQ, S, D], F32,
                               kind="ExternalOutput").ap()
    with ExitStack() as stack:
        ctx = Ctx(nc, stack)
        blk = ctx.E(nc.Block())
        _preamble(ctx, blk, nc, io)
        for em in (emit_stage5, emit_stage6a, emit_stage6b, emit_stage6c,
                   emit_stage7, emit_stage8):
            with ctx.stage():
                em(ctx, blk, io)
            ctx.barrier(blk)
    return nc


FUSED = True


def kernel(**inputs):
    f = _host_prep(inputs)
    x = np.ascontiguousarray(inputs["L0_attn_x"].astype(np.float32))
    ncores = 8
    if FUSED:
        nc = build_full()
        in_maps = []
        for ci in range(ncores):
            m = dict(f)
            m["x"] = np.ascontiguousarray(x[2 * ci:2 * ci + 2])
            in_maps.append(m)
        res = run_bass_kernel_spmd(nc, in_maps, core_ids=list(range(ncores)))
        out = np.concatenate([np.asarray(r["out"]) for r in res.results], axis=0)
        return out
    nc0 = build_l0()
    keys0 = {s[0] for s in _IN_SPECS} | {"ident_bf"}
    in_maps = []
    for ci in range(ncores):
        m = {k: v for k, v in f.items() if k in keys0}
        m["x"] = np.ascontiguousarray(x[2 * ci:2 * ci + 2])
        in_maps.append(m)
    res0 = run_bass_kernel_spmd(nc0, in_maps, core_ids=list(range(ncores)))
    nc1 = build_l1()
    keys1 = {"L1Wq", "L1Wk", "L1Wv", "L1Wo", "l1bq_pc", "l1bk_pc", "l1bv_row",
             "l1bk_row", "l1bo_row", "l1gam_row", "l1bet_row", "centT",
             "ga_pc", "be_pc", "b1_pc", "b2_rows", "w1", "w2", "ident_f32",
             "decay_mt", "wdec", "lamq", "dup2", "ident_bf", "ltri", "er512",
             "slotfull"}
    in_maps = []
    for ci in range(ncores):
        m = {k: v for k, v in f.items() if k in keys1}
        m["x1"] = np.asarray(res0.results[ci]["x1"])
        in_maps.append(m)
    res1 = run_bass_kernel_spmd(nc1, in_maps, core_ids=list(range(ncores)))
    return np.concatenate([np.asarray(r["out"]) for r in res1.results], axis=0)
